# version 01
# baseline (speedup 1.0000x reference)
import numpy as np
import ml_dtypes
import concourse.bass as bass
import concourse.mybir as mybir
from concourse.bass_utils import run_bass_kernel_spmd

F32 = mybir.dt.float32
BF16 = mybir.dt.bfloat16
AF = mybir.ActivationFunctionType
ALU = mybir.AluOpType
AX = mybir.AxisListType.X
BF = ml_dtypes.bfloat16

NCORE = 8
T = 1024
D = 2048
NCH = D // 128

ENGS = ("sync", "tensor", "vector", "scalar", "gpsimd")


class Buf:
    __slots__ = ("ap", "w", "r")

    def __init__(self, ap):
        self.ap = ap
        self.w = None
        self.r = []


class Sched:
    NSLOT = 8

    def __init__(self, nc, serial=False):
        self.nc = nc
        self.serial = serial
        self.items = {e: [] for e in ENGS}
        self.val = {e: 0 for e in ENGS}
        self.duse = {e: [0] * self.NSLOT for e in ENGS}
        self.dnext = {e: 0 for e in ENGS}

    def op(self, eng, build, reads=(), writes=(), extra=(), dma=False):
        raw = []
        for b in reads:
            if b.w:
                raw.append(b.w)
        for b in writes:
            if b.w:
                raw.append(b.w)
            raw.extend(b.r)
        raw.extend(extra)
        deps = list(raw)
        if self.serial:
            deps = [(e2, v) for e2, v in self.val.items() if v > 0]
            deps += [("D", e2, s, 16 * u) for e2 in ENGS
                     for s, u in enumerate(self.duse[e2]) if u > 0]
        if dma:
            s = self.dnext[eng]
            self.dnext[eng] = (s + 1) % self.NSLOT
            u = self.duse[eng][s] + 1
            if u >= 2:
                deps.append(("D", eng, s, 16 * (u - 1)))
            self.duse[eng][s] = u
            tok = ("D", eng, s, 16 * u)
            self.items[eng].append((tuple(deps), build, ("D", eng, s)))
        else:
            self.val[eng] += 1
            tok = (eng, self.val[eng])
            self.items[eng].append((tuple(deps), build, eng))
        for b in reads:
            b.r.append(tok)
        for b in writes:
            b.w = tok
            b.r = []
        return tok

    def fence_all(self, eng="sync"):
        extra = [(e, self.val[e]) for e in ENGS if e != eng and self.val[e] > 0]
        extra += [("D", e, s, 16 * u) for e in ENGS
                  for s, u in enumerate(self.duse[e]) if u > 0]
        self.op(eng, lambda s: s.nop(), extra=extra)

    def build(self):
        nc = self.nc
        sems = {e: nc.alloc_semaphore(f"sched_{e}") for e in ENGS}
        dsems = {e: [nc.alloc_semaphore(f"dma_{e}_{s}") for s in range(self.NSLOT)]
                 for e in ENGS if any(self.duse[e])}
        with nc.Block() as block:
            for e in ENGS:
                items = self.items[e]

                def body(engobj, items=items, myname=e):
                    hi = {}
                    done = 0
                    for deps, build, inckey in items:
                        for t in deps:
                            if t[0] == "D":
                                _, e2, s2, v2 = t
                                key = ("D", e2, s2)
                                if hi.get(key, 0) < v2:
                                    engobj.wait_ge(dsems[e2][s2], v2)
                                    hi[key] = v2
                            else:
                                e2, v2 = t
                                if e2 == myname:
                                    if myname == "tensor" or v2 <= done:
                                        continue
                                    engobj.wait_ge(sems[e2], v2)
                                    done = v2
                                    continue
                                if hi.get(e2, 0) < v2:
                                    engobj.wait_ge(sems[e2], v2)
                                    hi[e2] = v2
                        ins = build(engobj)
                        if inckey[0] == "D":
                            ins.then_inc(dsems[inckey[1]][inckey[2]], 16)
                        else:
                            ins.then_inc(sems[inckey], 1)
                getattr(block, e)(body)



def fm_shard(arr2d, core):
    return np.ascontiguousarray(arr2d[core * T:(core + 1) * T, :].T)


def col128(vec):
    return np.ascontiguousarray(vec.reshape(-1, 128).T)


def fm_full(dram_ap):
    return dram_ap.rearrange("(c p) t -> p c t", p=128)


def sb3(sb_ap, tok):
    return sb_ap.rearrange("p (c t) -> p c t", t=tok)


def fm_chunk(dram_ap, c):
    return dram_ap[c * 128:(c + 1) * 128, :]



def _aw(c, n, buf):
    return [buf] if (c == 0 or c == n - 1) else []


def ln_stats(S, x_sb, xsq_scr, ps_sum, ps_sumsq, ones_bf, nch, tok, cast_scr):
    nhalf = (tok + 511) // 512
    for c in range(nch):
        cb = cast_scr[c % 2]
        sq = xsq_scr[c % 2]
        xs = x_sb.ap[:, c * tok:(c + 1) * tok]
        S.op("vector", lambda v, o=cb.ap, i=xs: v.tensor_copy(o, i),
             reads=[x_sb], writes=[cb])
        S.op("scalar", lambda s, o=sq.ap, i=xs: s.activation(o, i, AF.Square),
             reads=[x_sb], writes=[sq])
        for h in range(nhalf):
            lo, hi_ = h * 512, min((h + 1) * 512, tok)
            S.op("tensor", lambda t, o=ps_sum.ap[0:1, lo:hi_], l=ones_bf.ap,
                 r=cb.ap[:, lo:hi_], st=(c == 0), sp=(c == nch - 1):
                 t.matmul(o, l, r, start=st, stop=sp),
                 reads=[cb, ones_bf], writes=_aw(c, nch, ps_sum))
            S.op("tensor", lambda t, o=ps_sumsq.ap[0:1, lo:hi_], l=ones_bf.ap,
                 r=sq.ap[:, lo:hi_], st=(c == 0), sp=(c == nch - 1):
                 t.matmul(o, l, r, start=st, stop=sp),
                 reads=[sq, ones_bf], writes=_aw(c, nch, ps_sumsq))


def bcast_row(S, row_buf, row_slice, out_sb, onesk1, ps_bc, tok):
    nhalf = (tok + 511) // 512
    for h in range(nhalf):
        lo, hi_ = h * 512, min((h + 1) * 512, tok)
        S.op("tensor", lambda t, o=ps_bc.ap[:, 0:hi_ - lo], l=onesk1.ap,
             r=row_slice[0:1, lo:hi_]:
             t.matmul(o, l, r, start=True, stop=True),
             reads=[row_buf, onesk1], writes=[ps_bc])
        S.op("vector", lambda v, o=out_sb.ap[:, lo:hi_], i=ps_bc.ap[:, 0:hi_ - lo]:
             v.tensor_copy(o, i),
             reads=[ps_bc], writes=[out_sb])



def sb(nc, name, shape, dtype):
    return Buf(nc.alloc_sbuf_tensor(name, shape, dtype).ap())


def psb(nc, name, shape, dtype=F32):
    return Buf(nc.alloc_psum_tensor(name, shape, dtype).ap())



def alias_over(ap, *olds):
    b = Buf(ap)
    rs = []
    for o in olds:
        if o.w:
            rs.append(o.w)
        rs.extend(o.r)
    b.r = rs
    return b



def build_l0qkv(debug=False, serial=False):
    nc = bass.Bass("TRN2", target_bir_lowering=False, debug=False)
    x_d = nc.dram_tensor("x", [D, T], F32, kind="ExternalInput").ap()
    qkvw_d = nc.dram_tensor("qkvw", [D, 3 * D], F32, kind="ExternalInput").ap()
    gamma_d = nc.dram_tensor("gamma", [128, NCH], F32, kind="ExternalInput").ap()
    beta_d = nc.dram_tensor("beta", [128, NCH], F32, kind="ExternalInput").ap()
    xn_o = nc.dram_tensor("xn", [D, T], F32, kind="ExternalOutput").ap()
    q_o = nc.dram_tensor("q", [D, T], BF16, kind="ExternalOutput").ap()
    k_o = nc.dram_tensor("k", [D, T], BF16, kind="ExternalOutput").ap()
    v_o = nc.dram_tensor("v", [D, T], BF16, kind="ExternalOutput").ap()

    if debug:
        dbg_rows = nc.dram_tensor("dbg_rows", [1, 16 * T], F32, kind="ExternalOutput").ap()
        dbg_bc = nc.dram_tensor("dbg_bc", [128, 2 * T], F32, kind="ExternalOutput").ap()
    ones_col_bf = nc.inline_tensor(np.ones((128, 1), dtype=BF), name="ones_col_bf").ap()
    ones_row_f = nc.inline_tensor(np.ones((1, 128), dtype=np.float32), name="ones_row_f").ap()

    S = Sched(nc, serial=serial)
    x_sb = sb(nc, "x_sb", [128, NCH * T], F32)
    xnorm = sb(nc, "xnorm_bf", [128, NCH * T], BF16)
    scr = [sb(nc, "scr_a", [128, T], BF16), sb(nc, "scr_b", [128, T], BF16)]
    sqr = [sb(nc, "scr_c", [128, T], BF16), sb(nc, "scr_d", [128, T], BF16)]
    xnc = [sb(nc, "xn_chunk0", [128, T], F32), sb(nc, "xn_chunk1", [128, T], F32)]
    rows = sb(nc, "rows", [1, 16 * T], F32)
    bc0 = sb(nc, "bc0", [128, T], F32)
    bc1 = sb(nc, "bc1", [128, T], F32)
    onesk1 = sb(nc, "onesk1", [1, 128], F32)
    onesb = sb(nc, "onesb", [128, 1], BF16)
    gb = sb(nc, "gb_sb", [128, 2 * NCH], F32)
    wb = [sb(nc, "w0", [128, NCH * 128], BF16), sb(nc, "w1", [128, NCH * 128], BF16)]
    ev = [sb(nc, "ev0", [128, T], BF16), sb(nc, "ev1", [128, T], BF16)]
    ps_sum = psb(nc, "ps_sum", [1, T])
    ps_sumsq = psb(nc, "ps_sumsq", [1, T])
    ps_bc = psb(nc, "ps_bc", [128, 512])
    ps_mm = [psb(nc, "ps_mm0", [128, 512]), psb(nc, "ps_mm1", [128, 512])]
    if True:
        ones_col_bf_b = Buf(ones_col_bf)
        ones_row_f_b = Buf(ones_row_f)

        S.op("sync", lambda s: s.dma_start(out=sb3(x_sb.ap, T), in_=fm_full(x_d)),
             writes=[x_sb], dma=True)
        S.op("sync", lambda s: s.dma_start(out=gb.ap[:, 0:NCH], in_=gamma_d),
             writes=[gb], dma=True)
        S.op("sync", lambda s: s.dma_start(out=gb.ap[:, NCH:2 * NCH], in_=beta_d),
             writes=[gb], dma=True)
        S.op("sync", lambda s: s.dma_start(out=onesb.ap, in_=ones_col_bf),
             reads=[ones_col_bf_b], writes=[onesb], dma=True)
        S.op("sync", lambda s: s.dma_start(out=onesk1.ap, in_=ones_row_f),
             reads=[ones_row_f_b], writes=[onesk1], dma=True)

        ln_stats(S, x_sb, sqr, ps_sum, ps_sumsq, onesb, NCH, T, scr)

        RW = rows.ap
        r_mu1 = RW[0:1, 0:T]
        r_ex2 = RW[0:1, T:2 * T]
        r_var1 = RW[0:1, 2 * T:3 * T]
        r_rstd1 = RW[0:1, 3 * T:4 * T]
        r_t0 = RW[0:1, 4 * T:5 * T]
        r_mu2 = RW[0:1, 5 * T:6 * T]
        r_ex22 = RW[0:1, 6 * T:7 * T]
        r_var2 = RW[0:1, 7 * T:8 * T]
        r_rstd2 = RW[0:1, 8 * T:9 * T]
        r_c1 = RW[0:1, 9 * T:10 * T]
        r_c2 = RW[0:1, 10 * T:11 * T]
        r_m1r1 = RW[0:1, 11 * T:12 * T]

        S.op("vector", lambda v: v.tensor_scalar(r_mu1, ps_sum.ap, 1.0 / D, None, ALU.mult),
             reads=[ps_sum], writes=[rows])
        S.op("vector", lambda v: v.tensor_scalar(r_ex2, ps_sumsq.ap, 1.0 / D, None, ALU.mult),
             reads=[ps_sumsq], writes=[rows])
        S.op("vector", lambda v: v.tensor_tensor(r_t0, r_mu1, r_mu1, ALU.mult),
             reads=[rows], writes=[rows])
        S.op("vector", lambda v: v.tensor_tensor(r_var1, r_ex2, r_t0, ALU.subtract),
             reads=[rows], writes=[rows])
        S.op("vector", lambda v: v.tensor_scalar(r_var1, r_var1, 1e-5, None, ALU.add),
             reads=[rows], writes=[rows])
        S.op("scalar", lambda s: s.activation(r_t0, r_var1, AF.Sqrt),
             reads=[rows], writes=[rows])
        S.op("vector", lambda v: v.reciprocal(r_rstd1, r_t0),
             reads=[rows], writes=[rows])
        S.op("vector", lambda v: v.tensor_tensor(r_m1r1, r_mu1, r_rstd1, ALU.mult),
             reads=[rows], writes=[rows])

        bcast_row(S, rows, r_rstd1, bc0, onesk1, ps_bc, T)
        bcast_row(S, rows, r_m1r1, bc1, onesk1, ps_bc, T)

        if debug:
            S.op("sync", lambda s: s.dma_start(out=dbg_bc[:, 0:T], in_=bc0.ap),
                 reads=[bc0], dma=True)
            S.op("sync", lambda s: s.dma_start(out=dbg_bc[:, T:2 * T], in_=bc1.ap),
                 reads=[bc1], dma=True)

        for c in range(NCH):
            xb = xnc[c % 2]
            xs = x_sb.ap[:, c * T:(c + 1) * T]
            S.op("vector", lambda v, o=xb.ap, i=xs: v.tensor_tensor(o, i, bc0.ap, ALU.mult),
                 reads=[x_sb, bc0], writes=[xb])
            S.op("vector", lambda v, o=xb.ap: v.tensor_tensor(o, o, bc1.ap, ALU.subtract),
                 reads=[bc1], writes=[xb])
            S.op("sync", lambda s, o=fm_chunk(xn_o, c), i=xb.ap:
                 s.dma_start(out=o, in_=i),
                 reads=[xb], dma=True)
            cb = scr[c % 2]
            sq = sqr[c % 2]
            S.op("vector", lambda v, o=cb.ap, i=xb.ap: v.tensor_copy(o, i),
                 reads=[xb], writes=[cb])
            S.op("scalar", lambda s, o=sq.ap, i=xb.ap: s.activation(o, i, AF.Square),
                 reads=[xb], writes=[sq])
            for h in range(2):
                lo, hi_ = h * 512, (h + 1) * 512
                S.op("tensor", lambda t, o=ps_sum.ap[0:1, lo:hi_], l=onesb.ap,
                     r=cb.ap[:, lo:hi_], st=(c == 0), sp=(c == NCH - 1):
                     t.matmul(o, l, r, start=st, stop=sp),
                     reads=[cb, onesb], writes=_aw(c, NCH, ps_sum))
                S.op("tensor", lambda t, o=ps_sumsq.ap[0:1, lo:hi_], l=onesb.ap,
                     r=sq.ap[:, lo:hi_], st=(c == 0), sp=(c == NCH - 1):
                     t.matmul(o, l, r, start=st, stop=sp),
                     reads=[sq, onesb], writes=_aw(c, NCH, ps_sumsq))

        S.op("vector", lambda v: v.tensor_scalar(r_mu2, ps_sum.ap, 1.0 / D, None, ALU.mult),
             reads=[ps_sum], writes=[rows])
        S.op("vector", lambda v: v.tensor_scalar(r_ex22, ps_sumsq.ap, 1.0 / D, None, ALU.mult),
             reads=[ps_sumsq], writes=[rows])
        S.op("vector", lambda v: v.tensor_tensor(r_t0, r_mu2, r_mu2, ALU.mult),
             reads=[rows], writes=[rows])
        S.op("vector", lambda v: v.tensor_tensor(r_var2, r_ex22, r_t0, ALU.subtract),
             reads=[rows], writes=[rows])
        S.op("vector", lambda v: v.tensor_scalar(r_var2, r_var2, 1e-5, None, ALU.add),
             reads=[rows], writes=[rows])
        S.op("scalar", lambda s: s.activation(r_t0, r_var2, AF.Sqrt),
             reads=[rows], writes=[rows])
        S.op("vector", lambda v: v.reciprocal(r_rstd2, r_t0),
             reads=[rows], writes=[rows])
        S.op("vector", lambda v: v.tensor_tensor(r_c1, r_rstd1, r_rstd2, ALU.mult),
             reads=[rows], writes=[rows])
        S.op("vector", lambda v: v.tensor_tensor(r_t0, r_m1r1, r_mu2, ALU.add),
             reads=[rows], writes=[rows])
        S.op("vector", lambda v: v.tensor_tensor(r_c2, r_t0, r_rstd2, ALU.mult),
             reads=[rows], writes=[rows])

        bcast_row(S, rows, r_c1, bc0, onesk1, ps_bc, T)
        bcast_row(S, rows, r_c2, bc1, onesk1, ps_bc, T)

        for c in range(NCH):
            xb = xnc[c % 2]
            xs = x_sb.ap[:, c * T:(c + 1) * T]
            xo = xnorm.ap[:, c * T:(c + 1) * T]
            S.op("vector", lambda v, o=xb.ap, i=xs: v.tensor_tensor(o, i, bc0.ap, ALU.mult),
                 reads=[x_sb, bc0], writes=[xb])
            S.op("vector", lambda v, o=xb.ap: v.tensor_tensor(o, o, bc1.ap, ALU.subtract),
                 reads=[bc1], writes=[xb])
            S.op("vector", lambda v, o=xo, i=xb.ap, g=gb.ap[:, c:c + 1],
                 b=gb.ap[:, NCH + c:NCH + c + 1]:
                 v.tensor_scalar(o, i, g, b, ALU.mult, ALU.add),
                 reads=[xb, gb], writes=[xnorm])

        for m in range(3 * NCH):
            w = wb[m % 2]
            wsrc = qkvw_d[:, m * 128:(m + 1) * 128].rearrange("(c p) n -> p c n", p=128)
            S.op("gpsimd", lambda g, o=sb3(w.ap, 128), i=wsrc: g.dma_start(out=o, in_=i),
                 writes=[w], dma=True)
            pm = ps_mm[m % 2]
            e = ev[m % 2]
            for h in range(2):
                lo, hi_ = h * 512, (h + 1) * 512
                for c in range(NCH):
                    S.op("tensor", lambda t, o=pm.ap, l=w.ap[:, c * 128:(c + 1) * 128],
                         r=xnorm.ap[:, c * T + lo:c * T + hi_],
                         st=(c == 0), sp=(c == NCH - 1):
                         t.matmul(o, l, r, start=st, stop=sp),
                         reads=[w, xnorm], writes=_aw(c, NCH, pm))
                S.op("vector", lambda v, o=e.ap[:, lo:hi_], i=pm.ap: v.tensor_copy(o, i),
                     reads=[pm], writes=[e])
            if m < NCH:
                dst = fm_chunk(q_o, m)
            elif m < 2 * NCH:
                dst = fm_chunk(k_o, m - NCH)
            else:
                dst = fm_chunk(v_o, m - 2 * NCH)
            S.op("scalar", lambda s, o=dst, i=e.ap: s.dma_start(out=o, in_=i),
                 reads=[e], dma=True)

        if debug:
            S.op("sync", lambda s: s.dma_start(out=dbg_rows, in_=rows.ap),
                 reads=[rows], dma=True)
        S.fence_all("sync")
        S.build()
    return nc





def build_l0ffn():
    nc = bass.Bass("TRN2", target_bir_lowering=False, debug=False)
    xn_d = nc.dram_tensor("xn", [D, T], F32, kind="ExternalInput").ap()
    wfc_d = nc.dram_tensor("wfc", [4 * D, D], F32, kind="ExternalInput").ap()
    wpr_d = nc.dram_tensor("wpr", [D, 4 * D], F32, kind="ExternalInput").ap()
    ffn_o = nc.dram_tensor("ffn", [D, T], F32, kind="ExternalOutput").ap()

    sfc_bf = Buf(nc.dram_tensor("sfc_bf", [4 * D, D], BF16, kind="Internal").ap())
    spr_bf = Buf(nc.dram_tensor("spr_bf", [D, 4 * D], BF16, kind="Internal").ap())

    ones_row_f = nc.inline_tensor(np.ones((1, 128), dtype=np.float32), name="ones_row_f").ap()
    ones_col_bf = nc.inline_tensor(np.ones((128, 1), dtype=BF), name="ones_col_bf").ap()
    ident_f = nc.inline_tensor(np.eye(128, dtype=np.float32), name="ident_f").ap()

    S = Sched(nc)
    xn_sb = sb(nc, "xn_sb", [128, NCH * T], F32)
    gh = alias_over(xn_sb.ap, xn_sb)
    gh.ap = xn_sb.ap.bitcast(BF16)
    xhat = sb(nc, "xhat", [128, NCH * T], BF16)
    wT = [sb(nc, "wT0", [128, NCH * 512], BF16),
          sb(nc, "wT1", [128, NCH * 512], BF16)]
    wsc = sb(nc, "wsc", [128, D], BF16)
    wT2 = [sb(nc, "wT2a", [128, T], BF16), sb(nc, "wT2b", [128, T], BF16)]
    wf32 = sb(nc, "wf32", [128, D], F32)
    wc32 = sb(nc, "wc32", [128, D], F32)
    bc_mu = sb(nc, "bc_mu", [128, T], F32)
    bc_c1 = sb(nc, "bc_c1", [128, T], F32)
    bc_c2 = sb(nc, "bc_c2", [128, T], F32)
    tree = sb(nc, "tree", [128, T], F32)
    sqs = [sb(nc, "sqs0", [128, T], BF16), sb(nc, "sqs1", [128, T], BF16)]
    rows = sb(nc, "rows", [1, 7 * T + 64], F32)
    amax_tm = sb(nc, "amax_tm", [128, 1], F32)
    cols = sb(nc, "cols", [128, 8], F32)
    onesk1 = sb(nc, "onesk1", [1, 128], F32)
    onesb = sb(nc, "onesb", [128, 1], BF16)
    idf = sb(nc, "idf", [128, 128], F32)
    ev = [sb(nc, "ev0", [128, 512], F32), sb(nc, "ev1", [128, 512], F32)]

    pool = nc.alloc_psum_tensor("pool", [128, 4096], F32)
    pap = pool.ap()

    def bank(i, j=None):
        j = i + 1 if j is None else j
        return pap[:, i * 512:j * 512]

    bank_owner = {}

    def take(lo, hi, view):
        olds = [bank_owner[i] for i in range(lo, hi) if i in bank_owner]
        b = alias_over(view, *olds)
        for i in range(lo, hi):
            bank_owner[i] = b
        return b

    consts_loaded = []

    def load_consts():
        S.op("sync", lambda s: s.dma_start(out=onesk1.ap, in_=ones_row_f),
             writes=[onesk1], dma=True)
        S.op("sync", lambda s: s.dma_start(out=onesb.ap, in_=ones_col_bf),
             writes=[onesb], dma=True)
        S.op("sync", lambda s: s.dma_start(out=idf.ap, in_=ident_f),
             writes=[idf], dma=True)

    load_consts()

    S.op("sync", lambda s: s.dma_start(out=sb3(xn_sb.ap, T), in_=fm_full(xn_d)),
         writes=[xn_sb], dma=True)

    ps_sum = take(0, 2, pap[0:1, 0:1024])
    ps_sumsq = take(2, 4, pap[0:1, 1024:2048])
    for c in range(NCH):
        cb = sqs[c % 2]
        xs = xn_sb.ap[:, c * T:(c + 1) * T]
        S.op("vector", lambda v, o=cb.ap, i=xs: v.tensor_copy(o, i),
             reads=[xn_sb], writes=[cb])
        for h in range(2):
            lo, hi_ = h * 512, (h + 1) * 512
            S.op("tensor", lambda t, o=ps_sum.ap[0:1, lo:hi_], l=onesb.ap,
                 r=cb.ap[:, lo:hi_], st=(c == 0), sp=(c == NCH - 1):
                 t.matmul(o, l, r, start=st, stop=sp),
                 reads=[cb, onesb], writes=_aw(c, NCH, ps_sum))
    for c in range(NCH):
        sq = sqs[c % 2]
        xs = xn_sb.ap[:, c * T:(c + 1) * T]
        S.op("scalar", lambda s, o=sq.ap, i=xs: s.activation(o, i, AF.Square),
             reads=[xn_sb], writes=[sq])
        for h in range(2):
            lo, hi_ = h * 512, (h + 1) * 512
            S.op("tensor", lambda t, o=ps_sumsq.ap[0:1, lo:hi_], l=onesb.ap,
                 r=sq.ap[:, lo:hi_], st=(c == 0), sp=(c == NCH - 1):
                 t.matmul(o, l, r, start=st, stop=sp),
                 reads=[sq, onesb], writes=_aw(c, NCH, ps_sumsq))

    RW = rows.ap
    r_mu = RW[0:1, 0:T]
    r_var = RW[0:1, T:2 * T]
    r_t0 = RW[0:1, 2 * T:3 * T]
    r_rstd = RW[0:1, 3 * T:4 * T]
    r_amax = RW[0:1, 4 * T:5 * T]
    r_c1 = RW[0:1, 5 * T:6 * T]
    r_c2 = RW[0:1, 6 * T:7 * T]
    r_g1 = RW[0:1, 7 * T:7 * T + 16]
    r_g2 = RW[0:1, 7 * T + 16:7 * T + 32]

    def row_stats(eps):
        S.op("vector", lambda v: v.tensor_scalar(r_mu, ps_sum.ap, 1.0 / D, None, ALU.mult),
             reads=[ps_sum], writes=[rows])
        S.op("vector", lambda v: v.tensor_scalar(r_t0, ps_sumsq.ap, 1.0 / D, None, ALU.mult),
             reads=[ps_sumsq], writes=[rows])
        S.op("vector", lambda v: v.tensor_tensor(r_var, r_mu, r_mu, ALU.mult),
             reads=[rows], writes=[rows])
        S.op("vector", lambda v: v.tensor_tensor(r_var, r_t0, r_var, ALU.subtract),
             reads=[rows], writes=[rows])
        S.op("vector", lambda v, e=eps: v.tensor_scalar(r_var, r_var, e, None, ALU.add),
             reads=[rows], writes=[rows])
        S.op("scalar", lambda s: s.activation(r_t0, r_var, AF.Sqrt),
             reads=[rows], writes=[rows])
        S.op("vector", lambda v: v.reciprocal(r_rstd, r_t0),
             reads=[rows], writes=[rows])

    row_stats(1e-6)

    ps_bc = take(4, 5, bank(4))
    bcast_row(S, rows, r_mu, bc_mu, onesk1, ps_bc, T)

    for c in range(NCH):
        cb = sqs[c % 2]
        xs = xn_sb.ap[:, c * T:(c + 1) * T]
        sq = sqs[c % 2]
        S.op("vector", lambda v, i=xs, o=bc_c1.ap: v.tensor_tensor(o, i, bc_mu.ap, ALU.subtract),
             reads=[xn_sb, bc_mu], writes=[bc_c1])
        S.op("scalar", lambda s, o=sq.ap: s.activation(o, bc_c1.ap, AF.Abs),
             reads=[bc_c1], writes=[sq])
        if c == 0:
            S.op("vector", lambda v, i=sq.ap: v.tensor_copy(tree.ap, i),
                 reads=[sq], writes=[tree])
        else:
            S.op("vector", lambda v, i=sq.ap: v.tensor_tensor(tree.ap, tree.ap, i, ALU.max),
                 reads=[sq], writes=[tree])

    ps_t = take(5, 6, bank(5))
    ps_r = take(6, 7, bank(6))
    for tt in range(T // 128):
        S.op("tensor", lambda t, i=tree.ap[:, tt * 128:(tt + 1) * 128], o=ps_t.ap[:, 0:128]:
             t.transpose(o, i, idf.ap),
             reads=[tree, idf], writes=[ps_t])
        S.op("vector", lambda v, o=amax_tm.ap, i=ps_t.ap[:, 0:128]:
             v.tensor_reduce(o, i, AX, ALU.max),
             reads=[ps_t], writes=[amax_tm])
        S.op("tensor", lambda t, o=ps_r.ap[0:1, 0:128], i=amax_tm.ap:
             t.transpose(o, i, idf.ap),
             reads=[amax_tm, idf], writes=[ps_r])
        S.op("vector", lambda v, o=r_amax[0:1, tt * 128:(tt + 1) * 128], i=ps_r.ap[0:1, 0:128]:
             v.tensor_copy(o, i),
             reads=[ps_r], writes=[rows])

    S.op("vector", lambda v: v.reciprocal(r_c1, r_amax), reads=[rows], writes=[rows])
    S.op("vector", lambda v: v.tensor_scalar(r_c1, r_c1, 128.0, None, ALU.mult),
         reads=[rows], writes=[rows])
    S.op("vector", lambda v: v.tensor_tensor(r_c2, r_rstd, r_amax, ALU.mult),
         reads=[rows], writes=[rows])
    S.op("vector", lambda v: v.tensor_scalar(r_c2, r_c2, 1.0 / 128.0, None, ALU.mult),
         reads=[rows], writes=[rows])
    bcast_row(S, rows, r_c1, bc_c1, onesk1, ps_bc, T)
    bcast_row(S, rows, r_c2, bc_c2, onesk1, ps_bc, T)

    for c in range(NCH):
        xs = xn_sb.ap[:, c * T:(c + 1) * T]
        xo = xhat.ap[:, c * T:(c + 1) * T]
        S.op("vector", lambda v, i=xs: v.tensor_tensor(tree.ap, i, bc_mu.ap, ALU.subtract),
             reads=[xn_sb, bc_mu], writes=[tree])
        S.op("vector", lambda v: v.tensor_tensor(tree.ap, tree.ap, bc_c1.ap, ALU.mult),
             reads=[bc_c1], writes=[tree])
        S.op("vector", lambda v: v.tensor_scalar(tree.ap, tree.ap, 128.0, -128.0, ALU.min, ALU.max),
             reads=[], writes=[tree])
        S.op("vector", lambda v, o=xo: v.tensor_tensor(o, tree.ap, bc_c2.ap, ALU.mult),
             reads=[tree, bc_c2], writes=[xhat])

    def panels(w_d, nrows, ncols):
        for i in range(nrows // 128):
            for cb in range(ncols // D):
                yield i, cb, w_d[i * 128:(i + 1) * 128, cb * D:(cb + 1) * D]

    def gamma_w(w_d, nrows, ncols, r_g, cidx):
        ps_g = take(7, 8, bank(7))
        plist = list(panels(w_d, nrows, ncols))
        first = [True]
        for pi, (i, cb, pan) in enumerate(plist):
            S.op("sync", lambda s, i2=pan: s.dma_start(out=wf32.ap, in_=i2),
                 writes=[wf32], dma=True)
            S.op("scalar", lambda s: s.activation(wsc.ap, wf32.ap, AF.Abs),
                 reads=[wf32], writes=[wsc])
            for h in range(D // 512):
                lo, hi_ = h * 512, (h + 1) * 512
                st = first[0]
                first[0] = False
                sp = (pi == len(plist) - 1) and (h == D // 512 - 1)
                S.op("tensor", lambda t, o=ps_g.ap[0:1, 0:512], l=onesb.ap,
                     r=wsc.ap[:, lo:hi_], st=st, sp=sp:
                     t.matmul(o, l, r, start=st, stop=sp),
                     reads=[wsc, onesb],
                     writes=[ps_g] if (st or sp) else [])
        S.op("vector", lambda v: v.tensor_reduce(r_g[0:1, 0:1], ps_g.ap[0:1, 0:512], AX, ALU.add),
             reads=[ps_g], writes=[rows])
        S.op("vector", lambda v, n=float(nrows * ncols): v.tensor_scalar(
             r_g[0:1, 0:1], r_g[0:1, 0:1], 1.0 / n, None, ALU.mult),
             reads=[rows], writes=[rows])
        S.op("vector", lambda v: v.tensor_scalar(r_g[0:1, 1:2], r_g[0:1, 0:1],
                                                 1e-6, 0.5, ALU.add, ALU.mult),
             reads=[rows], writes=[rows])
        S.op("vector", lambda v: v.tensor_scalar(r_g[0:1, 2:3], r_g[0:1, 1:2],
                                                 -1.0, None, ALU.mult),
             reads=[rows], writes=[rows])
        ps_c = take(7, 8, bank(7))
        S.op("tensor", lambda t, o=ps_c.ap[:, 0:3], l=onesk1.ap, r=r_g[0:1, 0:3]:
             t.matmul(o, l, r, start=True, stop=True),
             reads=[rows, onesk1], writes=[ps_c])
        S.op("vector", lambda v, o=cols.ap[:, cidx:cidx + 3], i=ps_c.ap[:, 0:3]:
             v.tensor_copy(o, i),
             reads=[ps_c], writes=[cols])

    def ternarize_w(w_d, s_bf, nrows, ncols, cidx):
        for i, cb, pan in panels(w_d, nrows, ncols):
            S.op("sync", lambda s, i2=pan: s.dma_start(out=wf32.ap, in_=i2),
                 writes=[wf32], dma=True)
            S.op("vector", lambda v: v.tensor_scalar(
                 wc32.ap, wf32.ap, cols.ap[:, cidx + 1:cidx + 2],
                 cols.ap[:, cidx + 2:cidx + 3], ALU.min, ALU.max),
                 reads=[wf32, cols], writes=[wc32])
            S.op("vector", lambda v: v.tensor_tensor(wc32.ap, wf32.ap, wc32.ap, ALU.subtract),
                 reads=[wf32], writes=[wc32])
            S.op("scalar", lambda s: s.activation(wsc.ap, wc32.ap, AF.Sign),
                 reads=[wc32], writes=[wsc])
            S.op("scalar", lambda s, o=s_bf.ap[i * 128:(i + 1) * 128, cb * D:(cb + 1) * D]:
                 s.dma_start(out=o, in_=wsc.ap),
                 reads=[wsc], writes=[s_bf], dma=True)

    gamma_w(wfc_d, 4 * D, D, r_g1, 0)
    ternarize_w(wfc_d, sfc_bf, 4 * D, D, 0)
    gamma_w(wpr_d, D, 4 * D, r_g2, 3)
    ternarize_w(wpr_d, spr_bf, D, 4 * D, 3)

    for half in range(2):
        toff = half * 512

        ps_h = [take(i, i + 1, bank(i)) for i in range(4)]
        for blk in range(16):
            w = wT[blk % 2]
            for dc in range(NCH):
                S.op("sync", lambda s, o=w.ap[:, dc * 512:(dc + 1) * 512],
                     i=sfc_bf.ap[blk * 512:(blk + 1) * 512, dc * 128:(dc + 1) * 128]:
                     s.dma_start_transpose(o, i),
                     reads=[sfc_bf], writes=[w], dma=True)
            for m in range(4):
                pm = ps_h[m]
                for dc in range(NCH):
                    S.op("tensor", lambda t, o=pm.ap, w=w,
                         l=w.ap[:, dc * 512 + m * 128:dc * 512 + m * 128 + 128],
                         r=xhat.ap[:, dc * T + toff:dc * T + toff + 512],
                         st=(dc == 0), sp=(dc == NCH - 1):
                         t.matmul(o, l, r, start=st, stop=sp),
                         reads=[w, xhat], writes=_aw(dc, NCH, pm))
                ci = blk * 4 + m
                S.op("scalar", lambda s, o=gh.ap[:, ci * 512:(ci + 1) * 512], i=pm.ap:
                     s.activation(o, i, AF.Gelu, scale=cols.ap[:, 0:1]),
                     reads=[pm, cols], writes=[gh])

        ps_sum2 = take(0, 1, pap[0:1, 0:512])
        ps_sumsq2 = take(1, 2, pap[0:1, 512:1024])
        for c in range(64):
            gs = gh.ap[:, c * 512:(c + 1) * 512]
            S.op("tensor", lambda t, o=ps_sum2.ap, l=onesb.ap, r=gs,
                 st=(c == 0), sp=(c == 63):
                 t.matmul(o, l, r, start=st, stop=sp),
                 reads=[gh, onesb], writes=_aw(c, 64, ps_sum2))
        for c in range(64):
            gs = gh.ap[:, c * 512:(c + 1) * 512]
            sq = sqs[c % 2]
            S.op("scalar", lambda s, o=sq.ap[:, 0:512], i=gs: s.activation(o, i, AF.Square),
                 reads=[gh], writes=[sq])
            S.op("tensor", lambda t, o=ps_sumsq2.ap, l=onesb.ap, r=sq.ap[:, 0:512],
                 st=(c == 0), sp=(c == 63):
                 t.matmul(o, l, r, start=st, stop=sp),
                 reads=[sq, onesb], writes=_aw(c, 64, ps_sumsq2))

        r2_mu = RW[0:1, 0:512]
        r2_var = RW[0:1, 512:1024]
        r2_t0 = RW[0:1, 1024:1536]
        r2_rstd = RW[0:1, 1536:2048]
        r2_amax = RW[0:1, 2048:2560]
        r2_c1 = RW[0:1, 2560:3072]
        r2_c2 = RW[0:1, 3072:3584]
        DH = 4 * D
        S.op("vector", lambda v: v.tensor_scalar(r2_mu, ps_sum2.ap, 1.0 / DH, None, ALU.mult),
             reads=[ps_sum2], writes=[rows])
        S.op("vector", lambda v: v.tensor_scalar(r2_t0, ps_sumsq2.ap, 1.0 / DH, None, ALU.mult),
             reads=[ps_sumsq2], writes=[rows])
        S.op("vector", lambda v: v.tensor_tensor(r2_var, r2_mu, r2_mu, ALU.mult),
             reads=[rows], writes=[rows])
        S.op("vector", lambda v: v.tensor_tensor(r2_var, r2_t0, r2_var, ALU.subtract),
             reads=[rows], writes=[rows])
        S.op("vector", lambda v: v.tensor_scalar(r2_var, r2_var, 1e-6, None, ALU.add),
             reads=[rows], writes=[rows])
        S.op("scalar", lambda s: s.activation(r2_t0, r2_var, AF.Sqrt),
             reads=[rows], writes=[rows])
        S.op("vector", lambda v: v.reciprocal(r2_rstd, r2_t0),
             reads=[rows], writes=[rows])
        ps_bc2 = take(4, 5, bank(4))
        S.op("tensor", lambda t, o=ps_bc2.ap, l=onesk1.ap, r=r2_mu:
             t.matmul(o, l, r, start=True, stop=True),
             reads=[rows, onesk1], writes=[ps_bc2])
        S.op("vector", lambda v: v.tensor_copy(bc_mu.ap[:, 0:512], ps_bc2.ap),
             reads=[ps_bc2], writes=[bc_mu])
        for c in range(64):
            gs = gh.ap[:, c * 512:(c + 1) * 512]
            sq = sqs[c % 2]
            S.op("vector", lambda v, i=gs: v.tensor_tensor(
                 tree.ap[:, 512:1024], i, bc_mu.ap[:, 0:512], ALU.subtract),
                 reads=[gh, bc_mu], writes=[tree])
            S.op("scalar", lambda s, o=sq.ap[:, 0:512]: s.activation(o, tree.ap[:, 512:1024], AF.Abs),
                 reads=[tree], writes=[sq])
            if c == 0:
                S.op("vector", lambda v, i=sq.ap[:, 0:512]: v.tensor_copy(tree.ap[:, 0:512], i),
                     reads=[sq], writes=[tree])
            else:
                S.op("vector", lambda v, i=sq.ap[:, 0:512]: v.tensor_tensor(
                     tree.ap[:, 0:512], tree.ap[:, 0:512], i, ALU.max),
                     reads=[sq], writes=[tree])
        ps_t2 = take(5, 6, bank(5))
        ps_r2 = take(6, 7, bank(6))
        for tt in range(4):
            S.op("tensor", lambda t, i=tree.ap[:, tt * 128:(tt + 1) * 128], o=ps_t2.ap[:, 0:128]:
                 t.transpose(o, i, idf.ap),
                 reads=[tree, idf], writes=[ps_t2])
            S.op("vector", lambda v, o=amax_tm.ap, i=ps_t2.ap[:, 0:128]:
                 v.tensor_reduce(o, i, AX, ALU.max),
                 reads=[ps_t2], writes=[amax_tm])
            S.op("tensor", lambda t, o=ps_r2.ap[0:1, 0:128], i=amax_tm.ap:
                 t.transpose(o, i, idf.ap),
                 reads=[amax_tm, idf], writes=[ps_r2])
            S.op("vector", lambda v, o=r2_amax[0:1, tt * 128:(tt + 1) * 128], i=ps_r2.ap[0:1, 0:128]:
                 v.tensor_copy(o, i),
                 reads=[ps_r2], writes=[rows])
        S.op("vector", lambda v: v.reciprocal(r2_c1, r2_amax), reads=[rows], writes=[rows])
        S.op("vector", lambda v: v.tensor_scalar(r2_c1, r2_c1, 128.0, None, ALU.mult),
             reads=[rows], writes=[rows])
        S.op("vector", lambda v: v.tensor_tensor(r2_c2, r2_rstd, r2_amax, ALU.mult),
             reads=[rows], writes=[rows])
        S.op("vector", lambda v: v.tensor_scalar(r2_c2, r2_c2, 1.0 / 128.0, None, ALU.mult),
             reads=[rows], writes=[rows])
        for rsl, dst in ((r2_c1, bc_c1), (r2_c2, bc_c2)):
            S.op("tensor", lambda t, o=ps_bc2.ap, l=onesk1.ap, r=rsl:
                 t.matmul(o, l, r, start=True, stop=True),
                 reads=[rows, onesk1], writes=[ps_bc2])
            S.op("vector", lambda v, d=dst: v.tensor_copy(d.ap[:, 0:512], ps_bc2.ap),
                 reads=[ps_bc2], writes=[dst])

        for c in range(64):
            gs = gh.ap[:, c * 512:(c + 1) * 512]
            S.op("vector", lambda v, i=gs: v.tensor_tensor(
                 tree.ap[:, 0:512], i, bc_mu.ap[:, 0:512], ALU.subtract),
                 reads=[gh, bc_mu], writes=[tree])
            S.op("vector", lambda v: v.tensor_tensor(
                 tree.ap[:, 0:512], tree.ap[:, 0:512], bc_c1.ap[:, 0:512], ALU.mult),
                 reads=[bc_c1], writes=[tree])
            S.op("vector", lambda v: v.tensor_scalar(
                 tree.ap[:, 0:512], tree.ap[:, 0:512], 128.0, -128.0, ALU.min, ALU.max),
                 reads=[], writes=[tree])
            S.op("vector", lambda v, o=gs: v.tensor_tensor(
                 o, tree.ap[:, 0:512], bc_c2.ap[:, 0:512], ALU.mult),
                 reads=[tree, bc_c2], writes=[gh])

        for grp in range(2):
            ps8 = [take(i, i + 1, bank(i)) for i in range(8)]
            for dc in range(64):
                w2 = wT2[dc % 2]
                S.op("sync", lambda s, o=w2.ap,
                     i=spr_bf.ap[grp * 1024:(grp + 1) * 1024, dc * 128:(dc + 1) * 128]:
                     s.dma_start_transpose(o, i),
                     reads=[spr_bf], writes=[w2], dma=True)
                for m in range(8):
                    S.op("tensor", lambda t, o=ps8[m].ap, w2=w2,
                         l=w2.ap[:, m * 128:(m + 1) * 128],
                         r=gh.ap[:, dc * 512:(dc + 1) * 512],
                         st=(dc == 0), sp=(dc == 63):
                         t.matmul(o, l, r, start=st, stop=sp),
                         reads=[w2, gh], writes=_aw(dc, 64, ps8[m]))
            for m in range(8):
                e = ev[m % 2]
                S.op("vector", lambda v, o=e.ap, i=ps8[m].ap:
                     v.tensor_scalar(o, i, cols.ap[:, 3:4], None, ALU.mult),
                     reads=[ps8[m], cols], writes=[e])
                oc = grp * 8 + m
                S.op("scalar", lambda s, o=fm_chunk(ffn_o, oc)[:, toff:toff + 512], i=e.ap:
                     s.dma_start(out=o, in_=i),
                     reads=[e], dma=True)

    S.fence_all("sync")
    S.build()
    return nc




SCALE0 = 1.0 / float(np.sqrt(2048.0))


def build_l0attn():
    nc = bass.Bass("TRN2", target_bir_lowering=False, debug=False)
    q_d = nc.dram_tensor("q", [D, T], BF16, kind="ExternalInput").ap()
    kf_d = nc.dram_tensor("kf", [D, 8 * T], BF16, kind="ExternalInput").ap()
    vf_d = nc.dram_tensor("vf", [8 * T, D], BF16, kind="ExternalInput").ap()
    x_d = nc.dram_tensor("x", [D, T], F32, kind="ExternalInput").ap()
    xn_d = nc.dram_tensor("xn", [D, T], F32, kind="ExternalInput").ap()
    ffn_d = nc.dram_tensor("ffn", [D, T], F32, kind="ExternalInput").ap()
    wo_d = nc.dram_tensor("wo", [D, D], F32, kind="ExternalInput").ap()
    qidx_d = nc.dram_tensor("qidx", [128, 8], F32, kind="ExternalInput").ap()
    xnew_o = nc.dram_tensor("xnew", [D, T], F32, kind="ExternalOutput").ap()

    iota512 = nc.inline_tensor(np.arange(512, dtype=np.float32).reshape(1, 512),
                               name="iota512").ap()
    ones_row_f = nc.inline_tensor(np.ones((1, 128), dtype=np.float32), name="ones_row_f").ap()
    ident_b = nc.inline_tensor(np.eye(128, dtype=BF), name="ident_b").ap()

    S = Sched(nc)
    q_sb = sb(nc, "q_sb", [128, NCH * T], BF16)
    yac = [sb(nc, f"yac{i}", [128, D], F32) for i in range(4)]
    kblk = [sb(nc, "kblk0", [128, NCH * 512], BF16), sb(nc, "kblk1", [128, NCH * 512], BF16)]
    vblk = [sb(nc, "vblk0", [128, 4 * D], BF16), sb(nc, "vblk1", [128, 4 * D], BF16)]
    yT = sb(nc, "yT", [128, NCH * 512], BF16)
    ssb = [sb(nc, "ssb0", [128, 512], F32), sb(nc, "ssb1", [128, 512], F32)]
    pen = [sb(nc, "pen0", [128, 512], F32), sb(nc, "pen1", [128, 512], F32)]
    pT = [sb(nc, "pT0", [128, 512], BF16), sb(nc, "pT1", [128, 512], BF16)]
    pbf = [sb(nc, "pbf0", [128, 512], BF16), sb(nc, "pbf1", [128, 512], BF16)]
    wob = [sb(nc, "wob0", [128, NCH * 128], BF16), sb(nc, "wob1", [128, NCH * 128], BF16)]
    sumb = [sb(nc, "sumb0", [128, 3 * 512], F32), sb(nc, "sumb1", [128, 3 * 512], F32)]
    evx = [sb(nc, "evx0", [128, 512], F32), sb(nc, "evx1", [128, 512], F32)]
    mrow = sb(nc, "mrow", [128, 16], F32)
    qidx_sb = sb(nc, "qidx_sb", [128, 8], F32)
    io_sb = sb(nc, "io_sb", [1, 512], F32)
    jrow = sb(nc, "jrow", [128, 512], F32)
    onesk1 = sb(nc, "onesk1", [1, 128], F32)
    idb = sb(nc, "idb", [128, 128], BF16)

    pool = nc.alloc_psum_tensor("pool", [128, 3584], F32)
    poolb = nc.alloc_psum_tensor("poolb", [128, 1024], BF16)
    pap = pool.ap()
    bap = poolb.ap()
    bank_owner = {}

    def take(lo, hi, view):
        olds = [bank_owner[i] for i in range(lo, hi) if i in bank_owner]
        b = alias_over(view, *olds)
        for i in range(lo, hi):
            bank_owner[i] = b
        return b

    S.op("sync", lambda s: s.dma_start(out=sb3(q_sb.ap, T), in_=fm_full(q_d)),
         writes=[q_sb], dma=True)
    S.op("sync", lambda s: s.dma_start(out=qidx_sb.ap, in_=qidx_d),
         writes=[qidx_sb], dma=True)
    S.op("sync", lambda s: s.dma_start(out=onesk1.ap, in_=ones_row_f),
         writes=[onesk1], dma=True)
    S.op("sync", lambda s: s.dma_start(out=idb.ap, in_=ident_b),
         writes=[idb], dma=True)
    S.op("sync", lambda s: s.dma_start(out=io_sb.ap, in_=iota512),
         writes=[io_sb], dma=True)
    for i in range(4):
        S.op("vector", lambda v, i2=yac[i].ap: v.memset(i2, 0.0), writes=[yac[i]])
    ps_j = take(2, 3, pap[:, 2 * 512:3 * 512])
    S.op("tensor", lambda t: t.matmul(ps_j.ap, onesk1.ap, io_sb.ap, start=True, stop=True),
         reads=[onesk1, io_sb], writes=[ps_j])
    S.op("vector", lambda v: v.tensor_copy(jrow.ap, ps_j.ap),
         reads=[ps_j], writes=[jrow])

    MR = mrow.ap

    def m_ap(qt):
        return MR[:, qt:qt + 1]

    def l_ap(qt):
        return MR[:, 4 + qt:5 + qt]
    negm = MR[:, 8:9]
    alpha = MR[:, 9:10]
    t0c = MR[:, 10:11]
    rl = MR[:, 11:12]
    bm = MR[:, 12:13]

    for g in range(2):
        nkb = 15 + g
        for qt in range(4):
            S.op("vector", lambda v, a=m_ap(qt): v.memset(a, -3e38), writes=[mrow])
            S.op("vector", lambda v, a=l_ap(qt): v.memset(a, 0.0), writes=[mrow])
        if g == 1:
            for i in range(4):
                S.op("vector", lambda v, i2=yac[i].ap: v.memset(i2, 0.0), writes=[yac[i]])
        for kb in range(nkb):
            kbu = kblk[kb % 2]
            vbu = vblk[kb % 2]
            S.op("sync", lambda s, o=sb3(kbu.ap, 512), i=fm_full(kf_d[:, kb * 512:(kb + 1) * 512]):
                 s.dma_start(out=o, in_=i),
                 writes=[kbu], dma=True)
            S.op("sync", lambda s, o=vbu.ap.rearrange("p (w d) -> p w d", d=D),
                 i=vf_d[kb * 512:(kb + 1) * 512, :].rearrange("(w p) d -> p w d", p=128):
                 s.dma_start(out=o, in_=i),
                 writes=[vbu], dma=True)
            for qt in range(4):
                ps_s = take(0 + qt % 2, 1 + qt % 2, pap[:, (qt % 2) * 512:(qt % 2 + 1) * 512])
                for c in range(NCH):
                    S.op("tensor", lambda t, o=ps_s.ap,
                         l=q_sb.ap[:, c * T + (g * 4 + qt) * 128:c * T + (g * 4 + qt) * 128 + 128],
                         r=kbu.ap[:, c * 512:(c + 1) * 512],
                         st=(c == 0), sp=(c == NCH - 1):
                         t.matmul(o, l, r, start=st, stop=sp),
                         reads=[q_sb, kbu], writes=_aw(c, NCH, ps_s))
                pe = pen[qt % 2]
                S.op("vector", lambda v, o=pe.ap, qc=qidx_sb.ap[:, g * 4 + qt:g * 4 + qt + 1],
                     kb512=float(kb * 512):
                     v.tensor_scalar(o, jrow.ap, qc, kb512, ALU.subtract, ALU.add),
                     reads=[jrow, qidx_sb], writes=[pe])
                S.op("vector", lambda v, o=pe.ap: v.tensor_scalar(o, o, 0.0, -60000.0, ALU.max, ALU.mult),
                     reads=[], writes=[pe])
                sb_s = ssb[qt % 2]
                S.op("vector", lambda v, o=sb_s.ap, i=ps_s.ap, p=pe.ap:
                     v.scalar_tensor_tensor(o, i, SCALE0, p, ALU.mult, ALU.add),
                     reads=[ps_s, pe], writes=[sb_s])
                S.op("vector", lambda v, i=sb_s.ap: v.tensor_reduce(bm, i, AX, ALU.max),
                     reads=[sb_s], writes=[mrow])
                S.op("vector", lambda v, m=m_ap(qt): v.tensor_tensor(t0c, m, bm, ALU.max),
                     reads=[mrow], writes=[mrow])
                S.op("vector", lambda v, m=m_ap(qt): v.tensor_tensor(alpha, m, t0c, ALU.subtract),
                     reads=[mrow], writes=[mrow])
                S.op("scalar", lambda s: s.activation(alpha, alpha, AF.Exp),
                     reads=[mrow], writes=[mrow])
                S.op("vector", lambda v, m=m_ap(qt): v.tensor_copy(m, t0c),
                     reads=[mrow], writes=[mrow])
                S.op("vector", lambda v: v.tensor_scalar(negm, t0c, -1.0, None, ALU.mult),
                     reads=[mrow], writes=[mrow])
                pb = pbf[qt % 2]
                S.op("scalar", lambda s, o=pb.ap, i=sb_s.ap: s.activation(
                     o, i, AF.Exp, bias=negm, accum_out=bm),
                     reads=[sb_s, mrow], writes=[pb, mrow])
                S.op("vector", lambda v, la=l_ap(qt): v.scalar_tensor_tensor(
                     la, la, alpha, bm, ALU.mult, ALU.add),
                     reads=[mrow], writes=[mrow])
                ps_tr = take(7, 8, bap[:, 0:512])
                pt = pT[qt % 2]
                for w in range(4):
                    S.op("tensor", lambda t, o=ps_tr.ap[:, w * 128:(w + 1) * 128],
                         i=pb.ap[:, w * 128:(w + 1) * 128]:
                         t.transpose(o, i, idb.ap),
                         reads=[pb, idb], writes=[ps_tr] if w in (0, 3) else [])
                S.op("vector", lambda v, o=pt.ap, i=ps_tr.ap: v.tensor_copy(o, i),
                     reads=[ps_tr], writes=[pt])
                for db in range(4):
                    ps_y = take(3 + db % 2, 4 + db % 2, pap[:, (3 + db % 2) * 512:(4 + db % 2) * 512])
                    for kc in range(4):
                        S.op("tensor", lambda t, o=ps_y.ap,
                             l=pt.ap[:, kc * 128:(kc + 1) * 128],
                             r=vbu.ap[:, kc * D + db * 512:kc * D + db * 512 + 512],
                             st=(kc == 0), sp=(kc == 3):
                             t.matmul(o, l, r, start=st, stop=sp),
                             reads=[pt, vbu], writes=_aw(kc, 4, ps_y))
                    S.op("vector", lambda v, ya=yac[qt].ap[:, db * 512:(db + 1) * 512],
                         i=ps_y.ap:
                         v.scalar_tensor_tensor(ya, ya, alpha, i, ALU.mult, ALU.add),
                         reads=[ps_y, mrow], writes=[yac[qt]])
        for qt in range(4):
            S.op("vector", lambda v, la=l_ap(qt): v.reciprocal(rl, la),
                 reads=[mrow], writes=[mrow])
            S.op("vector", lambda v, ya=yac[qt].ap: v.tensor_scalar(ya, ya, rl, None, ALU.mult),
                 reads=[mrow], writes=[yac[qt]])
            ps_tr = take(7, 8, bap[:, 0:512])
            for c in range(NCH):
                S.op("scalar", lambda s, o=pbf[0].ap[:, 0:128], i=yac[qt].ap[:, c * 128:(c + 1) * 128]:
                     s.copy(o, i),
                     reads=[yac[qt]], writes=[pbf[0]])
                S.op("tensor", lambda t, o=ps_tr.ap[:, 0:128], i=pbf[0].ap[:, 0:128]:
                     t.transpose(o, i, idb.ap),
                     reads=[pbf[0], idb], writes=[ps_tr])
                S.op("vector", lambda v, o=yT.ap[:, c * 512 + qt * 128:c * 512 + qt * 128 + 128],
                     i=ps_tr.ap[:, 0:128]:
                     v.tensor_copy(o, i),
                     reads=[ps_tr], writes=[yT])
        for m in range(NCH):
            w = wob[m % 2]
            wsrc = wo_d[:, m * 128:(m + 1) * 128].rearrange("(c p) n -> p c n", p=128)
            S.op("gpsimd", lambda g2, o=sb3(w.ap, 128), i=wsrc: g2.dma_start(out=o, in_=i),
                 writes=[w], dma=True)
            ps_o = take(5, 6, pap[:, 5 * 512:6 * 512])
            for c in range(NCH):
                S.op("tensor", lambda t, o=ps_o.ap, l=w.ap[:, c * 128:(c + 1) * 128],
                     r=yT.ap[:, c * 512:(c + 1) * 512],
                     st=(c == 0), sp=(c == NCH - 1):
                     t.matmul(o, l, r, start=st, stop=sp),
                     reads=[w, yT], writes=_aw(c, NCH, ps_o))
            sm = sumb[m % 2]
            off = g * 512
            S.op("sync", lambda s, o=sm.ap[:, 0:512], i=fm_chunk(x_d, m)[:, off:off + 512]:
                 s.dma_start(out=o, in_=i), writes=[sm], dma=True)
            S.op("sync", lambda s, o=sm.ap[:, 512:1024], i=fm_chunk(xn_d, m)[:, off:off + 512]:
                 s.dma_start(out=o, in_=i), writes=[sm], dma=True)
            S.op("sync", lambda s, o=sm.ap[:, 1024:1536], i=fm_chunk(ffn_d, m)[:, off:off + 512]:
                 s.dma_start(out=o, in_=i), writes=[sm], dma=True)
            e = evx[m % 2]
            S.op("vector", lambda v, o=e.ap, s2=sm.ap: v.tensor_tensor(
                 o, s2[:, 0:512], s2[:, 512:1024], ALU.add),
                 reads=[sm], writes=[e])
            S.op("vector", lambda v, o=e.ap, s2=sm.ap: v.tensor_tensor(
                 o, o, s2[:, 1024:1536], ALU.add),
                 reads=[sm], writes=[e])
            S.op("vector", lambda v, o=e.ap, i=ps_o.ap: v.tensor_tensor(o, o, i, ALU.add),
                 reads=[ps_o], writes=[e])
            S.op("scalar", lambda s, o=fm_chunk(xnew_o, m)[:, off:off + 512], i=e.ap:
                 s.dma_start(out=o, in_=i),
                 reads=[e], dma=True)

    S.fence_all("sync")
    S.build()
    return nc




def build_l1front():
    nc = bass.Bass("TRN2", target_bir_lowering=False, debug=False)
    x_d = nc.dram_tensor("x", [D, T], F32, kind="ExternalInput").ap()
    ipw_d = nc.dram_tensor("ipw", [5136, D], F32, kind="ExternalInput").ap()
    dtb_d = nc.dram_tensor("dtb", [16, 1], F32, kind="ExternalInput").ap()
    xn2_o = nc.dram_tensor("xn2", [D, T], F32, kind="ExternalOutput").ap()
    z_o = nc.dram_tensor("z", [D, T], BF16, kind="ExternalOutput").ap()
    xbc_o = nc.dram_tensor("xbc", [3072, T], F32, kind="ExternalOutput").ap()
    dts_o = nc.dram_tensor("dts", [16, T], F32, kind="ExternalOutput").ap()

    ipw_bf = Buf(nc.dram_tensor("ipw_bf", [5136, D], BF16, kind="Internal").ap())
    ones_row_f = nc.inline_tensor(np.ones((1, 128), dtype=np.float32), name="ones_row_f").ap()
    ones_col_bf = nc.inline_tensor(np.ones((128, 1), dtype=BF), name="ones_col_bf").ap()

    S = Sched(nc)
    x_sb = sb(nc, "x_sb", [128, NCH * T], F32)
    xn2b = sb(nc, "xn2b", [128, NCH * T], BF16)
    wT = [sb(nc, "wT0", [128, NCH * 512], BF16), sb(nc, "wT1", [128, NCH * 512], BF16)]
    scr = [sb(nc, "scr0", [128, T], BF16), sb(nc, "scr1", [128, T], BF16)]
    xc = [sb(nc, "xc0", [128, T], F32), sb(nc, "xc1", [128, T], F32)]
    bc0 = sb(nc, "bc0", [128, T], F32)
    bc1 = sb(nc, "bc1", [128, T], F32)
    rows = sb(nc, "rows", [1, 5 * T], F32)
    dtr = sb(nc, "dtr", [16, 6 * T], F32)
    ev = [sb(nc, "ev0", [128, T], F32), sb(nc, "ev1", [128, T], F32)]
    onesk1 = sb(nc, "onesk1", [1, 128], F32)
    onesb = sb(nc, "onesb", [128, 1], BF16)
    dtb_sb = sb(nc, "dtb_sb", [16, 1], F32)

    pool = nc.alloc_psum_tensor("pool", [128, 4096], F32)
    pap = pool.ap()
    bank_owner = {}

    def take(lo, hi, view):
        olds = [bank_owner[i] for i in range(lo, hi) if i in bank_owner]
        b = alias_over(view, *olds)
        for i in range(lo, hi):
            bank_owner[i] = b
        return b

    S.op("sync", lambda s: s.dma_start(out=sb3(x_sb.ap, T), in_=fm_full(x_d)),
         writes=[x_sb], dma=True)
    S.op("sync", lambda s: s.dma_start(out=onesk1.ap, in_=ones_row_f), writes=[onesk1], dma=True)
    S.op("sync", lambda s: s.dma_start(out=onesb.ap, in_=ones_col_bf), writes=[onesb], dma=True)
    S.op("sync", lambda s: s.dma_start(out=dtb_sb.ap, in_=dtb_d), writes=[dtb_sb], dma=True)
    for p in range(8):
        S.op("gpsimd", lambda g, o=ipw_bf.ap[p * 642:(p + 1) * 642, :],
             i=ipw_d[p * 642:(p + 1) * 642, :]: g.dma_start(out=o, in_=i),
             writes=[ipw_bf], dma=True)

    ps_sum = take(0, 2, pap[0:1, 0:1024])
    ps_sumsq = take(2, 4, pap[0:1, 1024:2048])
    for c in range(NCH):
        cb = scr[c % 2]
        xs = x_sb.ap[:, c * T:(c + 1) * T]
        S.op("vector", lambda v, o=cb.ap, i=xs: v.tensor_copy(o, i),
             reads=[x_sb], writes=[cb])
        for h in range(2):
            lo, hi_ = h * 512, (h + 1) * 512
            S.op("tensor", lambda t, o=ps_sum.ap[0:1, lo:hi_], l=onesb.ap,
                 r=cb.ap[:, lo:hi_], st=(c == 0), sp=(c == NCH - 1):
                 t.matmul(o, l, r, start=st, stop=sp),
                 reads=[cb, onesb], writes=_aw(c, NCH, ps_sum))
    for c in range(NCH):
        sq = scr[c % 2]
        xs = x_sb.ap[:, c * T:(c + 1) * T]
        S.op("scalar", lambda s, o=sq.ap, i=xs: s.activation(o, i, AF.Square),
             reads=[x_sb], writes=[sq])
        for h in range(2):
            lo, hi_ = h * 512, (h + 1) * 512
            S.op("tensor", lambda t, o=ps_sumsq.ap[0:1, lo:hi_], l=onesb.ap,
                 r=sq.ap[:, lo:hi_], st=(c == 0), sp=(c == NCH - 1):
                 t.matmul(o, l, r, start=st, stop=sp),
                 reads=[sq, onesb], writes=_aw(c, NCH, ps_sumsq))

    RW = rows.ap
    r_mu = RW[0:1, 0:T]
    r_var = RW[0:1, T:2 * T]
    r_t0 = RW[0:1, 2 * T:3 * T]
    r_rstd = RW[0:1, 3 * T:4 * T]
    r_mr = RW[0:1, 4 * T:5 * T]
    S.op("vector", lambda v: v.tensor_scalar(r_mu, ps_sum.ap, 1.0 / D, None, ALU.mult),
         reads=[ps_sum], writes=[rows])
    S.op("vector", lambda v: v.tensor_scalar(r_t0, ps_sumsq.ap, 1.0 / D, None, ALU.mult),
         reads=[ps_sumsq], writes=[rows])
    S.op("vector", lambda v: v.tensor_tensor(r_var, r_mu, r_mu, ALU.mult),
         reads=[rows], writes=[rows])
    S.op("vector", lambda v: v.tensor_tensor(r_var, r_t0, r_var, ALU.subtract),
         reads=[rows], writes=[rows])
    S.op("vector", lambda v: v.tensor_scalar(r_var, r_var, 1e-5, None, ALU.add),
         reads=[rows], writes=[rows])
    S.op("scalar", lambda s: s.activation(r_t0, r_var, AF.Sqrt),
         reads=[rows], writes=[rows])
    S.op("vector", lambda v: v.reciprocal(r_rstd, r_t0),
         reads=[rows], writes=[rows])
    S.op("vector", lambda v: v.tensor_tensor(r_mr, r_mu, r_rstd, ALU.mult),
         reads=[rows], writes=[rows])
    ps_bc = take(4, 5, pap[:, 4 * 512:5 * 512])
    bcast_row(S, rows, r_rstd, bc0, onesk1, ps_bc, T)
    bcast_row(S, rows, r_mr, bc1, onesk1, ps_bc, T)

    for c in range(NCH):
        xb = xc[c % 2]
        xs = x_sb.ap[:, c * T:(c + 1) * T]
        S.op("vector", lambda v, o=xb.ap, i=xs: v.tensor_tensor(o, i, bc0.ap, ALU.mult),
             reads=[x_sb, bc0], writes=[xb])
        S.op("vector", lambda v, o=xb.ap: v.tensor_tensor(o, o, bc1.ap, ALU.subtract),
             reads=[bc1], writes=[xb])
        S.op("sync", lambda s, o=fm_chunk(xn2_o, c), i=xb.ap: s.dma_start(out=o, in_=i),
             reads=[xb], dma=True)
        S.op("vector", lambda v, o=xn2b.ap[:, c * T:(c + 1) * T], i=xb.ap: v.tensor_copy(o, i),
             reads=[xb], writes=[xn2b])

    for blk in range(10):
        w = wT[blk % 2]
        for dc in range(NCH):
            S.op("sync", lambda s, o=w.ap[:, dc * 512:(dc + 1) * 512],
                 i=ipw_bf.ap[blk * 512:(blk + 1) * 512, dc * 128:(dc + 1) * 128]:
                 s.dma_start_transpose(o, i),
                 reads=[ipw_bf], writes=[w], dma=True)
        for m in range(4):
            oc = blk * 4 + m
            pm = take(5 + m % 2, 6 + m % 2, pap[:, (5 + m % 2) * 512:(6 + m % 2) * 512])
            e = ev[m % 2]
            for h in range(2):
                lo, hi_ = h * 512, (h + 1) * 512
                for dc in range(NCH):
                    S.op("tensor", lambda t, o=pm.ap, w=w,
                         l=w.ap[:, dc * 512 + m * 128:dc * 512 + m * 128 + 128],
                         r=xn2b.ap[:, dc * T + lo:dc * T + hi_],
                         st=(dc == 0), sp=(dc == NCH - 1):
                         t.matmul(o, l, r, start=st, stop=sp),
                         reads=[w, xn2b], writes=_aw(dc, NCH, pm))
                if oc < NCH:
                    S.op("vector", lambda v, o=scr[0].ap[:, lo:hi_], i=pm.ap: v.tensor_copy(o, i),
                         reads=[pm], writes=[scr[0]])
                else:
                    S.op("vector", lambda v, o=e.ap[:, lo:hi_], i=pm.ap: v.tensor_copy(o, i),
                         reads=[pm], writes=[e])
            if oc < NCH:
                S.op("scalar", lambda s, o=fm_chunk(z_o, oc), i=scr[0].ap:
                     s.dma_start(out=o, in_=i),
                     reads=[scr[0]], dma=True)
            else:
                S.op("scalar", lambda s, o=fm_chunk(xbc_o, oc - NCH), i=e.ap:
                     s.dma_start(out=o, in_=i),
                     reads=[e], dma=True)
    wdt = wT[0]
    for dc in range(NCH):
        S.op("sync", lambda s, o=wdt.ap[:, dc * 16:(dc + 1) * 16],
             i=ipw_bf.ap[5120:5136, dc * 128:(dc + 1) * 128]:
             s.dma_start_transpose(o, i),
             reads=[ipw_bf], writes=[wdt], dma=True)
    DTR = dtr.ap
    d_raw = DTR[0:16, 0:T]
    d_a = DTR[0:16, T:2 * T]
    d_e = DTR[0:16, 2 * T:3 * T]
    d_r = DTR[0:16, 3 * T:4 * T]
    d_s = DTR[0:16, 4 * T:5 * T]
    pm = take(5, 6, pap[:, 5 * 512:6 * 512])
    for h in range(2):
        lo, hi_ = h * 512, (h + 1) * 512
        for dc in range(NCH):
            S.op("tensor", lambda t, o=pm.ap[0:16, 0:512],
                 l=wdt.ap[:, dc * 16:(dc + 1) * 16],
                 r=xn2b.ap[:, dc * T + lo:dc * T + hi_],
                 st=(dc == 0), sp=(dc == NCH - 1):
                 t.matmul(o, l, r, start=st, stop=sp),
                 reads=[wdt, xn2b], writes=_aw(dc, NCH, pm))
        S.op("vector", lambda v, o=d_raw[0:16, lo:hi_], i=pm.ap[0:16, 0:512]:
             v.tensor_copy(o, i),
             reads=[pm], writes=[dtr])
    S.op("vector", lambda v: v.tensor_scalar(d_raw, d_raw, dtb_sb.ap, None, ALU.add),
         reads=[dtb_sb], writes=[dtr])
    S.op("scalar", lambda s: s.activation(d_a, d_raw, AF.Abs),
         reads=[dtr], writes=[dtr])
    S.op("scalar", lambda s: s.activation(d_e, d_a, AF.Exp, scale=-1.0),
         reads=[dtr], writes=[dtr])
    S.op("vector", lambda v: v.tensor_scalar(d_e, d_e, 1.0, None, ALU.add),
         reads=[dtr], writes=[dtr])
    S.op("scalar", lambda s: s.activation(d_e, d_e, AF.Ln),
         reads=[dtr], writes=[dtr])
    S.op("vector", lambda v: v.tensor_scalar(d_r, d_raw, 0.0, None, ALU.max),
         reads=[dtr], writes=[dtr])
    S.op("vector", lambda v: v.tensor_tensor(d_s, d_r, d_e, ALU.add),
         reads=[dtr], writes=[dtr])
    S.op("scalar", lambda s, o=dts_o: s.dma_start(out=o, in_=d_s),
         reads=[dtr], dma=True)

    S.fence_all("sync")
    S.build()
    return nc




def build_l1conv():
    nc = bass.Bass("TRN2", target_bir_lowering=False, debug=False)
    xbc_d = nc.dram_tensor("xbc", [3072, T], F32, kind="ExternalInput").ap()
    halo_d = nc.dram_tensor("halo", [3072, 3], F32, kind="ExternalInput").ap()
    cw_d = nc.dram_tensor("cw", [128, 96], F32, kind="ExternalInput").ap()
    cb_d = nc.dram_tensor("cb", [128, 24], F32, kind="ExternalInput").ap()
    dts_d = nc.dram_tensor("dts", [16, T], F32, kind="ExternalInput").ap()
    alog_d = nc.dram_tensor("alog", [16, 1], F32, kind="ExternalInput").ap()
    xbcp_o = nc.dram_tensor("xbcp", [3072, T], BF16, kind="ExternalOutput").ap()
    s_o = nc.dram_tensor("s", [128, 4 * 16 * 128], BF16, kind="ExternalOutput").ap()
    dec_o = nc.dram_tensor("dec", [1, 64], F32, kind="ExternalOutput").ap()

    ut_f = nc.inline_tensor(np.triu(np.ones((128, 128), dtype=np.float32)), name="ut_f").ap()
    on_f = nc.inline_tensor(np.ones((128, 128), dtype=np.float32), name="on_f").ap()
    ident_f = nc.inline_tensor(np.eye(128, dtype=np.float32), name="ident_f").ap()
    ident_b = nc.inline_tensor(np.eye(128, dtype=BF), name="ident_b").ap()
    ones_row_f = nc.inline_tensor(np.ones((1, 128), dtype=np.float32), name="ones_row_f").ap()

    S = Sched(nc)
    xbcp = sb(nc, "xbcp_sb", [128, 24 * T], BF16)
    xe = [sb(nc, "xe0", [128, T + 3], F32), sb(nc, "xe1", [128, T + 3], F32)]
    acc = [sb(nc, "acc0", [128, T], F32), sb(nc, "acc1", [128, T], F32)]
    cw_sb = sb(nc, "cw_sb", [128, 96], F32)
    cb_sb = sb(nc, "cb_sb", [128, 24], F32)
    dts_sb = sb(nc, "dts_sb", [16, T], F32)
    arow = sb(nc, "arow", [16, 8], F32)
    datm = sb(nc, "datm", [128, 8 * 16], F32)
    dttm = sb(nc, "dttm", [128, 8 * 16], F32)
    dxtm = sb(nc, "dxtm", [128, 8 * 16], F32)
    comb = sb(nc, "comb", [128, 8 * 16], F32)
    dafm = sb(nc, "dafm", [16, 256], F32)
    x_tm = sb(nc, "x_tm", [128, 8 * 128], BF16)
    b_tm = sb(nc, "b_tm", [128, 8 * 128], BF16)
    xdd = sb(nc, "xdd", [128, 2 * 128], BF16)
    sev = [sb(nc, "sev0", [128, 128], BF16), sb(nc, "sev1", [128, 128], BF16)]
    tot_row = sb(nc, "tot_row", [1, 16], F32)
    totb = sb(nc, "totb", [128, 16], F32)
    dec_row = sb(nc, "dec_row", [1, 64], F32)
    onesk1 = sb(nc, "onesk1", [1, 128], F32)
    idf = sb(nc, "idf", [128, 128], F32)
    idb = sb(nc, "idb", [128, 128], BF16)
    utf = sb(nc, "utf", [128, 128], F32)
    onf = sb(nc, "onf", [128, 128], F32)

    pool = nc.alloc_psum_tensor("pool", [128, 3072], F32)
    poolb = nc.alloc_psum_tensor("poolb", [128, 1024], BF16)
    pap = pool.ap()
    bap = poolb.ap()
    bank_owner = {}

    def take(lo, hi, view):
        olds = [bank_owner[i] for i in range(lo, hi) if i in bank_owner]
        b = alias_over(view, *olds)
        for i in range(lo, hi):
            bank_owner[i] = b
        return b

    for name, dst, srcap in (("cw", cw_sb, cw_d), ("cb", cb_sb, cb_d), ("dts", dts_sb, dts_d),
                             ("ok1", onesk1, ones_row_f), ("idf", idf, ident_f),
                             ("idb", idb, ident_b), ("utf", utf, ut_f), ("onf", onf, on_f)):
        S.op("sync", lambda s, o=dst.ap, i=srcap: s.dma_start(out=o, in_=i),
             writes=[dst], dma=True)
    S.op("sync", lambda s: s.dma_start(out=arow.ap[0:16, 0:1], in_=alog_d),
         writes=[arow], dma=True)
    S.op("scalar", lambda s: s.activation(arow.ap[0:16, 1:2], arow.ap[0:16, 0:1], AF.Exp),
         reads=[arow], writes=[arow])
    S.op("vector", lambda v: v.tensor_scalar(arow.ap[0:16, 1:2], arow.ap[0:16, 1:2],
                                             -1.0, None, ALU.mult),
         reads=[arow], writes=[arow])

    for cc in range(24):
        x = xe[cc % 2]
        a = acc[cc % 2]
        S.op("sync", lambda s, o=x.ap[:, 0:3], i=halo_d[cc * 128:(cc + 1) * 128, :]:
             s.dma_start(out=o, in_=i), writes=[x], dma=True)
        S.op("sync", lambda s, o=x.ap[:, 3:T + 3], i=xbc_d[cc * 128:(cc + 1) * 128, :]:
             s.dma_start(out=o, in_=i), writes=[x], dma=True)
        S.op("vector", lambda v, x=x, a=a, wc=cw_sb.ap[:, cc * 4:cc * 4 + 1]:
             v.tensor_scalar(a.ap, x.ap[:, 0:T], wc, None, ALU.mult),
             reads=[x, cw_sb], writes=[a])
        for k2 in range(1, 4):
            S.op("vector", lambda v, x=x, a=a, k2=k2, wc=cw_sb.ap[:, cc * 4 + k2:cc * 4 + k2 + 1]:
                 v.scalar_tensor_tensor(a.ap, x.ap[:, k2:T + k2], wc, a.ap, ALU.mult, ALU.add),
                 reads=[x, cw_sb], writes=[a])
        S.op("scalar", lambda s, a=a, o=xbcp.ap[:, cc * T:(cc + 1) * T],
             bc=cb_sb.ap[:, cc:cc + 1]:
             s.activation(o, a.ap, AF.Silu, bias=bc),
             reads=[a, cb_sb], writes=[xbcp])
        S.op("scalar", lambda s, o=xbcp_o[cc * 128:(cc + 1) * 128, :],
             i=xbcp.ap[:, cc * T:(cc + 1) * T]:
             s.dma_start(out=o, in_=i),
             reads=[xbcp], dma=True)

    for j in range(4):
        S.op("vector", lambda v, j=j: v.tensor_scalar(
             dafm.ap, dts_sb.ap[0:16, j * 256:(j + 1) * 256], arow.ap[0:16, 1:2], None, ALU.mult),
             reads=[dts_sb, arow], writes=[dafm])
        S.op("vector", lambda v, j=j: v.tensor_reduce(
             arow.ap[0:16, 2 + j:3 + j], dafm.ap, AX, ALU.add),
             reads=[dafm], writes=[arow])
        ps_t = take(4, 6, bap[:, 0:1024])
        for st2 in range(2):
            ps_f = take(0, 1, pap[:, 0:512])
            S.op("tensor", lambda t, o=ps_f.ap[:, 0:16],
                 i=dafm.ap[0:16, st2 * 128:(st2 + 1) * 128], st2=st2:
                 t.transpose(o, i, idf.ap[0:16, 0:16]),
                 reads=[dafm, idf], writes=[ps_f])
            S.op("vector", lambda v, o=datm.ap[:, (j * 2 + st2) * 16:(j * 2 + st2 + 1) * 16],
                 i=ps_f.ap[:, 0:16]: v.tensor_copy(o, i),
                 reads=[ps_f], writes=[datm])
            S.op("tensor", lambda t, o=ps_f.ap[:, 16:32],
                 i=dts_sb.ap[0:16, j * 256 + st2 * 128:j * 256 + (st2 + 1) * 128]:
                 t.transpose(o, i, idf.ap[0:16, 0:16]),
                 reads=[dts_sb, idf], writes=[ps_f])
            S.op("vector", lambda v, o=dttm.ap[:, (j * 2 + st2) * 16:(j * 2 + st2 + 1) * 16],
                 i=ps_f.ap[:, 16:32]: v.tensor_copy(o, i),
                 reads=[ps_f], writes=[dttm])
        ps_a = take(1, 2, pap[:, 512:1024])
        S.op("tensor", lambda t, o=ps_a.ap[:, 0:16], l=utf.ap,
             r=datm.ap[:, (j * 2) * 16:(j * 2 + 1) * 16]:
             t.matmul(o, l, r, start=True, stop=True),
             reads=[utf, datm], writes=[ps_a])
        S.op("tensor", lambda t, o=ps_a.ap[:, 16:32], l=onf.ap,
             r=datm.ap[:, (j * 2) * 16:(j * 2 + 1) * 16]:
             t.matmul(o, l, r, start=True, stop=False),
             reads=[onf, datm], writes=[ps_a])
        S.op("tensor", lambda t, o=ps_a.ap[:, 16:32], l=utf.ap,
             r=datm.ap[:, (j * 2 + 1) * 16:(j * 2 + 2) * 16]:
             t.matmul(o, l, r, start=False, stop=True),
             reads=[utf, datm], writes=[ps_a])
        ps_r = take(2, 3, pap[:, 1024:1536])
        S.op("tensor", lambda t, o=ps_r.ap[0:1, 0:16], i=arow.ap[0:16, 2 + j:3 + j]:
             t.transpose(o, i, idf.ap[0:16, 0:16]),
             reads=[arow, idf], writes=[ps_r])
        S.op("vector", lambda v: v.tensor_copy(tot_row.ap, ps_r.ap[0:1, 0:16]),
             reads=[ps_r], writes=[tot_row])
        ps_b = take(2, 3, pap[:, 1024:1536])
        S.op("tensor", lambda t, o=ps_b.ap[:, 0:16], l=onesk1.ap, r=tot_row.ap:
             t.matmul(o, l, r, start=True, stop=True),
             reads=[onesk1, tot_row], writes=[ps_b])
        S.op("vector", lambda v: v.tensor_copy(totb.ap, ps_b.ap[:, 0:16]),
             reads=[ps_b], writes=[totb])
        for st2 in range(2):
            S.op("vector", lambda v, st2=st2,
                 o=dxtm.ap[:, (j * 2 + st2) * 16:(j * 2 + st2 + 1) * 16],
                 i=ps_a.ap[:, st2 * 16:(st2 + 1) * 16]:
                 v.tensor_tensor(o, totb.ap, i, ALU.subtract),
                 reads=[ps_a, totb], writes=[dxtm])
            S.op("scalar", lambda s, st2=st2,
                 o=dxtm.ap[:, (j * 2 + st2) * 16:(j * 2 + st2 + 1) * 16]:
                 s.activation(o, o, AF.Exp),
                 reads=[dxtm], writes=[dxtm])
            S.op("vector", lambda v, st2=st2,
                 o=comb.ap[:, (j * 2 + st2) * 16:(j * 2 + st2 + 1) * 16],
                 a=dttm.ap[:, (j * 2 + st2) * 16:(j * 2 + st2 + 1) * 16],
                 b=dxtm.ap[:, (j * 2 + st2) * 16:(j * 2 + st2 + 1) * 16]:
                 v.tensor_tensor(o, a, b, ALU.mult),
                 reads=[dttm, dxtm], writes=[comb])
        for kv in range(4):
            for st2 in range(2):
                ps_tb = take(4, 6, bap[:, 0:1024])
                S.op("tensor", lambda t, o=ps_tb.ap[:, 0:128],
                     i=xbcp.ap[:, kv * T + j * 256 + st2 * 128:kv * T + j * 256 + (st2 + 1) * 128]:
                     t.transpose(o, i, idb.ap),
                     reads=[xbcp, idb], writes=[ps_tb])
                S.op("vector", lambda v, o=x_tm.ap[:, (kv * 2 + st2) * 128:(kv * 2 + st2 + 1) * 128],
                     i=ps_tb.ap[:, 0:128]: v.tensor_copy(o, i),
                     reads=[ps_tb], writes=[x_tm])
                S.op("tensor", lambda t, o=ps_tb.ap[:, 128:256],
                     i=xbcp.ap[:, (4 + kv) * T + j * 256 + st2 * 128:(4 + kv) * T + j * 256 + (st2 + 1) * 128]:
                     t.transpose(o, i, idb.ap),
                     reads=[xbcp, idb], writes=[ps_tb])
                S.op("vector", lambda v, o=b_tm.ap[:, (kv * 2 + st2) * 128:(kv * 2 + st2 + 1) * 128],
                     i=ps_tb.ap[:, 128:256]: v.tensor_copy(o, i),
                     reads=[ps_tb], writes=[b_tm])
        for h in range(16):
            kv = h // 4
            for st2 in range(2):
                S.op("vector", lambda v, st2=st2, kv=kv, h=h,
                     o=xdd.ap[:, st2 * 128:(st2 + 1) * 128],
                     xi=x_tm.ap[:, (kv * 2 + st2) * 128:(kv * 2 + st2 + 1) * 128],
                     cc2=comb.ap[:, (j * 2 + st2) * 16 + h:(j * 2 + st2) * 16 + h + 1]:
                     v.tensor_scalar(o, xi, cc2, None, ALU.mult),
                     reads=[x_tm, comb], writes=[xdd])
            ps_s = take(3 if h % 2 == 0 else 0, 4 if h % 2 == 0 else 1,
                        pap[:, (3 if h % 2 == 0 else 0) * 512:(4 if h % 2 == 0 else 1) * 512])
            for st2 in range(2):
                S.op("tensor", lambda t, st2=st2, kv=kv, o=ps_s.ap[:, 0:128],
                     l=b_tm.ap[:, (kv * 2 + st2) * 128:(kv * 2 + st2 + 1) * 128],
                     r=xdd.ap[:, st2 * 128:(st2 + 1) * 128],
                     st=(st2 == 0), sp=(st2 == 1):
                     t.matmul(o, l, r, start=st, stop=sp),
                     reads=[b_tm, xdd], writes=_aw(st2, 2, ps_s))
            e = sev[h % 2]
            S.op("vector", lambda v, o=e.ap, i=ps_s.ap[:, 0:128]: v.tensor_copy(o, i),
                 reads=[ps_s], writes=[e])
            S.op("scalar", lambda s, o=s_o[:, (j * 16 + h) * 128:(j * 16 + h + 1) * 128], i=e.ap:
                 s.dma_start(out=o, in_=i),
                 reads=[e], dma=True)
    for j in range(4):
        S.op("scalar", lambda s, j=j: s.activation(
             arow.ap[0:16, 6:7], arow.ap[0:16, 2 + j:3 + j], AF.Exp),
             reads=[arow], writes=[arow])
        ps_r = take(2, 3, pap[:, 1024:1536])
        S.op("tensor", lambda t, o=ps_r.ap[0:1, 0:16], i=arow.ap[0:16, 6:7]:
             t.transpose(o, i, idf.ap[0:16, 0:16]),
             reads=[arow, idf], writes=[ps_r])
        S.op("vector", lambda v, j=j: v.tensor_copy(dec_row.ap[0:1, j * 16:(j + 1) * 16],
                                                    ps_r.ap[0:1, 0:16]),
             reads=[ps_r], writes=[dec_row])
    S.op("sync", lambda s: s.dma_start(out=dec_o, in_=dec_row.ap),
         reads=[dec_row], dma=True)

    S.fence_all("sync")
    S.build()
    return nc




def build_l1ssd(debug=False):
    nc = bass.Bass("TRN2", target_bir_lowering=False, debug=False)
    xbcp_d = nc.dram_tensor("xbcp", [3072, T], BF16, kind="ExternalInput").ap()
    dts_d = nc.dram_tensor("dts", [16, T], F32, kind="ExternalInput").ap()
    z_d = nc.dram_tensor("z", [D, T], BF16, kind="ExternalInput").ap()
    sin_d = nc.dram_tensor("sin", [128, 32 * 2048], BF16, kind="ExternalInput").ap()
    din_d = nc.dram_tensor("din", [1, 512], F32, kind="ExternalInput").ap()
    sown_d = nc.dram_tensor("sown", [128, 8192], BF16, kind="ExternalInput").ap()
    dec_d = nc.dram_tensor("dec", [1, 64], F32, kind="ExternalInput").ap()
    alog_d = nc.dram_tensor("alog", [16, 1], F32, kind="ExternalInput").ap()
    dD_d = nc.dram_tensor("dD", [1, 16], F32, kind="ExternalInput").ap()
    nw_d = nc.dram_tensor("nw", [128, 16], F32, kind="ExternalInput").ap()
    opw_d = nc.dram_tensor("opw", [D, D], F32, kind="ExternalInput").ap()
    a1_o = nc.dram_tensor("attn1", [D, T], F32, kind="ExternalOutput").ap()
    if debug:
        ydbg_o = nc.dram_tensor("ydbg", [D, T], F32, kind="ExternalOutput").ap()
        gtd_o = nc.dram_tensor("gtdbg", [128, 512], F32, kind="ExternalOutput").ap()

    opw_bf = Buf(nc.dram_tensor("opw_bf", [D, D], BF16, kind="Internal").ap())
    p0 = (np.arange(256)[None, :] >= np.arange(128)[:, None]).astype(np.float32)
    p1 = (np.arange(256)[None, :] >= (128 + np.arange(128))[:, None]).astype(np.float32)
    P0 = nc.inline_tensor(p0, name="P0").ap()
    P1 = nc.inline_tensor(p1, name="P1").ap()
    bd = np.zeros((16, 16 * 128), dtype=np.float32)
    for h in range(16):
        bd[h, h * 128:(h + 1) * 128] = 1.0
    BD = nc.inline_tensor(bd, name="BD").ap()
    ident_f = nc.inline_tensor(np.eye(128, dtype=np.float32), name="ident_f").ap()
    ident_b = nc.inline_tensor(np.eye(128, dtype=BF), name="ident_b").ap()
    ones_row_f = nc.inline_tensor(np.ones((1, 128), dtype=np.float32), name="ones_row_f").ap()
    ones_col_bf = nc.inline_tensor(np.ones((128, 1), dtype=BF), name="ones_col_bf").ap()

    S = Sched(nc)
    xbcp = sb(nc, "xbcp_sb", [128, 24 * T], BF16)
    z_sb = sb(nc, "z_sb", [128, NCH * T], BF16)
    yn = sb(nc, "yn", [128, NCH * T], BF16)
    H = sb(nc, "H", [128, 2048], F32)
    sinb = [sb(nc, "sin0", [128, 2048], BF16), sb(nc, "sin1", [128, 2048], BF16)]
    sob = [sb(nc, "sob0", [128, 2048], BF16), sb(nc, "sob1", [128, 2048], BF16)]
    dts_sb = sb(nc, "dts_sb", [16, T], F32)
    dinb = sb(nc, "dinb", [128, 512], F32)
    cdb = sb(nc, "cdb", [128, 64], F32)
    Db = sb(nc, "Db", [128, 16], F32)
    nw_sb = sb(nc, "nw_sb", [128, 16], F32)
    arow = sb(nc, "arow", [16, 4], F32)
    dafm = sb(nc, "dafm", [16, 256], F32)
    acsfm = sb(nc, "acsfm", [16, 256], F32)
    acstm = sb(nc, "acstm", [128, 32], F32)
    dttm = sb(nc, "dttm", [128, 32], F32)
    x_tm = sb(nc, "x_tm", [128, 8 * 128], BF16)
    xdt = sb(nc, "xdt", [128, 2 * 128], BF16)
    gT = sb(nc, "gT", [128, 2 * 256], BF16)
    ce = sb(nc, "ce", [128, 256], BF16)
    eacs = sb(nc, "eacs", [128, 256], BF16)
    hbf = sb(nc, "hbf", [128, 128], BF16)
    scr1 = sb(nc, "scr1", [128, 256], F32)
    scr2 = sb(nc, "scr2", [128, 256], F32)
    ysb = sb(nc, "ysb", [128, 256], F32)
    gsb = sb(nc, "gsb", [128, 256], F32)
    szb = sb(nc, "szb", [128, 256], BF16)
    sqb = sb(nc, "sqb", [128, 256], BF16)
    rrow = sb(nc, "rrow", [1, 256], F32)
    wT = [sb(nc, "wT0", [128, NCH * 512], BF16), sb(nc, "wT1", [128, NCH * 512], BF16)]
    ev = [sb(nc, "ev0", [128, 512], F32), sb(nc, "ev1", [128, 512], F32)]
    bd_sb = sb(nc, "bd_sb", [16, 16 * 128], F32)
    p0_sb = sb(nc, "p0_sb", [128, 256], F32)
    p1_sb = sb(nc, "p1_sb", [128, 256], F32)
    onesk1 = sb(nc, "onesk1", [1, 128], F32)
    onesb = sb(nc, "onesb", [128, 1], BF16)
    idf = sb(nc, "idf", [128, 128], F32)
    idb = sb(nc, "idb", [128, 128], BF16)

    pool = nc.alloc_psum_tensor("pool", [128, 3584], F32)
    poolb = nc.alloc_psum_tensor("poolb", [128, 1024], BF16)
    pap = pool.ap()
    bap = poolb.ap()
    bank_owner = {}

    def take(lo, hi, view):
        olds = [bank_owner[i] for i in range(lo, hi) if i in bank_owner]
        b = alias_over(view, *olds)
        for i in range(lo, hi):
            bank_owner[i] = b
        return b

    S.op("sync", lambda s: s.dma_start(out=sb3(xbcp.ap, T), in_=fm_full(xbcp_d)),
         writes=[xbcp], dma=True)
    S.op("sync", lambda s: s.dma_start(out=sb3(z_sb.ap, T), in_=fm_full(z_d)),
         writes=[z_sb], dma=True)
    for dst, srcap in ((dts_sb, dts_d), (nw_sb, nw_d), (bd_sb, BD), (p0_sb, P0),
                       (p1_sb, P1), (onesk1, ones_row_f), (onesb, ones_col_bf),
                       (idf, ident_f), (idb, ident_b)):
        S.op("sync", lambda s, o=dst.ap, i=srcap: s.dma_start(out=o, in_=i),
             writes=[dst], dma=True)
    S.op("sync", lambda s: s.dma_start(out=arow.ap[0:16, 0:1], in_=alog_d),
         writes=[arow], dma=True)
    for p in range(4):
        S.op("gpsimd", lambda g, o=opw_bf.ap[p * 512:(p + 1) * 512, :],
             i=opw_d[p * 512:(p + 1) * 512, :]: g.dma_start(out=o, in_=i),
             writes=[opw_bf], dma=True)
    S.op("scalar", lambda s: s.activation(arow.ap[0:16, 1:2], arow.ap[0:16, 0:1], AF.Exp),
         reads=[arow], writes=[arow])
    S.op("vector", lambda v: v.tensor_scalar(arow.ap[0:16, 1:2], arow.ap[0:16, 1:2],
                                             -1.0, None, ALU.mult),
         reads=[arow], writes=[arow])
    dinr = sb(nc, "dinr", [1, 512], F32)
    decr = sb(nc, "decr", [1, 64], F32)
    dDr = sb(nc, "dDr", [1, 16], F32)
    for dst, srcap in ((dinr, din_d), (decr, dec_d), (dDr, dD_d)):
        S.op("sync", lambda s, o=dst.ap, i=srcap: s.dma_start(out=o, in_=i),
             writes=[dst], dma=True)
    ps_b = take(5, 6, pap[:, 5 * 512:6 * 512])
    S.op("tensor", lambda t: t.matmul(ps_b.ap, onesk1.ap, dinr.ap, start=True, stop=True),
         reads=[onesk1, dinr], writes=[ps_b])
    S.op("vector", lambda v: v.tensor_copy(dinb.ap, ps_b.ap), reads=[ps_b], writes=[dinb])
    ps_b2 = take(5, 6, pap[:, 5 * 512:6 * 512])
    S.op("tensor", lambda t: t.matmul(ps_b2.ap[:, 0:64], onesk1.ap, decr.ap, start=True, stop=True),
         reads=[onesk1, decr], writes=[ps_b2])
    S.op("vector", lambda v: v.tensor_copy(cdb.ap, ps_b2.ap[:, 0:64]), reads=[ps_b2], writes=[cdb])
    ps_b3 = take(5, 6, pap[:, 5 * 512:6 * 512])
    S.op("tensor", lambda t: t.matmul(ps_b3.ap[:, 0:16], onesk1.ap, dDr.ap, start=True, stop=True),
         reads=[onesk1, dDr], writes=[ps_b3])
    S.op("vector", lambda v: v.tensor_copy(Db.ap, ps_b3.ap[:, 0:16]), reads=[ps_b3], writes=[Db])

    S.op("vector", lambda v: v.memset(H.ap, 0.0), writes=[H])
    for c in range(32):
        sc = sinb[c % 2]
        S.op("sync", lambda s, o=sc.ap, i=sin_d[:, c * 2048:(c + 1) * 2048]:
             s.dma_start(out=o, in_=i),
             writes=[sc], dma=True)
        for h in range(16):
            S.op("vector", lambda v, c=c, h=h, sc=sc:
                 v.scalar_tensor_tensor(H.ap[:, h * 128:(h + 1) * 128],
                                        H.ap[:, h * 128:(h + 1) * 128],
                                        dinb.ap[:, c * 16 + h:c * 16 + h + 1],
                                        sc.ap[:, h * 128:(h + 1) * 128],
                                        ALU.mult, ALU.add),
                 reads=[sc, dinb], writes=[H])

    mask_sb = [p0_sb, p1_sb]
    for j in range(4):
        soj = sob[j % 2]
        S.op("sync", lambda s, o=soj.ap, i=sown_d[:, j * 2048:(j + 1) * 2048]:
             s.dma_start(out=o, in_=i),
             writes=[soj], dma=True)
        S.op("vector", lambda v, j=j: v.tensor_scalar(
             dafm.ap, dts_sb.ap[0:16, j * 256:(j + 1) * 256], arow.ap[0:16, 1:2], None, ALU.mult),
             reads=[dts_sb, arow], writes=[dafm])
        ps_f = take(6, 7, pap[:, 6 * 512:7 * 512])
        for st2 in range(2):
            S.op("tensor", lambda t, st2=st2, o=ps_f.ap[:, st2 * 16:(st2 + 1) * 16],
                 i=dafm.ap[0:16, st2 * 128:(st2 + 1) * 128]:
                 t.transpose(o, i, idf.ap[0:16, 0:16]),
                 reads=[dafm, idf], writes=[ps_f])
            S.op("tensor", lambda t, st2=st2, j=j, o=ps_f.ap[:, 32 + st2 * 16:32 + (st2 + 1) * 16],
                 i=dts_sb.ap[0:16, j * 256 + st2 * 128:j * 256 + (st2 + 1) * 128]:
                 t.transpose(o, i, idf.ap[0:16, 0:16]),
                 reads=[dts_sb, idf], writes=[ps_f])
        S.op("vector", lambda v: v.tensor_copy(acstm.ap[:, 0:32], ps_f.ap[:, 0:32]),
             reads=[ps_f], writes=[acstm])
        S.op("vector", lambda v: v.tensor_copy(dttm.ap, ps_f.ap[:, 32:64]),
             reads=[ps_f], writes=[dttm])
        ps_af = take(6, 7, pap[:, 6 * 512:7 * 512])
        S.op("tensor", lambda t, o=ps_af.ap[0:16, 0:256], l=acstm.ap[:, 0:16], r=p0_sb.ap:
             t.matmul(o, l, r, start=True, stop=False),
             reads=[acstm, p0_sb], writes=[ps_af])
        S.op("tensor", lambda t, o=ps_af.ap[0:16, 0:256], l=acstm.ap[:, 16:32], r=p1_sb.ap:
             t.matmul(o, l, r, start=False, stop=True),
             reads=[acstm, p1_sb], writes=[ps_af])
        S.op("vector", lambda v: v.tensor_copy(acsfm.ap, ps_af.ap[0:16, 0:256]),
             reads=[ps_af], writes=[acsfm])
        ps_f2 = take(6, 7, pap[:, 6 * 512:7 * 512])
        for st2 in range(2):
            S.op("tensor", lambda t, st2=st2, o=ps_f2.ap[:, st2 * 16:(st2 + 1) * 16],
                 i=acsfm.ap[0:16, st2 * 128:(st2 + 1) * 128]:
                 t.transpose(o, i, idf.ap[0:16, 0:16]),
                 reads=[acsfm, idf], writes=[ps_f2])
        S.op("vector", lambda v: v.tensor_copy(acstm.ap, ps_f2.ap[:, 0:32]),
             reads=[ps_f2], writes=[acstm])
        for kv in range(4):
            for st2 in range(2):
                ps_tb = take(7, 8, bap[:, 0:512])
                S.op("tensor", lambda t, kv=kv, st2=st2, j=j, o=ps_tb.ap[:, 0:128],
                     i=xbcp.ap[:, kv * T + j * 256 + st2 * 128:kv * T + j * 256 + (st2 + 1) * 128]:
                     t.transpose(o, i, idb.ap),
                     reads=[xbcp, idb], writes=[ps_tb])
                S.op("vector", lambda v, kv=kv, st2=st2,
                     o=x_tm.ap[:, (kv * 2 + st2) * 128:(kv * 2 + st2 + 1) * 128],
                     i=ps_tb.ap[:, 0:128]: v.tensor_copy(o, i),
                     reads=[ps_tb], writes=[x_tm])
        for h in range(16):
            kv = h // 4
            ps_ab = take(0, 1, pap[:, 0:512])
            S.op("tensor", lambda t, h=h, o=ps_ab.ap[:, 0:256],
                 l=bd_sb.ap[0:16, h * 128:(h + 1) * 128], r=acsfm.ap:
                 t.matmul(o, l, r, start=True, stop=True),
                 reads=[bd_sb, acsfm], writes=[ps_ab])
            S.op("scalar", lambda s, o=eacs.ap, i=ps_ab.ap[:, 0:256]:
                 s.activation(o, i, AF.Exp),
                 reads=[ps_ab], writes=[eacs])
            S.op("vector", lambda v, h=h, j=j, o=ce.ap,
                 i=xbcp.ap[:, (8 + h) * T + j * 256:(8 + h) * T + (j + 1) * 256]:
                 v.tensor_tensor(o, i, eacs.ap, ALU.mult),
                 reads=[xbcp, eacs], writes=[ce])
            ps_g = [take(1, 2, pap[:, 512:1024]), take(2, 3, pap[:, 1024:1536])]
            for st2 in range(2):
                S.op("tensor", lambda t, st2=st2, kv=kv, h=h, j=j, o=ps_g[st2].ap[:, 0:256],
                     l=xbcp.ap[:, (4 + kv) * T + j * 256 + st2 * 128:(4 + kv) * T + j * 256 + (st2 + 1) * 128],
                     r=xbcp.ap[:, (8 + h) * T + j * 256:(8 + h) * T + (j + 1) * 256]:
                     t.matmul(o, l, r, start=True, stop=True),
                     reads=[xbcp], writes=[ps_g[st2]])
            for st2 in range(2):
                S.op("vector", lambda v, st2=st2, h=h, o=scr1.ap, i=ps_ab.ap[:, 0:256]:
                     v.tensor_scalar(o, i, acstm.ap[:, st2 * 16 + h:st2 * 16 + h + 1], None, ALU.subtract),
                     reads=[ps_ab, acstm], writes=[scr1])
                S.op("vector", lambda v, st2=st2, o=scr1.ap, m=mask_sb[st2].ap:
                     v.tensor_tensor(o, o, m, ALU.mult),
                     reads=[mask_sb[st2]], writes=[scr1])
                S.op("scalar", lambda s: s.activation(scr2.ap, scr1.ap, AF.Exp),
                     reads=[scr1], writes=[scr2])
                S.op("vector", lambda v, st2=st2, h=h, o=scr2.ap, g2=ps_g[st2].ap[:, 0:256]:
                     v.scalar_tensor_tensor(o, o, dttm.ap[:, st2 * 16 + h:st2 * 16 + h + 1],
                                            g2, ALU.mult, ALU.mult),
                     reads=[ps_g[st2], dttm], writes=[scr2])
                S.op("vector", lambda v, st2=st2, o=gT.ap[:, st2 * 256:(st2 + 1) * 256],
                     m=mask_sb[st2].ap:
                     v.tensor_tensor(o, scr2.ap, m, ALU.mult),
                     reads=[scr2, mask_sb[st2]], writes=[gT])
            S.op("vector", lambda v, h=h: v.tensor_copy(hbf.ap, H.ap[:, h * 128:(h + 1) * 128]),
                 reads=[H], writes=[hbf])
            ps_y = take(3 + h % 2, 4 + h % 2, pap[:, (3 + h % 2) * 512:(4 + h % 2) * 512])
            for st2 in range(2):
                S.op("tensor", lambda t, st2=st2, kv=kv, o=ps_y.ap[:, 0:256],
                     l=x_tm.ap[:, (kv * 2 + st2) * 128:(kv * 2 + st2 + 1) * 128],
                     r=gT.ap[:, st2 * 256:(st2 + 1) * 256],
                     st=(st2 == 0):
                     t.matmul(o, l, r, start=st, stop=False),
                     reads=[x_tm, gT], writes=[ps_y] if st2 == 0 else [])
            S.op("tensor", lambda t, o=ps_y.ap[:, 0:256], l=hbf.ap, r=ce.ap:
                 t.matmul(o, l, r, start=False, stop=True),
                 reads=[hbf, ce], writes=[ps_y])
            S.op("vector", lambda v, h=h, j=j, kv=kv, o=ysb.ap,
                 xi=xbcp.ap[:, kv * T + j * 256:kv * T + (j + 1) * 256],
                 i=ps_y.ap[:, 0:256]:
                 v.scalar_tensor_tensor(o, xi, Db.ap[:, h:h + 1], i, ALU.mult, ALU.add),
                 reads=[xbcp, Db, ps_y], writes=[ysb])
            if debug:
                S.op("sync", lambda s, h=h, j=j, o=ydbg_o[h * 128:(h + 1) * 128, j * 256:(j + 1) * 256],
                     i=ysb.ap: s.dma_start(out=o, in_=i),
                     reads=[ysb], dma=True)
                if h == 0 and j == 0:
                    S.op("vector", lambda v: v.tensor_copy(scr1.ap, gT.ap[:, 0:256]),
                         reads=[gT], writes=[scr1])
                    S.op("vector", lambda v: v.tensor_copy(scr2.ap, gT.ap[:, 256:512]),
                         reads=[gT], writes=[scr2])
                    S.op("sync", lambda s: s.dma_start(out=gtd_o[:, 0:256], in_=scr1.ap),
                         reads=[scr1], dma=True)
                    S.op("sync", lambda s: s.dma_start(out=gtd_o[:, 256:512], in_=scr2.ap),
                         reads=[scr2], dma=True)
            S.op("scalar", lambda s, h=h, j=j, o=szb.ap,
                 i=z_sb.ap[:, h * T + j * 256:h * T + (j + 1) * 256]:
                 s.activation(o, i, AF.Silu),
                 reads=[z_sb], writes=[szb])
            S.op("vector", lambda v: v.tensor_tensor(gsb.ap, ysb.ap, szb.ap, ALU.mult),
                 reads=[ysb, szb], writes=[gsb])
            S.op("scalar", lambda s: s.activation(sqb.ap, gsb.ap, AF.Square),
                 reads=[gsb], writes=[sqb])
            ps_r = take(5, 6, pap[:, 5 * 512:6 * 512])
            S.op("tensor", lambda t, o=ps_r.ap[0:1, 0:256]: t.matmul(
                 o, onesb.ap, sqb.ap, start=True, stop=True),
                 reads=[onesb, sqb], writes=[ps_r])
            S.op("vector", lambda v: v.tensor_scalar(rrow.ap, ps_r.ap[0:1, 0:256],
                                                     1.0 / 128.0, 1e-5, ALU.mult, ALU.add),
                 reads=[ps_r], writes=[rrow])
            S.op("scalar", lambda s: s.activation(rrow.ap, rrow.ap, AF.Sqrt),
                 reads=[rrow], writes=[rrow])
            S.op("vector", lambda v: v.reciprocal(rrow.ap, rrow.ap),
                 reads=[rrow], writes=[rrow])
            ps_rb = take(5, 6, pap[:, 5 * 512:6 * 512])
            S.op("tensor", lambda t, o=ps_rb.ap[:, 0:256]: t.matmul(
                 o, onesk1.ap, rrow.ap, start=True, stop=True),
                 reads=[onesk1, rrow], writes=[ps_rb])
            S.op("vector", lambda v, h=h, j=j,
                 o=yn.ap[:, h * T + j * 256:h * T + (j + 1) * 256],
                 i=ps_rb.ap[:, 0:256]:
                 v.scalar_tensor_tensor(o, gsb.ap, nw_sb.ap[:, h:h + 1], i, ALU.mult, ALU.mult),
                 reads=[gsb, nw_sb, ps_rb], writes=[yn])
        for h in range(16):
            S.op("vector", lambda v, h=h, j=j, soj=soj:
                 v.scalar_tensor_tensor(H.ap[:, h * 128:(h + 1) * 128],
                                        H.ap[:, h * 128:(h + 1) * 128],
                                        cdb.ap[:, j * 16 + h:j * 16 + h + 1],
                                        soj.ap[:, h * 128:(h + 1) * 128],
                                        ALU.mult, ALU.add),
                 reads=[soj, cdb], writes=[H])

    for blk in range(4):
        w = wT[blk % 2]
        for dc in range(NCH):
            S.op("sync", lambda s, blk=blk, dc=dc, o=w.ap[:, dc * 512:(dc + 1) * 512],
                 i=opw_bf.ap[blk * 512:(blk + 1) * 512, dc * 128:(dc + 1) * 128]:
                 s.dma_start_transpose(o, i),
                 reads=[opw_bf], writes=[w], dma=True)
        for m in range(4):
            oc = blk * 4 + m
            pm = take(m % 2, m % 2 + 1, pap[:, (m % 2) * 512:(m % 2 + 1) * 512])
            for hh in range(2):
                lo, hi_ = hh * 512, (hh + 1) * 512
                e = ev[hh]
                for dc in range(NCH):
                    S.op("tensor", lambda t, pm=pm,
                         l=w.ap[:, dc * 512 + m * 128:dc * 512 + m * 128 + 128],
                         r=yn.ap[:, dc * T + lo:dc * T + hi_],
                         st=(dc == 0), sp=(dc == NCH - 1):
                         t.matmul(pm.ap, l, r, start=st, stop=sp),
                         reads=[w, yn], writes=_aw(dc, NCH, pm))
                S.op("vector", lambda v, e=e, pm=pm: v.tensor_copy(e.ap, pm.ap),
                     reads=[pm], writes=[e])
                S.op("scalar", lambda s, e=e, o=fm_chunk(a1_o, oc)[:, lo:hi_], i=e.ap:
                     s.dma_start(out=o, in_=i),
                     reads=[e], dma=True)

    S.fence_all("sync")
    S.build()
    return nc




def build_l1moe():
    nc = bass.Bass("TRN2", target_bir_lowering=False, debug=False)
    xnew_d = nc.dram_tensor("xnew", [D, T], F32, kind="ExternalInput").ap()
    xn2_d = nc.dram_tensor("xn2", [D, T], F32, kind="ExternalInput").ap()
    a1_d = nc.dram_tensor("attn1", [D, T], F32, kind="ExternalInput").ap()
    sw1_d = nc.dram_tensor("sw1", [2 * D, 1408], F32, kind="ExternalInput").ap()
    sb1_d = nc.dram_tensor("sb1", [128, 2 * 11], F32, kind="ExternalInput").ap()
    sw2_d = nc.dram_tensor("sw2", [2 * 1408, D], F32, kind="ExternalInput").ap()
    sb2_d = nc.dram_tensor("sb2", [128, 2 * 16], F32, kind="ExternalInput").ap()
    rw1_d = nc.dram_tensor("rw1", [16 * D, 1408], F32, kind="ExternalInput").ap()
    rb1_d = nc.dram_tensor("rb1", [128, 16 * 11], F32, kind="ExternalInput").ap()
    rw2_d = nc.dram_tensor("rw2", [16 * 1408, D], F32, kind="ExternalInput").ap()
    rb2_d = nc.dram_tensor("rb2", [16, D], F32, kind="ExternalInput").ap()
    cent_d = nc.dram_tensor("cent", [16, D], F32, kind="ExternalInput").ap()
    cbias_d = nc.dram_tensor("cbias", [1, 16], F32, kind="ExternalInput").ap()
    fin_o = nc.dram_tensor("fin", [D, T], F32, kind="ExternalOutput").ap()

    bd = np.zeros((16, 16 * 128), dtype=np.float32)
    for h in range(16):
        bd[h, h * 128:(h + 1) * 128] = 1.0
    BD = nc.inline_tensor(bd, name="BD").ap()
    ident_f = nc.inline_tensor(np.eye(128, dtype=np.float32), name="ident_f").ap()
    ones_row_f = nc.inline_tensor(np.ones((1, 128), dtype=np.float32), name="ones_row_f").ap()

    S = Sched(nc)
    xn2b = sb(nc, "xn2b", [128, NCH * T], BF16)
    acc = sb(nc, "acc", [128, NCH * T], F32)
    h_sb = sb(nc, "h_sb", [128, 11 * T], BF16)
    w1b = [sb(nc, "w1b0", [128, NCH * 128], BF16), sb(nc, "w1b1", [128, NCH * 128], BF16)]
    w2b = [sb(nc, "w2b0", [128, 11 * 128], BF16), sb(nc, "w2b1", [128, 11 * 128], BF16)]
    gb = sb(nc, "gb", [128, T], F32)
    xcf = [sb(nc, "xcf0", [128, T], F32), sb(nc, "xcf1", [128, T], F32)]
    centT = sb(nc, "centT", [128, 16 * 16], F32)
    cent_sb = sb(nc, "cent_sb", [16, D], F32)
    sc_t = sb(nc, "sc_t", [128, 16], F32)
    eqv = sb(nc, "eqv", [128, 16], F32)
    gv = sb(nc, "gv", [128, 16], F32)
    mrow = sb(nc, "mrow", [128, 4], F32)
    biasb = sb(nc, "biasb", [128, 16], F32)
    b2s = sb(nc, "b2s", [128, 16], F32)
    sb1c = sb(nc, "sb1c", [128, 2 * 11], F32)
    sb2c = sb(nc, "sb2c", [128, 2 * 16], F32)
    rb1c = sb(nc, "rb1c", [128, 16 * 11], F32)
    rb2b = sb(nc, "rb2b", [16, D], BF16)
    g_fm = sb(nc, "g_fm", [16, 8 * 128], F32)
    g_fmb = sb(nc, "g_fmb", [16, 8 * 128], BF16)
    cb_row = sb(nc, "cb_row", [1, 16], F32)
    sumb = [sb(nc, "sumb0", [128, 3 * 512], F32), sb(nc, "sumb1", [128, 3 * 512], F32)]
    evx = [sb(nc, "evx0", [128, 512], F32), sb(nc, "evx1", [128, 512], F32)]
    bd_sb = sb(nc, "bd_sb", [16, 16 * 128], F32)
    onesk1 = sb(nc, "onesk1", [1, 128], F32)
    idf = sb(nc, "idf", [128, 128], F32)

    pool = nc.alloc_psum_tensor("pool", [128, 4096], F32)
    pap = pool.ap()
    bank_owner = {}

    def take(lo, hi, view):
        olds = [bank_owner[i] for i in range(lo, hi) if i in bank_owner]
        b = alias_over(view, *olds)
        for i in range(lo, hi):
            bank_owner[i] = b
        return b

    for dst, srcap in ((sb1c, sb1_d), (sb2c, sb2_d), (rb1c, rb1_d),
                       (cb_row, cbias_d), (bd_sb, BD), (onesk1, ones_row_f), (idf, ident_f)):
        S.op("sync", lambda s, o=dst.ap, i=srcap: s.dma_start(out=o, in_=i),
             writes=[dst], dma=True)
    S.op("gpsimd", lambda g: g.dma_start(out=rb2b.ap, in_=rb2_d),
         writes=[rb2b], dma=True)
    S.op("sync", lambda s: s.dma_start(out=cent_sb.ap, in_=cent_d),
         writes=[cent_sb], dma=True)
    ps_ct = take(6, 7, pap[:, 6 * 512:7 * 512])
    for dc in range(NCH):
        S.op("tensor", lambda t, dc=dc, o=ps_ct.ap[:, 0:16],
             i=cent_sb.ap[0:16, dc * 128:(dc + 1) * 128]:
             t.transpose(o, i, idf.ap[0:16, 0:16]),
             reads=[cent_sb, idf], writes=[ps_ct])
        S.op("vector", lambda v, dc=dc, o=centT.ap[:, dc * 16:(dc + 1) * 16],
             i=ps_ct.ap[:, 0:16]: v.tensor_copy(o, i),
             reads=[ps_ct], writes=[centT])
    ps_sc = take(5, 6, pap[:, 5 * 512:6 * 512])
    for c in range(NCH):
        xf = xcf[c % 2]
        S.op("sync", lambda s, o=xf.ap, i=fm_chunk(xn2_d, c): s.dma_start(out=o, in_=i),
             writes=[xf], dma=True)
        S.op("vector", lambda v, c=c, xf=xf: v.tensor_copy(xn2b.ap[:, c * T:(c + 1) * T], xf.ap),
             reads=[xf], writes=[xn2b])
        for tt in range(8):
            S.op("tensor", lambda t, c=c, tt=tt, xf=xf, o=ps_sc.ap[:, tt * 16:(tt + 1) * 16],
                 l=xf.ap[:, tt * 128:(tt + 1) * 128],
                 r=centT.ap[:, c * 16:(c + 1) * 16],
                 st=(c == 0 and tt == 0), sp=(c == NCH - 1 and tt == 7):
                 t.matmul(o, l, r, start=st, stop=sp),
                 reads=[xf, centT],
                 writes=[ps_sc] if (c == 0 and tt == 0) or (c == NCH - 1 and tt == 7) else [])
    S.op("vector", lambda v: v.tensor_tensor(b2s.ap, sb2c.ap[:, 0:16], sb2c.ap[:, 16:32], ALU.add),
         reads=[sb2c], writes=[b2s])
    ps_b = take(4, 5, pap[:, 4 * 512:5 * 512])
    S.op("tensor", lambda t: t.matmul(ps_b.ap[:, 0:16], onesk1.ap, cb_row.ap, start=True, stop=True),
         reads=[onesk1, cb_row], writes=[ps_b])
    S.op("vector", lambda v: v.tensor_copy(biasb.ap, ps_b.ap[:, 0:16]),
         reads=[ps_b], writes=[biasb])

    for tt in range(8):
        S.op("scalar", lambda s, tt=tt: s.activation(sc_t.ap, ps_sc.ap[:, tt * 16:(tt + 1) * 16], AF.Sigmoid),
             reads=[ps_sc], writes=[sc_t])
        S.op("vector", lambda v: v.tensor_tensor(sc_t.ap, sc_t.ap, biasb.ap, ALU.add),
             reads=[biasb], writes=[sc_t])
        S.op("vector", lambda v: v.memset(gv.ap, 0.0), writes=[gv])
        S.op("vector", lambda v: v.memset(mrow.ap[:, 1:2], 0.0), writes=[mrow])
        for r in range(4):
            S.op("vector", lambda v: v.tensor_reduce(mrow.ap[:, 0:1], sc_t.ap, AX, ALU.max),
                 reads=[sc_t], writes=[mrow])
            S.op("vector", lambda v: v.tensor_scalar(eqv.ap, sc_t.ap, mrow.ap[:, 0:1], None, ALU.subtract),
                 reads=[sc_t, mrow], writes=[eqv])
            S.op("vector", lambda v: v.tensor_scalar(eqv.ap, eqv.ap, 1e12, 1.0, ALU.mult, ALU.add),
                 reads=[], writes=[eqv])
            S.op("vector", lambda v: v.tensor_scalar(eqv.ap, eqv.ap, 0.0, None, ALU.max),
                 reads=[], writes=[eqv])
            S.op("vector", lambda v: v.scalar_tensor_tensor(gv.ap, eqv.ap, mrow.ap[:, 0:1],
                                                            gv.ap, ALU.mult, ALU.add),
                 reads=[eqv, mrow], writes=[gv])
            S.op("vector", lambda v: v.scalar_tensor_tensor(sc_t.ap, eqv.ap, -1e4,
                                                            sc_t.ap, ALU.mult, ALU.add),
                 reads=[eqv], writes=[sc_t])
            S.op("vector", lambda v: v.tensor_tensor(mrow.ap[:, 1:2], mrow.ap[:, 1:2],
                                                     mrow.ap[:, 0:1], ALU.add),
                 reads=[], writes=[mrow])
        S.op("vector", lambda v: v.tensor_scalar(mrow.ap[:, 2:3], mrow.ap[:, 1:2], 1e-8, None, ALU.max),
             reads=[], writes=[mrow])
        S.op("vector", lambda v: v.reciprocal(mrow.ap[:, 3:4], mrow.ap[:, 2:3]),
             reads=[], writes=[mrow])
        S.op("vector", lambda v: v.tensor_scalar(gv.ap, gv.ap, mrow.ap[:, 3:4], None, ALU.mult),
             reads=[mrow], writes=[gv])
        ps_t = take(6, 7, pap[:, 6 * 512:7 * 512])
        S.op("tensor", lambda t, o=ps_t.ap[0:16, 0:128]: t.transpose(o, gv.ap, idf.ap),
             reads=[gv, idf], writes=[ps_t])
        S.op("vector", lambda v, tt=tt: v.tensor_copy(g_fm.ap[0:16, tt * 128:(tt + 1) * 128],
                                                      ps_t.ap[0:16, 0:128]),
             reads=[ps_t], writes=[g_fm])
    S.op("vector", lambda v: v.tensor_copy(g_fmb.ap, g_fm.ap),
         reads=[g_fm], writes=[g_fmb])

    def expert(eidx, w1ap, w2ap, b1col_off, b1buf, gated, j, first):
        for m in range(11):
            w = w1b[m % 2]
            wsrc = w1ap[:, m * 128:(m + 1) * 128].rearrange("(c p) n -> p c n", p=128)
            S.op("gpsimd", lambda g, o=sb3(w.ap, 128), i=wsrc: g.dma_start(out=o, in_=i),
                 writes=[w], dma=True)
            pm = take(0 + m % 2, 1 + m % 2, pap[:, (m % 2) * 512:(m % 2 + 1) * 512])
            for hh in range(2):
                lo, hi_ = hh * 512, (hh + 1) * 512
                for dc in range(NCH):
                    S.op("tensor", lambda t, pm=pm, w=w,
                         l=w.ap[:, dc * 128:(dc + 1) * 128],
                         r=xn2b.ap[:, dc * T + lo:dc * T + hi_],
                         st=(dc == 0), sp=(dc == NCH - 1):
                         t.matmul(pm.ap, l, r, start=st, stop=sp),
                         reads=[w, xn2b], writes=_aw(dc, NCH, pm))
                S.op("scalar", lambda s, pm=pm, m=m, lo=lo, hi_=hi_,
                     o=h_sb.ap[:, m * T + lo:m * T + hi_],
                     bc=b1buf.ap[:, b1col_off + m:b1col_off + m + 1]:
                     s.activation(o, pm.ap, AF.Relu, bias=bc),
                     reads=[pm, b1buf], writes=[h_sb])
        if gated:
            ps_g = take(2, 3, pap[:, 2 * 512:3 * 512])
            for hh in range(2):
                lo, hi_ = hh * 512, (hh + 1) * 512
                S.op("tensor", lambda t, j=j, lo=lo, hi_=hi_, o=ps_g.ap,
                     l=bd_sb.ap[0:16, j * 128:(j + 1) * 128], r=g_fm.ap[0:16, lo:hi_]:
                     t.matmul(o, l, r, start=True, stop=True),
                     reads=[bd_sb, g_fm], writes=[ps_g])
                S.op("vector", lambda v, lo=lo, hi_=hi_, o=gb.ap[:, lo:hi_], i=ps_g.ap:
                     v.tensor_copy(o, i),
                     reads=[ps_g], writes=[gb])
            for m in range(11):
                S.op("vector", lambda v, m=m, o=h_sb.ap[:, m * T:(m + 1) * T]:
                     v.tensor_tensor(o, o, gb.ap, ALU.mult),
                     reads=[gb], writes=[h_sb])
        for mo in range(NCH):
            w = w2b[mo % 2]
            wsrc = w2ap[:, mo * 128:(mo + 1) * 128].rearrange("(c p) n -> p c n", p=128)
            S.op("gpsimd", lambda g, o=w.ap.rearrange("p (c n) -> p c n", n=128), i=wsrc:
                 g.dma_start(out=o, in_=i),
                 writes=[w], dma=True)
            pm = take(0 + mo % 2, 1 + mo % 2, pap[:, (mo % 2) * 512:(mo % 2 + 1) * 512])
            for hh in range(2):
                lo, hi_ = hh * 512, (hh + 1) * 512
                for dc in range(11):
                    S.op("tensor", lambda t, pm=pm, w=w,
                         l=w.ap[:, dc * 128:(dc + 1) * 128],
                         r=h_sb.ap[:, dc * T + lo:dc * T + hi_],
                         st=(dc == 0), sp=(dc == 10):
                         t.matmul(pm.ap, l, r, start=st, stop=sp),
                         reads=[w, h_sb], writes=_aw(dc, 11, pm))
                if first:
                    S.op("vector", lambda v, pm=pm, mo=mo, lo=lo, hi_=hi_,
                         o=acc.ap[:, mo * T + lo:mo * T + hi_],
                         bc=b2s.ap[:, mo:mo + 1]:
                         v.tensor_scalar(o, pm.ap, bc, None, ALU.add),
                         reads=[pm, b2s], writes=[acc])
                else:
                    S.op("vector", lambda v, pm=pm, mo=mo, lo=lo, hi_=hi_,
                         o=acc.ap[:, mo * T + lo:mo * T + hi_]:
                         v.tensor_tensor(o, o, pm.ap, ALU.add),
                         reads=[pm], writes=[acc])

    expert(0, sw1_d[0:D, :], sw2_d[0:1408, :], 0, sb1c, False, 0, True)
    expert(1, sw1_d[D:2 * D, :], sw2_d[1408:2 * 1408, :], 11, sb1c, False, 0, False)
    for j in range(16):
        expert(2 + j, rw1_d[j * D:(j + 1) * D, :], rw2_d[j * 1408:(j + 1) * 1408, :],
               j * 11, rb1c, True, j, False)

    for mo in range(NCH):
        pm = take(0 + mo % 2, 1 + mo % 2, pap[:, (mo % 2) * 512:(mo % 2 + 1) * 512])
        for hh in range(2):
            lo, hi_ = hh * 512, (hh + 1) * 512
            S.op("tensor", lambda t, pm=pm, mo=mo, lo=lo, hi_=hi_,
                 l=rb2b.ap[0:16, mo * 128:(mo + 1) * 128], r=g_fmb.ap[0:16, lo:hi_]:
                 t.matmul(pm.ap, l, r, start=True, stop=True),
                 reads=[rb2b, g_fmb], writes=[pm])
            S.op("vector", lambda v, pm=pm, mo=mo, lo=lo, hi_=hi_,
                 o=acc.ap[:, mo * T + lo:mo * T + hi_]:
                 v.tensor_tensor(o, o, pm.ap, ALU.add),
                 reads=[pm], writes=[acc])

    for c in range(NCH):
        for hh in range(2):
            lo, hi_ = hh * 512, (hh + 1) * 512
            sm = sumb[(c * 2 + hh) % 2]
            e = evx[(c * 2 + hh) % 2]
            S.op("sync", lambda s, o=sm.ap[:, 0:512], i=fm_chunk(xnew_d, c)[:, lo:hi_]:
                 s.dma_start(out=o, in_=i), writes=[sm], dma=True)
            S.op("sync", lambda s, o=sm.ap[:, 512:1024], i=fm_chunk(xn2_d, c)[:, lo:hi_]:
                 s.dma_start(out=o, in_=i), writes=[sm], dma=True)
            S.op("sync", lambda s, o=sm.ap[:, 1024:1536], i=fm_chunk(a1_d, c)[:, lo:hi_]:
                 s.dma_start(out=o, in_=i), writes=[sm], dma=True)
            S.op("vector", lambda v, sm=sm, e=e: v.tensor_tensor(
                 e.ap, sm.ap[:, 0:512], sm.ap[:, 512:1024], ALU.add),
                 reads=[sm], writes=[e])
            S.op("vector", lambda v, sm=sm, e=e: v.tensor_tensor(
                 e.ap, e.ap, sm.ap[:, 1024:1536], ALU.add),
                 reads=[sm], writes=[e])
            S.op("vector", lambda v, c=c, lo=lo, hi_=hi_, e=e: v.tensor_tensor(
                 e.ap, e.ap, acc.ap[:, c * T + lo:c * T + hi_], ALU.add),
                 reads=[acc], writes=[e])
            S.op("scalar", lambda s, c=c, lo=lo, hi_=hi_, e=e,
                 o=fm_chunk(fin_o, c)[:, lo:hi_]:
                 s.dma_start(out=o, in_=e.ap),
                 reads=[e], dma=True)

    S.fence_all("sync")
    S.build()
    return nc


_NC_CACHE = {}


def _get_nc(name, builder):
    if name not in _NC_CACHE:
        _NC_CACHE[name] = builder()
    return _NC_CACHE[name]


def kernel(**d):
    cores = list(range(NCORE))
    x2d = d["L0_attn_x"][0]

    nc1 = _get_nc("l0qkv", build_l0qkv)
    im = [{"x": fm_shard(x2d, i), "qkvw": d["L0_attn_qkv"],
           "gamma": col128(d["L0_attn_gamma"]), "beta": col128(d["L0_attn_beta"])}
          for i in cores]
    r1 = run_bass_kernel_spmd(nc1, im, core_ids=cores).results

    nc2 = _get_nc("l0ffn", build_l0ffn)
    im = [{"xn": r1[i]["xn"], "wfc": d["L0_ffn_w_fc"], "wpr": d["L0_ffn_w_proj"]}
          for i in cores]
    r2 = run_bass_kernel_spmd(nc2, im, core_ids=cores).results

    nc3 = _get_nc("l0attn", build_l0attn)
    kf = np.ascontiguousarray(np.concatenate([r1[i]["k"] for i in cores], axis=1))
    vf = np.ascontiguousarray(np.concatenate([r1[i]["v"].T for i in cores], axis=0))
    im = []
    for i in cores:
        qidx = (np.arange(128)[:, None] + i * T +
                (np.arange(8)[None, :] * 128)).astype(np.float32)
        im.append({"q": r1[i]["q"], "kf": kf, "vf": vf, "x": fm_shard(x2d, i),
                   "xn": r1[i]["xn"], "ffn": r2[i]["ffn"],
                   "wo": d["L0_attn_o_proj"], "qidx": qidx})
    r3 = run_bass_kernel_spmd(nc3, im, core_ids=cores).results

    nc4 = _get_nc("l1front", build_l1front)
    im = [{"x": r3[i]["xnew"], "ipw": d["L1_attn_in_proj_w"],
           "dtb": np.ascontiguousarray(d["L1_attn_dt_bias"].reshape(16, 1))}
          for i in cores]
    r4 = run_bass_kernel_spmd(nc4, im, core_ids=cores).results

    nc5 = _get_nc("l1conv", build_l1conv)
    cw = np.ascontiguousarray(d["L1_attn_conv_w"].reshape(24, 128, 4).transpose(1, 0, 2).reshape(128, 96))
    cb = np.ascontiguousarray(d["L1_attn_conv_b"].reshape(24, 128).T)
    alog = np.ascontiguousarray(d["L1_attn_A_log"].reshape(16, 1))
    im = []
    for i in cores:
        halo = (np.zeros((3072, 3), np.float32) if i == 0
                else np.ascontiguousarray(r4[i - 1]["xbc"][:, T - 3:T]))
        im.append({"xbc": r4[i]["xbc"], "halo": halo, "cw": cw, "cb": cb,
                   "dts": r4[i]["dts"], "alog": alog})
    r5 = run_bass_kernel_spmd(nc5, im, core_ids=cores).results

    nc6 = _get_nc("l1ssd", build_l1ssd)
    s_full = np.concatenate([r5[i]["s"] for i in cores], axis=1)
    dec_all = np.stack([r5[i]["dec"][0] for i in cores], axis=0)
    im = []
    for i in cores:
        sin = s_full.copy()
        sin[:, i * 4 * 2048:] = 0
        din = np.ones((1, 512), np.float32)
        din[0, :i * 4 * 16] = dec_all[:i].reshape(-1)[:i * 4 * 16]
        im.append({"xbcp": r5[i]["xbcp"], "dts": r4[i]["dts"], "z": r4[i]["z"],
                   "sin": sin, "din": din, "sown": r5[i]["s"], "dec": r5[i]["dec"],
                   "alog": alog,
                   "dD": np.ascontiguousarray(d["L1_attn_D"].reshape(1, 16)),
                   "nw": col128(d["L1_attn_norm_w"]),
                   "opw": d["L1_attn_out_proj_w"]})
    r6 = run_bass_kernel_spmd(nc6, im, core_ids=cores).results

    nc7 = _get_nc("l1moe", build_l1moe)
    moe_w = {
        "sw1": np.ascontiguousarray(d["L1_ffn_shared_w1"].reshape(2 * D, 1408)),
        "sb1": np.ascontiguousarray(d["L1_ffn_shared_b1"].reshape(2, 11, 128).transpose(2, 0, 1).reshape(128, 22)),
        "sw2": np.ascontiguousarray(d["L1_ffn_shared_w2"].reshape(2 * 1408, D)),
        "sb2": np.ascontiguousarray(d["L1_ffn_shared_b2"].reshape(2, 16, 128).transpose(2, 0, 1).reshape(128, 32)),
        "rw1": np.ascontiguousarray(d["L1_ffn_routed_w1"].reshape(16 * D, 1408)),
        "rb1": np.ascontiguousarray(d["L1_ffn_routed_b1"].reshape(16, 11, 128).transpose(2, 0, 1).reshape(128, 176)),
        "rw2": np.ascontiguousarray(d["L1_ffn_routed_w2"].reshape(16 * 1408, D)),
        "rb2": d["L1_ffn_routed_b2"],
        "cent": d["L1_ffn_centroids"],
        "cbias": np.ascontiguousarray(d["L1_ffn_bias"].reshape(1, 16)),
    }
    im = [dict(moe_w, xnew=r3[i]["xnew"], xn2=r4[i]["xn2"], attn1=r6[i]["attn1"])
          for i in cores]
    r7 = run_bass_kernel_spmd(nc7, im, core_ids=cores).results

    out = np.concatenate([r7[i]["fin"].T for i in cores], axis=0)
    return np.ascontiguousarray(out.reshape(1, 8 * T, D), dtype=np.float32)


# version 02
# speedup vs baseline: 1.0032x; 1.0032x over previous
import numpy as np
import ml_dtypes
import concourse.bass as bass
import concourse.mybir as mybir
from concourse.bass_utils import run_bass_kernel_spmd

F32 = mybir.dt.float32
BF16 = mybir.dt.bfloat16
AF = mybir.ActivationFunctionType
ALU = mybir.AluOpType
AX = mybir.AxisListType.X
BF = ml_dtypes.bfloat16

NCORE = 8
T = 1024
D = 2048
NCH = D // 128

ENGS = ("sync", "tensor", "vector", "scalar", "gpsimd")


class Buf:
    __slots__ = ("ap", "w", "r")

    def __init__(self, ap):
        self.ap = ap
        self.w = None
        self.r = []


class Sched:
    NSLOT = 8

    def __init__(self, nc, serial=False):
        self.nc = nc
        self.serial = serial
        self.items = {e: [] for e in ENGS}
        self.val = {e: 0 for e in ENGS}
        self.duse = {e: [0] * self.NSLOT for e in ENGS}
        self.dnext = {e: 0 for e in ENGS}

    def op(self, eng, build, reads=(), writes=(), extra=(), dma=False):
        raw = []
        for b in reads:
            if b.w:
                raw.append(b.w)
        for b in writes:
            if b.w:
                raw.append(b.w)
            raw.extend(b.r)
        raw.extend(extra)
        deps = list(raw)
        if self.serial:
            deps = [(e2, v) for e2, v in self.val.items() if v > 0]
            deps += [("D", e2, s, 16 * u) for e2 in ENGS
                     for s, u in enumerate(self.duse[e2]) if u > 0]
        if dma:
            s = self.dnext[eng]
            self.dnext[eng] = (s + 1) % self.NSLOT
            u = self.duse[eng][s] + 1
            if u >= 2:
                deps.append(("D", eng, s, 16 * (u - 1)))
            self.duse[eng][s] = u
            tok = ("D", eng, s, 16 * u)
            self.items[eng].append((tuple(deps), build, ("D", eng, s)))
        else:
            self.val[eng] += 1
            tok = (eng, self.val[eng])
            self.items[eng].append((tuple(deps), build, eng))
        for b in reads:
            b.r.append(tok)
        for b in writes:
            b.w = tok
            b.r = []
        return tok

    def fence_all(self, eng="sync"):
        extra = [(e, self.val[e]) for e in ENGS if e != eng and self.val[e] > 0]
        extra += [("D", e, s, 16 * u) for e in ENGS
                  for s, u in enumerate(self.duse[e]) if u > 0]
        self.op(eng, lambda s: s.nop(), extra=extra)

    def build(self):
        nc = self.nc
        sems = {e: nc.alloc_semaphore(f"sched_{e}") for e in ENGS}
        dsems = {e: [nc.alloc_semaphore(f"dma_{e}_{s}") for s in range(self.NSLOT)]
                 for e in ENGS if any(self.duse[e])}
        with nc.Block() as block:
            for e in ENGS:
                items = self.items[e]

                def body(engobj, items=items, myname=e):
                    hi = {}
                    done = 0
                    for deps, build, inckey in items:
                        for t in deps:
                            if t[0] == "D":
                                _, e2, s2, v2 = t
                                key = ("D", e2, s2)
                                if hi.get(key, 0) < v2:
                                    engobj.wait_ge(dsems[e2][s2], v2)
                                    hi[key] = v2
                            else:
                                e2, v2 = t
                                if e2 == myname:
                                    if myname == "tensor" or v2 <= done:
                                        continue
                                    engobj.wait_ge(sems[e2], v2)
                                    done = v2
                                    continue
                                if hi.get(e2, 0) < v2:
                                    engobj.wait_ge(sems[e2], v2)
                                    hi[e2] = v2
                        ins = build(engobj)
                        if inckey[0] == "D":
                            ins.then_inc(dsems[inckey[1]][inckey[2]], 16)
                        else:
                            ins.then_inc(sems[inckey], 1)
                getattr(block, e)(body)



def fm_shard(arr2d, core):
    return np.ascontiguousarray(arr2d[core * T:(core + 1) * T, :].T)


def col128(vec):
    return np.ascontiguousarray(vec.reshape(-1, 128).T)


def fm_full(dram_ap):
    return dram_ap.rearrange("(c p) t -> p c t", p=128)


def sb3(sb_ap, tok):
    return sb_ap.rearrange("p (c t) -> p c t", t=tok)


def fm_chunk(dram_ap, c):
    return dram_ap[c * 128:(c + 1) * 128, :]



def _aw(c, n, buf):
    return [buf] if (c == 0 or c == n - 1) else []


def ln_stats(S, x_sb, xsq_scr, ps_sum, ps_sumsq, ones_bf, nch, tok, cast_scr):
    nhalf = (tok + 511) // 512
    for c in range(nch):
        cb = cast_scr[c % 2]
        sq = xsq_scr[c % 2]
        xs = x_sb.ap[:, c * tok:(c + 1) * tok]
        S.op("vector", lambda v, o=cb.ap, i=xs: v.tensor_copy(o, i),
             reads=[x_sb], writes=[cb])
        S.op("scalar", lambda s, o=sq.ap, i=xs: s.activation(o, i, AF.Square),
             reads=[x_sb], writes=[sq])
        for h in range(nhalf):
            lo, hi_ = h * 512, min((h + 1) * 512, tok)
            S.op("tensor", lambda t, o=ps_sum.ap[0:1, lo:hi_], l=ones_bf.ap,
                 r=cb.ap[:, lo:hi_], st=(c == 0), sp=(c == nch - 1):
                 t.matmul(o, l, r, start=st, stop=sp),
                 reads=[cb, ones_bf], writes=_aw(c, nch, ps_sum))
            S.op("tensor", lambda t, o=ps_sumsq.ap[0:1, lo:hi_], l=ones_bf.ap,
                 r=sq.ap[:, lo:hi_], st=(c == 0), sp=(c == nch - 1):
                 t.matmul(o, l, r, start=st, stop=sp),
                 reads=[sq, ones_bf], writes=_aw(c, nch, ps_sumsq))


def bcast_row(S, row_buf, row_slice, out_sb, onesk1, ps_bc, tok):
    nhalf = (tok + 511) // 512
    for h in range(nhalf):
        lo, hi_ = h * 512, min((h + 1) * 512, tok)
        S.op("tensor", lambda t, o=ps_bc.ap[:, 0:hi_ - lo], l=onesk1.ap,
             r=row_slice[0:1, lo:hi_]:
             t.matmul(o, l, r, start=True, stop=True),
             reads=[row_buf, onesk1], writes=[ps_bc])
        S.op("vector", lambda v, o=out_sb.ap[:, lo:hi_], i=ps_bc.ap[:, 0:hi_ - lo]:
             v.tensor_copy(o, i),
             reads=[ps_bc], writes=[out_sb])



def sb(nc, name, shape, dtype):
    return Buf(nc.alloc_sbuf_tensor(name, shape, dtype).ap())


def psb(nc, name, shape, dtype=F32):
    return Buf(nc.alloc_psum_tensor(name, shape, dtype).ap())



def alias_over(ap, *olds):
    b = Buf(ap)
    rs = []
    for o in olds:
        if o.w:
            rs.append(o.w)
        rs.extend(o.r)
    b.r = rs
    return b



def build_l0qkv(debug=False, serial=False):
    nc = bass.Bass("TRN2", target_bir_lowering=False, debug=False)
    x_d = nc.dram_tensor("x", [D, T], F32, kind="ExternalInput").ap()
    qkvw_d = nc.dram_tensor("qkvw", [D, 3 * D], F32, kind="ExternalInput").ap()
    gamma_d = nc.dram_tensor("gamma", [128, NCH], F32, kind="ExternalInput").ap()
    beta_d = nc.dram_tensor("beta", [128, NCH], F32, kind="ExternalInput").ap()
    xn_o = nc.dram_tensor("xn", [D, T], F32, kind="ExternalOutput").ap()
    q_o = nc.dram_tensor("q", [D, T], BF16, kind="ExternalOutput").ap()
    k_o = nc.dram_tensor("k", [D, T], BF16, kind="ExternalOutput").ap()
    v_o = nc.dram_tensor("v", [D, T], BF16, kind="ExternalOutput").ap()

    if debug:
        dbg_rows = nc.dram_tensor("dbg_rows", [1, 16 * T], F32, kind="ExternalOutput").ap()
        dbg_bc = nc.dram_tensor("dbg_bc", [128, 2 * T], F32, kind="ExternalOutput").ap()
    ones_col_bf = nc.inline_tensor(np.ones((128, 1), dtype=BF), name="ones_col_bf").ap()
    ones_row_f = nc.inline_tensor(np.ones((1, 128), dtype=np.float32), name="ones_row_f").ap()

    S = Sched(nc, serial=serial)
    x_sb = sb(nc, "x_sb", [128, NCH * T], F32)
    xnorm = sb(nc, "xnorm_bf", [128, NCH * T], BF16)
    scr = [sb(nc, "scr_a", [128, T], BF16), sb(nc, "scr_b", [128, T], BF16)]
    sqr = [sb(nc, "scr_c", [128, T], BF16), sb(nc, "scr_d", [128, T], BF16)]
    xnc = [sb(nc, "xn_chunk0", [128, T], F32), sb(nc, "xn_chunk1", [128, T], F32)]
    rows = sb(nc, "rows", [1, 16 * T], F32)
    bc0 = sb(nc, "bc0", [128, T], F32)
    bc1 = sb(nc, "bc1", [128, T], F32)
    onesk1 = sb(nc, "onesk1", [1, 128], F32)
    onesb = sb(nc, "onesb", [128, 1], BF16)
    gb = sb(nc, "gb_sb", [128, 2 * NCH], F32)
    wb = [sb(nc, "w0", [128, NCH * 128], BF16), sb(nc, "w1", [128, NCH * 128], BF16)]
    ev = [sb(nc, "ev0", [128, T], BF16), sb(nc, "ev1", [128, T], BF16)]
    ps_sum = psb(nc, "ps_sum", [1, T])
    ps_sumsq = psb(nc, "ps_sumsq", [1, T])
    ps_bc = psb(nc, "ps_bc", [128, 512])
    ps_mm = [psb(nc, "ps_mm0", [128, 512]), psb(nc, "ps_mm1", [128, 512])]
    if True:
        ones_col_bf_b = Buf(ones_col_bf)
        ones_row_f_b = Buf(ones_row_f)

        S.op("sync", lambda s: s.dma_start(out=sb3(x_sb.ap, T), in_=fm_full(x_d)),
             writes=[x_sb], dma=True)
        S.op("sync", lambda s: s.dma_start(out=gb.ap[:, 0:NCH], in_=gamma_d),
             writes=[gb], dma=True)
        S.op("sync", lambda s: s.dma_start(out=gb.ap[:, NCH:2 * NCH], in_=beta_d),
             writes=[gb], dma=True)
        S.op("sync", lambda s: s.dma_start(out=onesb.ap, in_=ones_col_bf),
             reads=[ones_col_bf_b], writes=[onesb], dma=True)
        S.op("sync", lambda s: s.dma_start(out=onesk1.ap, in_=ones_row_f),
             reads=[ones_row_f_b], writes=[onesk1], dma=True)

        ln_stats(S, x_sb, sqr, ps_sum, ps_sumsq, onesb, NCH, T, scr)

        RW = rows.ap
        r_mu1 = RW[0:1, 0:T]
        r_ex2 = RW[0:1, T:2 * T]
        r_var1 = RW[0:1, 2 * T:3 * T]
        r_rstd1 = RW[0:1, 3 * T:4 * T]
        r_t0 = RW[0:1, 4 * T:5 * T]
        r_mu2 = RW[0:1, 5 * T:6 * T]
        r_ex22 = RW[0:1, 6 * T:7 * T]
        r_var2 = RW[0:1, 7 * T:8 * T]
        r_rstd2 = RW[0:1, 8 * T:9 * T]
        r_c1 = RW[0:1, 9 * T:10 * T]
        r_c2 = RW[0:1, 10 * T:11 * T]
        r_m1r1 = RW[0:1, 11 * T:12 * T]

        S.op("vector", lambda v: v.tensor_scalar(r_mu1, ps_sum.ap, 1.0 / D, None, ALU.mult),
             reads=[ps_sum], writes=[rows])
        S.op("vector", lambda v: v.tensor_scalar(r_ex2, ps_sumsq.ap, 1.0 / D, None, ALU.mult),
             reads=[ps_sumsq], writes=[rows])
        S.op("vector", lambda v: v.tensor_tensor(r_t0, r_mu1, r_mu1, ALU.mult),
             reads=[rows], writes=[rows])
        S.op("vector", lambda v: v.tensor_tensor(r_var1, r_ex2, r_t0, ALU.subtract),
             reads=[rows], writes=[rows])
        S.op("vector", lambda v: v.tensor_scalar(r_var1, r_var1, 1e-5, None, ALU.add),
             reads=[rows], writes=[rows])
        S.op("scalar", lambda s: s.activation(r_t0, r_var1, AF.Sqrt),
             reads=[rows], writes=[rows])
        S.op("vector", lambda v: v.reciprocal(r_rstd1, r_t0),
             reads=[rows], writes=[rows])
        S.op("vector", lambda v: v.tensor_tensor(r_m1r1, r_mu1, r_rstd1, ALU.mult),
             reads=[rows], writes=[rows])

        bcast_row(S, rows, r_rstd1, bc0, onesk1, ps_bc, T)
        bcast_row(S, rows, r_m1r1, bc1, onesk1, ps_bc, T)

        if debug:
            S.op("sync", lambda s: s.dma_start(out=dbg_bc[:, 0:T], in_=bc0.ap),
                 reads=[bc0], dma=True)
            S.op("sync", lambda s: s.dma_start(out=dbg_bc[:, T:2 * T], in_=bc1.ap),
                 reads=[bc1], dma=True)

        for c in range(NCH):
            xb = xnc[c % 2]
            xs = x_sb.ap[:, c * T:(c + 1) * T]
            S.op("vector", lambda v, o=xb.ap, i=xs: v.tensor_tensor(o, i, bc0.ap, ALU.mult),
                 reads=[x_sb, bc0], writes=[xb])
            S.op("vector", lambda v, o=xb.ap: v.tensor_tensor(o, o, bc1.ap, ALU.subtract),
                 reads=[bc1], writes=[xb])
            S.op("sync", lambda s, o=fm_chunk(xn_o, c), i=xb.ap:
                 s.dma_start(out=o, in_=i),
                 reads=[xb], dma=True)
            cb = scr[c % 2]
            sq = sqr[c % 2]
            S.op("vector", lambda v, o=cb.ap, i=xb.ap: v.tensor_copy(o, i),
                 reads=[xb], writes=[cb])
            S.op("scalar", lambda s, o=sq.ap, i=xb.ap: s.activation(o, i, AF.Square),
                 reads=[xb], writes=[sq])
            for h in range(2):
                lo, hi_ = h * 512, (h + 1) * 512
                S.op("tensor", lambda t, o=ps_sum.ap[0:1, lo:hi_], l=onesb.ap,
                     r=cb.ap[:, lo:hi_], st=(c == 0), sp=(c == NCH - 1):
                     t.matmul(o, l, r, start=st, stop=sp),
                     reads=[cb, onesb], writes=_aw(c, NCH, ps_sum))
                S.op("tensor", lambda t, o=ps_sumsq.ap[0:1, lo:hi_], l=onesb.ap,
                     r=sq.ap[:, lo:hi_], st=(c == 0), sp=(c == NCH - 1):
                     t.matmul(o, l, r, start=st, stop=sp),
                     reads=[sq, onesb], writes=_aw(c, NCH, ps_sumsq))

        S.op("vector", lambda v: v.tensor_scalar(r_mu2, ps_sum.ap, 1.0 / D, None, ALU.mult),
             reads=[ps_sum], writes=[rows])
        S.op("vector", lambda v: v.tensor_scalar(r_ex22, ps_sumsq.ap, 1.0 / D, None, ALU.mult),
             reads=[ps_sumsq], writes=[rows])
        S.op("vector", lambda v: v.tensor_tensor(r_t0, r_mu2, r_mu2, ALU.mult),
             reads=[rows], writes=[rows])
        S.op("vector", lambda v: v.tensor_tensor(r_var2, r_ex22, r_t0, ALU.subtract),
             reads=[rows], writes=[rows])
        S.op("vector", lambda v: v.tensor_scalar(r_var2, r_var2, 1e-5, None, ALU.add),
             reads=[rows], writes=[rows])
        S.op("scalar", lambda s: s.activation(r_t0, r_var2, AF.Sqrt),
             reads=[rows], writes=[rows])
        S.op("vector", lambda v: v.reciprocal(r_rstd2, r_t0),
             reads=[rows], writes=[rows])
        S.op("vector", lambda v: v.tensor_tensor(r_c1, r_rstd1, r_rstd2, ALU.mult),
             reads=[rows], writes=[rows])
        S.op("vector", lambda v: v.tensor_tensor(r_t0, r_m1r1, r_mu2, ALU.add),
             reads=[rows], writes=[rows])
        S.op("vector", lambda v: v.tensor_tensor(r_c2, r_t0, r_rstd2, ALU.mult),
             reads=[rows], writes=[rows])

        bcast_row(S, rows, r_c1, bc0, onesk1, ps_bc, T)
        bcast_row(S, rows, r_c2, bc1, onesk1, ps_bc, T)

        for c in range(NCH):
            xb = xnc[c % 2]
            xs = x_sb.ap[:, c * T:(c + 1) * T]
            xo = xnorm.ap[:, c * T:(c + 1) * T]
            S.op("vector", lambda v, o=xb.ap, i=xs: v.tensor_tensor(o, i, bc0.ap, ALU.mult),
                 reads=[x_sb, bc0], writes=[xb])
            S.op("vector", lambda v, o=xb.ap: v.tensor_tensor(o, o, bc1.ap, ALU.subtract),
                 reads=[bc1], writes=[xb])
            S.op("vector", lambda v, o=xo, i=xb.ap, g=gb.ap[:, c:c + 1],
                 b=gb.ap[:, NCH + c:NCH + c + 1]:
                 v.tensor_scalar(o, i, g, b, ALU.mult, ALU.add),
                 reads=[xb, gb], writes=[xnorm])

        for m in range(3 * NCH):
            w = wb[m % 2]
            wsrc = qkvw_d[:, m * 128:(m + 1) * 128].rearrange("(c p) n -> p c n", p=128)
            S.op("gpsimd", lambda g, o=sb3(w.ap, 128), i=wsrc: g.dma_start(out=o, in_=i),
                 writes=[w], dma=True)
            pm = ps_mm[m % 2]
            e = ev[m % 2]
            for h in range(2):
                lo, hi_ = h * 512, (h + 1) * 512
                for c in range(NCH):
                    S.op("tensor", lambda t, o=pm.ap, l=w.ap[:, c * 128:(c + 1) * 128],
                         r=xnorm.ap[:, c * T + lo:c * T + hi_],
                         st=(c == 0), sp=(c == NCH - 1):
                         t.matmul(o, l, r, start=st, stop=sp),
                         reads=[w, xnorm], writes=_aw(c, NCH, pm))
                S.op("vector", lambda v, o=e.ap[:, lo:hi_], i=pm.ap: v.tensor_copy(o, i),
                     reads=[pm], writes=[e])
            if m < NCH:
                dst = fm_chunk(q_o, m)
            elif m < 2 * NCH:
                dst = fm_chunk(k_o, m - NCH)
            else:
                dst = fm_chunk(v_o, m - 2 * NCH)
            S.op("scalar", lambda s, o=dst, i=e.ap: s.dma_start(out=o, in_=i),
                 reads=[e], dma=True)

        if debug:
            S.op("sync", lambda s: s.dma_start(out=dbg_rows, in_=rows.ap),
                 reads=[rows], dma=True)
        S.fence_all("sync")
        S.build()
    return nc





def build_l0ffn():
    nc = bass.Bass("TRN2", target_bir_lowering=False, debug=False)
    xn_d = nc.dram_tensor("xn", [D, T], F32, kind="ExternalInput").ap()
    wfc_d = nc.dram_tensor("wfc", [4 * D, D], F32, kind="ExternalInput").ap()
    wpr_d = nc.dram_tensor("wpr", [D, 4 * D], F32, kind="ExternalInput").ap()
    ffn_o = nc.dram_tensor("ffn", [D, T], F32, kind="ExternalOutput").ap()

    sfc_d = nc.dram_tensor("sfc_bf", [4 * D, D], BF16, kind="Internal").ap()
    spr_d = nc.dram_tensor("spr_bf", [D, 4 * D], BF16, kind="Internal").ap()
    sfc_p = [Buf(sfc_d[i * 128:(i + 1) * 128, :]) for i in range(64)]
    spr_p = [Buf(spr_d[i * 128:(i + 1) * 128, :]) for i in range(16)]

    ones_row_f = nc.inline_tensor(np.ones((1, 128), dtype=np.float32), name="ones_row_f").ap()
    ones_col_bf = nc.inline_tensor(np.ones((128, 1), dtype=BF), name="ones_col_bf").ap()
    ident_f = nc.inline_tensor(np.eye(128, dtype=np.float32), name="ident_f").ap()

    S = Sched(nc)
    xn_sb = sb(nc, "xn_sb", [128, NCH * T], F32)
    xnc = [Buf(xn_sb.ap[:, c * T:(c + 1) * T]) for c in range(NCH)]
    ghc = [Buf(xn_sb.ap.bitcast(BF16)[:, k * 512:(k + 1) * 512]) for k in range(64)]
    xhat = sb(nc, "xhat", [128, NCH * T], BF16)
    xhc = [Buf(xhat.ap[:, c * T:(c + 1) * T]) for c in range(NCH)]
    wT = [sb(nc, "wT0", [128, NCH * 512], BF16), sb(nc, "wT1", [128, NCH * 512], BF16)]
    wT2 = [sb(nc, "wT2a", [128, T], BF16), sb(nc, "wT2b", [128, T], BF16)]
    PW = 1024
    wf32 = [sb(nc, "wf32a", [128, PW], F32), sb(nc, "wf32b", [128, PW], F32)]
    wc32_t = sb(nc, "wc32a", [128, PW], F32)
    wc32 = [wc32_t, wc32_t]
    wsc = [sb(nc, "wsca", [128, PW], BF16), sb(nc, "wscb", [128, PW], BF16)]
    bc_mu = sb(nc, "bc_mu", [128, T], F32)
    bc_c1 = sb(nc, "bc_c1", [128, T], F32)
    bc_c2 = sb(nc, "bc_c2", [128, T], F32)
    tmax = sb(nc, "tmax", [128, T], F32)
    tmin = sb(nc, "tmin", [128, T], F32)
    scrv = sb(nc, "scrv", [128, T], F32)
    sqs = [sb(nc, "sqs0", [128, T], BF16), sb(nc, "sqs1", [128, T], BF16)]
    rows = sb(nc, "rows", [1, 7 * T + 64], F32)
    amax_tm = sb(nc, "amax_tm", [128, 2], F32)
    cols = sb(nc, "cols", [128, 8], F32)
    onesk1 = sb(nc, "onesk1", [1, 128], F32)
    onesb = sb(nc, "onesb", [128, 1], BF16)
    idf = sb(nc, "idf", [128, 128], F32)
    ev_t = sb(nc, "ev0", [128, 512], F32)
    ev = [ev_t, ev_t]

    pool = nc.alloc_psum_tensor("pool", [128, 4096], F32)
    pap = pool.ap()
    bank_owner = {}

    def bank(i, j=None):
        j = i + 1 if j is None else j
        return pap[:, i * 512:j * 512]

    def take(lo, hi, view):
        olds = [bank_owner[i] for i in range(lo, hi) if i in bank_owner]
        b = alias_over(view, *olds)
        for i in range(lo, hi):
            bank_owner[i] = b
        return b

    S.op("sync", lambda s: s.dma_start(out=onesk1.ap, in_=ones_row_f), writes=[onesk1], dma=True)
    S.op("sync", lambda s: s.dma_start(out=onesb.ap, in_=ones_col_bf), writes=[onesb], dma=True)
    S.op("sync", lambda s: s.dma_start(out=idf.ap, in_=ident_f), writes=[idf], dma=True)
    for c in range(NCH):
        S.op("sync", lambda s, c=c, o=xnc[c].ap, i=fm_chunk(xn_d, c): s.dma_start(out=o, in_=i),
             writes=[xnc[c]], dma=True)

    RW = rows.ap
    r_mu = RW[0:1, 0:T]
    r_var = RW[0:1, T:2 * T]
    r_t0 = RW[0:1, 2 * T:3 * T]
    r_rstd = RW[0:1, 3 * T:4 * T]
    r_amax = RW[0:1, 4 * T:5 * T]
    r_c1 = RW[0:1, 5 * T:6 * T]
    r_c2 = RW[0:1, 6 * T:7 * T]
    r_g1 = RW[0:1, 7 * T:7 * T + 16]
    r_g2 = RW[0:1, 7 * T + 16:7 * T + 32]

    def gamma_w(w_d, nrows, ncols, r_g, cidx, gbank):
        ps_g = take(gbank, gbank + 1, bank(gbank))
        plist = [(i, cb) for i in range(nrows // 128) for cb in range(ncols // PW)]
        first = [True]
        for pi, (i, cb) in enumerate(plist):
            wf = wf32[pi % 2]
            ws = wsc[pi % 2]
            S.op("sync", lambda s, wf=wf, i2=w_d[i * 128:(i + 1) * 128, cb * PW:(cb + 1) * PW]:
                 s.dma_start(out=wf.ap, in_=i2),
                 writes=[wf], dma=True)
            S.op("scalar", lambda s, wf=wf, ws=ws: s.activation(ws.ap, wf.ap, AF.Abs),
                 reads=[wf], writes=[ws])
            for h in range(PW // 512):
                lo, hi_ = h * 512, (h + 1) * 512
                st = first[0]
                first[0] = False
                sp = (pi == len(plist) - 1) and (h == PW // 512 - 1)
                S.op("tensor", lambda t, ws=ws, o=ps_g.ap[0:1, 0:512], l=onesb.ap,
                     r=ws.ap[:, lo:hi_], st=st, sp=sp:
                     t.matmul(o, l, r, start=st, stop=sp, skip_group_check=True),
                     reads=[ws, onesb],
                     writes=[ps_g] if (st or sp) else [])
        S.op("vector", lambda v: v.tensor_reduce(r_g[0:1, 0:1], ps_g.ap[0:1, 0:512], AX, ALU.add),
             reads=[ps_g], writes=[rows])
        S.op("vector", lambda v, n=float(nrows * ncols): v.tensor_scalar(
             r_g[0:1, 0:1], r_g[0:1, 0:1], 1.0 / n, None, ALU.mult),
             reads=[rows], writes=[rows])
        S.op("vector", lambda v: v.tensor_scalar(r_g[0:1, 1:2], r_g[0:1, 0:1],
                                                 1e-6, 0.5, ALU.add, ALU.mult),
             reads=[rows], writes=[rows])
        S.op("vector", lambda v: v.tensor_scalar(r_g[0:1, 2:3], r_g[0:1, 1:2],
                                                 -1.0, None, ALU.mult),
             reads=[rows], writes=[rows])
        ps_c = take(gbank, gbank + 1, bank(gbank))
        S.op("tensor", lambda t, o=ps_c.ap[:, 0:3], l=onesk1.ap, r=r_g[0:1, 0:3]:
             t.matmul(o, l, r, start=True, stop=True),
             reads=[rows, onesk1], writes=[ps_c])
        S.op("vector", lambda v, o=cols.ap[:, cidx:cidx + 3], i=ps_c.ap[:, 0:3]:
             v.tensor_copy(o, i),
             reads=[ps_c], writes=[cols])

    def ternarize_w(w_d, s_pbufs, nrows, ncols, cidx):
        plist = [(i, cb) for i in range(nrows // 128) for cb in range(ncols // PW)]
        for pi, (i, cb) in enumerate(plist):
            wf = wf32[pi % 2]
            wc = wc32[pi % 2]
            ws = wsc[pi % 2]
            S.op("sync", lambda s, wf=wf, i2=w_d[i * 128:(i + 1) * 128, cb * PW:(cb + 1) * PW]:
                 s.dma_start(out=wf.ap, in_=i2),
                 writes=[wf], dma=True)
            S.op("vector", lambda v, wf=wf, wc=wc: v.tensor_scalar(
                 wc.ap, wf.ap, cols.ap[:, cidx + 1:cidx + 2],
                 cols.ap[:, cidx + 2:cidx + 3], ALU.min, ALU.max),
                 reads=[wf, cols], writes=[wc])
            S.op("gpsimd", lambda g, wf=wf, wc=wc: g.tensor_tensor(wc.ap, wf.ap, wc.ap, ALU.subtract),
                 reads=[wf], writes=[wc])
            S.op("scalar", lambda s, wc=wc, ws=ws: s.activation(ws.ap, wc.ap, AF.Sign),
                 reads=[wc], writes=[ws])
            pb = s_pbufs[i]
            S.op("scalar", lambda s, ws=ws, o=pb.ap[:, cb * PW:(cb + 1) * PW]:
                 s.dma_start(out=o, in_=ws.ap),
                 reads=[ws], writes=[pb], dma=True)

    gamma_w(wfc_d, 4 * D, D, r_g1, 0, 6)
    ternarize_w(wfc_d, sfc_p, 4 * D, D, 0)
    gamma_w(wpr_d, D, 4 * D, r_g2, 3, 7)
    ternarize_w(wpr_d, spr_p, D, 4 * D, 3)

    ps_sum = take(0, 2, pap[0:1, 0:1024])
    ps_sumsq = take(2, 4, pap[0:1, 1024:2048])
    for c in range(NCH):
        cb = sqs[c % 2]
        S.op("vector", lambda v, c=c, o=cb.ap: v.tensor_copy(o, xnc[c].ap),
             reads=[xnc[c]], writes=[cb])
        for h in range(2):
            lo, hi_ = h * 512, (h + 1) * 512
            S.op("tensor", lambda t, cb=cb, o=ps_sum.ap[0:1, lo:hi_], l=onesb.ap,
                 r=cb.ap[:, lo:hi_], st=(c == 0), sp=(c == NCH - 1):
                 t.matmul(o, l, r, start=st, stop=sp),
                 reads=[cb, onesb], writes=_aw(c, NCH, ps_sum))
        if c == 0:
            S.op("vector", lambda v: v.tensor_copy(tmax.ap, xnc[0].ap),
                 reads=[xnc[0]], writes=[tmax])
            S.op("vector", lambda v: v.tensor_copy(tmin.ap, xnc[0].ap),
                 reads=[xnc[0]], writes=[tmin])
        else:
            S.op("vector", lambda v, c=c: v.tensor_tensor(tmax.ap, tmax.ap, xnc[c].ap, ALU.max),
                 reads=[xnc[c]], writes=[tmax])
            S.op("vector", lambda v, c=c: v.tensor_tensor(tmin.ap, tmin.ap, xnc[c].ap, ALU.min),
                 reads=[xnc[c]], writes=[tmin])
    for c in range(NCH):
        sq = sqs[c % 2]
        S.op("scalar", lambda s, c=c, o=sq.ap: s.activation(o, xnc[c].ap, AF.Square),
             reads=[xnc[c]], writes=[sq])
        for h in range(2):
            lo, hi_ = h * 512, (h + 1) * 512
            S.op("tensor", lambda t, sq=sq, o=ps_sumsq.ap[0:1, lo:hi_], l=onesb.ap,
                 r=sq.ap[:, lo:hi_], st=(c == 0), sp=(c == NCH - 1):
                 t.matmul(o, l, r, start=st, stop=sp),
                 reads=[sq, onesb], writes=_aw(c, NCH, ps_sumsq))

    def stat_rows(dD_, eps, width, pss, psq):
        mu, var, t0, rstd = (r_mu[0:1, 0:width], r_var[0:1, 0:width],
                             r_t0[0:1, 0:width], r_rstd[0:1, 0:width])
        S.op("vector", lambda v, pss=pss: v.tensor_scalar(mu, pss.ap, 1.0 / dD_, None, ALU.mult),
             reads=[pss], writes=[rows])
        S.op("vector", lambda v, psq=psq: v.tensor_scalar(t0, psq.ap, 1.0 / dD_, None, ALU.mult),
             reads=[psq], writes=[rows])
        S.op("vector", lambda v: v.tensor_tensor(var, mu, mu, ALU.mult),
             reads=[rows], writes=[rows])
        S.op("vector", lambda v: v.tensor_tensor(var, t0, var, ALU.subtract),
             reads=[rows], writes=[rows])
        S.op("vector", lambda v, e=eps: v.tensor_scalar(var, var, e, None, ALU.add),
             reads=[rows], writes=[rows])
        S.op("scalar", lambda s: s.activation(t0, var, AF.Sqrt),
             reads=[rows], writes=[rows])
        S.op("vector", lambda v: v.reciprocal(rstd, t0),
             reads=[rows], writes=[rows])

    def amax_rows(tmaxap, tminap, width):
        ps_t = take(4, 5, bank(4))
        ps_r = take(5, 6, bank(5))
        for tt in range(width // 128):
            S.op("tensor", lambda t, tt=tt, i=tmaxap[:, tt * 128:(tt + 1) * 128],
                 o=ps_t.ap[:, 0:128]:
                 t.transpose(o, i, idf.ap),
                 reads=[tmax, idf], writes=[ps_t])
            S.op("tensor", lambda t, tt=tt, i=tminap[:, tt * 128:(tt + 1) * 128],
                 o=ps_t.ap[:, 128:256]:
                 t.transpose(o, i, idf.ap),
                 reads=[tmin, idf], writes=[ps_t])
            S.op("vector", lambda v: v.tensor_reduce(amax_tm.ap[:, 0:1], ps_t.ap[:, 0:128], AX, ALU.max),
                 reads=[ps_t], writes=[amax_tm])
            S.op("vector", lambda v: v.tensor_reduce(amax_tm.ap[:, 1:2], ps_t.ap[:, 128:256], AX, ALU.min),
                 reads=[ps_t], writes=[amax_tm])
            S.op("tensor", lambda t, o=ps_r.ap[0:1, 0:128], i=amax_tm.ap[:, 0:1]:
                 t.transpose(o, i, idf.ap),
                 reads=[amax_tm, idf], writes=[ps_r])
            S.op("tensor", lambda t, o=ps_r.ap[0:1, 128:256], i=amax_tm.ap[:, 1:2]:
                 t.transpose(o, i, idf.ap),
                 reads=[amax_tm, idf], writes=[ps_r])
            S.op("vector", lambda v, tt=tt: v.tensor_tensor(
                 r_t0[0:1, tt * 128:(tt + 1) * 128], ps_r.ap[0:1, 0:128],
                 r_mu[0:1, tt * 128:(tt + 1) * 128], ALU.subtract),
                 reads=[ps_r, rows], writes=[rows])
            S.op("vector", lambda v, tt=tt: v.tensor_tensor(
                 r_amax[0:1, tt * 128:(tt + 1) * 128],
                 r_mu[0:1, tt * 128:(tt + 1) * 128], ps_r.ap[0:1, 128:256], ALU.subtract),
                 reads=[ps_r, rows], writes=[rows])
            S.op("vector", lambda v, tt=tt: v.tensor_tensor(
                 r_amax[0:1, tt * 128:(tt + 1) * 128],
                 r_amax[0:1, tt * 128:(tt + 1) * 128],
                 r_t0[0:1, tt * 128:(tt + 1) * 128], ALU.max),
                 reads=[rows], writes=[rows])

    def c_rows(width=T):
        c1, c2, am, rstd = (r_c1[0:1, 0:width], r_c2[0:1, 0:width],
                            r_amax[0:1, 0:width], r_rstd[0:1, 0:width])
        S.op("vector", lambda v: v.reciprocal(c1, am), reads=[rows], writes=[rows])
        S.op("vector", lambda v: v.tensor_scalar(c1, c1, 128.0, None, ALU.mult),
             reads=[rows], writes=[rows])
        S.op("vector", lambda v: v.tensor_tensor(c2, rstd, am, ALU.mult),
             reads=[rows], writes=[rows])
        S.op("vector", lambda v: v.tensor_scalar(c2, c2, 1.0 / 128.0, None, ALU.mult),
             reads=[rows], writes=[rows])

    stat_rows(float(D), 1e-6, T, ps_sum, ps_sumsq)
    amax_rows(tmax.ap, tmin.ap, T)
    c_rows()
    ps_bc = take(4, 5, bank(4))
    bcast_row(S, rows, r_mu, bc_mu, onesk1, ps_bc, T)
    bcast_row(S, rows, r_c1, bc_c1, onesk1, ps_bc, T)
    bcast_row(S, rows, r_c2, bc_c2, onesk1, ps_bc, T)

    for c in range(NCH):
        S.op("vector", lambda v, c=c: v.tensor_tensor(scrv.ap, xnc[c].ap, bc_mu.ap, ALU.subtract),
             reads=[xnc[c], bc_mu], writes=[scrv])
        S.op("vector", lambda v: v.tensor_tensor(scrv.ap, scrv.ap, bc_c1.ap, ALU.mult),
             reads=[bc_c1], writes=[scrv])
        S.op("vector", lambda v: v.tensor_scalar(scrv.ap, scrv.ap, 128.0, -128.0, ALU.min, ALU.max),
             reads=[], writes=[scrv])
        S.op("vector", lambda v, c=c: v.tensor_tensor(xhc[c].ap, scrv.ap, bc_c2.ap, ALU.mult),
             reads=[scrv, bc_c2], writes=[xhc[c]])

    for half in range(2):
        toff = half * 512

        ps_h = [take(i, i + 1, bank(i)) for i in range(4)]
        ps_sum = take(4, 5, pap[0:1, 4 * 512:5 * 512])
        ps_sumsq = take(5, 6, pap[0:1, 5 * 512:6 * 512])
        for blk in range(16):
            w = wT[blk % 2]
            for dc in range(NCH):
                q = "sync" if dc % 2 == 0 else "scalar"
                S.op(q, lambda s, blk=blk, dc=dc, w=w, o=w.ap[:, dc * 512:(dc + 1) * 512],
                     i=sfc_d[blk * 512:(blk + 1) * 512, dc * 128:(dc + 1) * 128]:
                     s.dma_start_transpose(o, i),
                     reads=sfc_p[blk * 4:blk * 4 + 4], writes=[w], dma=True)
            for m in range(4):
                pm = ps_h[m]
                gci = ghc[blk * 4 + m]
                for dc in range(NCH):
                    S.op("tensor", lambda t, w=w, pm=pm, m=m, dc=dc,
                         l=w.ap[:, dc * 512 + m * 128:dc * 512 + m * 128 + 128],
                         r=xhc[dc].ap[:, toff:toff + 512],
                         st=(dc == 0), sp=(dc == NCH - 1):
                         t.matmul(pm.ap, l, r, start=st, stop=sp),
                         reads=[w, xhc[dc]], writes=_aw(dc, NCH, pm))
                S.op("scalar", lambda s, pm=pm, gci=gci:
                     s.activation(gci.ap, pm.ap, AF.Gelu, scale=cols.ap[:, 0:1]),
                     reads=[pm, cols], writes=[gci])
                k = blk * 4 + m
                S.op("tensor", lambda t, gci=gci, o=ps_sum.ap[0:1, 0:512], l=onesb.ap,
                     st=(k == 0), sp=(k == 63):
                     t.matmul(o, l, gci.ap, start=st, stop=sp),
                     reads=[gci, onesb], writes=_aw(k, 64, ps_sum))
                sq = sqs[k % 2]
                S.op("scalar", lambda s, gci=gci, sq=sq: s.activation(sq.ap[:, 0:512], gci.ap, AF.Square),
                     reads=[gci], writes=[sq])
                S.op("tensor", lambda t, sq=sq, o=ps_sumsq.ap[0:1, 0:512], l=onesb.ap,
                     st=(k == 0), sp=(k == 63):
                     t.matmul(o, l, sq.ap[:, 0:512], start=st, stop=sp),
                     reads=[sq, onesb], writes=_aw(k, 64, ps_sumsq))
                if k == 0:
                    S.op("vector", lambda v, gci=gci: v.tensor_copy(tmax.ap[:, 0:512], gci.ap),
                         reads=[gci], writes=[tmax])
                    S.op("vector", lambda v, gci=gci: v.tensor_copy(tmin.ap[:, 0:512], gci.ap),
                         reads=[gci], writes=[tmin])
                else:
                    S.op("vector", lambda v, gci=gci: v.tensor_tensor(
                         tmax.ap[:, 0:512], tmax.ap[:, 0:512], gci.ap, ALU.max),
                         reads=[gci], writes=[tmax])
                    S.op("vector", lambda v, gci=gci: v.tensor_tensor(
                         tmin.ap[:, 0:512], tmin.ap[:, 0:512], gci.ap, ALU.min),
                         reads=[gci], writes=[tmin])

        stat_rows(float(4 * D), 1e-6, 512, ps_sum, ps_sumsq)
        amax_rows(tmax.ap[:, 0:512], tmin.ap[:, 0:512], 512)
        c_rows(512)
        ps_bc = take(4, 5, bank(4))
        bcast_row(S, rows, r_mu[0:1, 0:512], bc_mu, onesk1, ps_bc, 512)
        bcast_row(S, rows, r_c1[0:1, 0:512], bc_c1, onesk1, ps_bc, 512)
        bcast_row(S, rows, r_c2[0:1, 0:512], bc_c2, onesk1, ps_bc, 512)

        for k in range(64):
            gci = ghc[k]
            S.op("vector", lambda v, gci=gci: v.tensor_tensor(
                 scrv.ap[:, 0:512], gci.ap, bc_mu.ap[:, 0:512], ALU.subtract),
                 reads=[gci, bc_mu], writes=[scrv])
            S.op("vector", lambda v: v.tensor_tensor(
                 scrv.ap[:, 0:512], scrv.ap[:, 0:512], bc_c1.ap[:, 0:512], ALU.mult),
                 reads=[bc_c1], writes=[scrv])
            S.op("vector", lambda v: v.tensor_scalar(
                 scrv.ap[:, 0:512], scrv.ap[:, 0:512], 128.0, -128.0, ALU.min, ALU.max),
                 reads=[], writes=[scrv])
            S.op("vector", lambda v, gci=gci: v.tensor_tensor(
                 gci.ap, scrv.ap[:, 0:512], bc_c2.ap[:, 0:512], ALU.mult),
                 reads=[scrv, bc_c2], writes=[gci])

        for grp in range(2):
            ps8 = [take(i, i + 1, bank(i)) for i in range(8)]
            for dc in range(64):
                w2 = wT2[dc % 2]
                q = "sync" if dc % 2 == 0 else "scalar"
                S.op(q, lambda s, w2=w2, grp=grp, dc=dc, o=w2.ap,
                     i=spr_d[grp * 1024:(grp + 1) * 1024, dc * 128:(dc + 1) * 128]:
                     s.dma_start_transpose(o, i),
                     reads=spr_p[grp * 8:grp * 8 + 8], writes=[w2], dma=True)
                gci = ghc[dc]
                for m in range(8):
                    S.op("tensor", lambda t, w2=w2, m=m, gci=gci, ps=ps8[m],
                         l=w2.ap[:, m * 128:(m + 1) * 128],
                         st=(dc == 0), sp=(dc == 63):
                         t.matmul(ps.ap, l, gci.ap, start=st, stop=sp),
                         reads=[w2, gci], writes=_aw(dc, 64, ps8[m]))
            for m in range(8):
                e = ev[m % 2]
                S.op("vector", lambda v, e=e, ps=ps8[m]:
                     v.tensor_scalar(e.ap, ps.ap, cols.ap[:, 3:4], None, ALU.mult),
                     reads=[ps8[m], cols], writes=[e])
                oc = grp * 8 + m
                S.op("scalar", lambda s, e=e, o=fm_chunk(ffn_o, oc)[:, toff:toff + 512]:
                     s.dma_start(out=o, in_=e.ap),
                     reads=[e], dma=True)

    S.fence_all("sync")
    S.build()
    return nc



SCALE0 = 1.0 / float(np.sqrt(2048.0))


def build_l0attn():
    nc = bass.Bass("TRN2", target_bir_lowering=False, debug=False)
    q_d = nc.dram_tensor("q", [D, T], BF16, kind="ExternalInput").ap()
    kf_d = nc.dram_tensor("kf", [D, 8 * T], BF16, kind="ExternalInput").ap()
    vf_d = nc.dram_tensor("vf", [8 * T, D], BF16, kind="ExternalInput").ap()
    x_d = nc.dram_tensor("x", [D, T], F32, kind="ExternalInput").ap()
    xn_d = nc.dram_tensor("xn", [D, T], F32, kind="ExternalInput").ap()
    ffn_d = nc.dram_tensor("ffn", [D, T], F32, kind="ExternalInput").ap()
    wo_d = nc.dram_tensor("wo", [D, D], F32, kind="ExternalInput").ap()
    qidx_d = nc.dram_tensor("qidx", [128, 8], F32, kind="ExternalInput").ap()
    xnew_o = nc.dram_tensor("xnew", [D, T], F32, kind="ExternalOutput").ap()

    iota512 = nc.inline_tensor(np.arange(512, dtype=np.float32).reshape(1, 512),
                               name="iota512").ap()
    ones_row_f = nc.inline_tensor(np.ones((1, 128), dtype=np.float32), name="ones_row_f").ap()
    ident_b = nc.inline_tensor(np.eye(128, dtype=BF), name="ident_b").ap()

    S = Sched(nc)
    q_sb = sb(nc, "q_sb", [128, NCH * T], BF16)
    yac = [sb(nc, f"yac{i}", [128, D], F32) for i in range(4)]
    kblk = [sb(nc, "kblk0", [128, NCH * 512], BF16), sb(nc, "kblk1", [128, NCH * 512], BF16)]
    vblk = [sb(nc, "vblk0", [128, 4 * D], BF16), sb(nc, "vblk1", [128, 4 * D], BF16)]
    yT = sb(nc, "yT", [128, NCH * 512], BF16)
    ssb = [sb(nc, "ssb0", [128, 512], F32), sb(nc, "ssb1", [128, 512], F32)]
    pen = [sb(nc, "pen0", [128, 512], F32), sb(nc, "pen1", [128, 512], F32)]
    pT = [sb(nc, "pT0", [128, 512], BF16), sb(nc, "pT1", [128, 512], BF16)]
    pbf = [sb(nc, "pbf0", [128, 512], BF16), sb(nc, "pbf1", [128, 512], BF16)]
    wob = [sb(nc, "wob0", [128, NCH * 128], BF16), sb(nc, "wob1", [128, NCH * 128], BF16)]
    sumb = [sb(nc, "sumb0", [128, 3 * 512], F32), sb(nc, "sumb1", [128, 3 * 512], F32)]
    evx = [sb(nc, "evx0", [128, 512], F32), sb(nc, "evx1", [128, 512], F32)]
    mrow = sb(nc, "mrow", [128, 16], F32)
    qidx_sb = sb(nc, "qidx_sb", [128, 8], F32)
    io_sb = sb(nc, "io_sb", [1, 512], F32)
    jrow = sb(nc, "jrow", [128, 512], F32)
    onesk1 = sb(nc, "onesk1", [1, 128], F32)
    idb = sb(nc, "idb", [128, 128], BF16)

    pool = nc.alloc_psum_tensor("pool", [128, 3584], F32)
    poolb = nc.alloc_psum_tensor("poolb", [128, 1024], BF16)
    pap = pool.ap()
    bap = poolb.ap()
    bank_owner = {}

    def take(lo, hi, view):
        olds = [bank_owner[i] for i in range(lo, hi) if i in bank_owner]
        b = alias_over(view, *olds)
        for i in range(lo, hi):
            bank_owner[i] = b
        return b

    S.op("sync", lambda s: s.dma_start(out=sb3(q_sb.ap, T), in_=fm_full(q_d)),
         writes=[q_sb], dma=True)
    S.op("sync", lambda s: s.dma_start(out=qidx_sb.ap, in_=qidx_d),
         writes=[qidx_sb], dma=True)
    S.op("sync", lambda s: s.dma_start(out=onesk1.ap, in_=ones_row_f),
         writes=[onesk1], dma=True)
    S.op("sync", lambda s: s.dma_start(out=idb.ap, in_=ident_b),
         writes=[idb], dma=True)
    S.op("sync", lambda s: s.dma_start(out=io_sb.ap, in_=iota512),
         writes=[io_sb], dma=True)
    for i in range(4):
        S.op("vector", lambda v, i2=yac[i].ap: v.memset(i2, 0.0), writes=[yac[i]])
    ps_j = take(2, 3, pap[:, 2 * 512:3 * 512])
    S.op("tensor", lambda t: t.matmul(ps_j.ap, onesk1.ap, io_sb.ap, start=True, stop=True),
         reads=[onesk1, io_sb], writes=[ps_j])
    S.op("vector", lambda v: v.tensor_copy(jrow.ap, ps_j.ap),
         reads=[ps_j], writes=[jrow])

    MR = mrow.ap

    def m_ap(qt):
        return MR[:, qt:qt + 1]

    def l_ap(qt):
        return MR[:, 4 + qt:5 + qt]
    negm = MR[:, 8:9]
    alpha = MR[:, 9:10]
    t0c = MR[:, 10:11]
    rl = MR[:, 11:12]
    bm = MR[:, 12:13]

    for g in range(2):
        nkb = 15 + g
        for qt in range(4):
            S.op("vector", lambda v, a=m_ap(qt): v.memset(a, -3e38), writes=[mrow])
            S.op("vector", lambda v, a=l_ap(qt): v.memset(a, 0.0), writes=[mrow])
        if g == 1:
            for i in range(4):
                S.op("vector", lambda v, i2=yac[i].ap: v.memset(i2, 0.0), writes=[yac[i]])
        for kb in range(nkb):
            kbu = kblk[kb % 2]
            vbu = vblk[kb % 2]
            S.op("sync", lambda s, o=sb3(kbu.ap, 512), i=fm_full(kf_d[:, kb * 512:(kb + 1) * 512]):
                 s.dma_start(out=o, in_=i),
                 writes=[kbu], dma=True)
            S.op("sync", lambda s, o=vbu.ap.rearrange("p (w d) -> p w d", d=D),
                 i=vf_d[kb * 512:(kb + 1) * 512, :].rearrange("(w p) d -> p w d", p=128):
                 s.dma_start(out=o, in_=i),
                 writes=[vbu], dma=True)
            for qt in range(4):
                ps_s = take(0 + qt % 2, 1 + qt % 2, pap[:, (qt % 2) * 512:(qt % 2 + 1) * 512])
                for c in range(NCH):
                    S.op("tensor", lambda t, o=ps_s.ap,
                         l=q_sb.ap[:, c * T + (g * 4 + qt) * 128:c * T + (g * 4 + qt) * 128 + 128],
                         r=kbu.ap[:, c * 512:(c + 1) * 512],
                         st=(c == 0), sp=(c == NCH - 1):
                         t.matmul(o, l, r, start=st, stop=sp),
                         reads=[q_sb, kbu], writes=_aw(c, NCH, ps_s))
                pe = pen[qt % 2]
                S.op("vector", lambda v, o=pe.ap, qc=qidx_sb.ap[:, g * 4 + qt:g * 4 + qt + 1],
                     kb512=float(kb * 512):
                     v.tensor_scalar(o, jrow.ap, qc, kb512, ALU.subtract, ALU.add),
                     reads=[jrow, qidx_sb], writes=[pe])
                S.op("vector", lambda v, o=pe.ap: v.tensor_scalar(o, o, 0.0, -60000.0, ALU.max, ALU.mult),
                     reads=[], writes=[pe])
                sb_s = ssb[qt % 2]
                S.op("vector", lambda v, o=sb_s.ap, i=ps_s.ap, p=pe.ap:
                     v.scalar_tensor_tensor(o, i, SCALE0, p, ALU.mult, ALU.add),
                     reads=[ps_s, pe], writes=[sb_s])
                S.op("vector", lambda v, i=sb_s.ap: v.tensor_reduce(bm, i, AX, ALU.max),
                     reads=[sb_s], writes=[mrow])
                S.op("vector", lambda v, m=m_ap(qt): v.tensor_tensor(t0c, m, bm, ALU.max),
                     reads=[mrow], writes=[mrow])
                S.op("vector", lambda v, m=m_ap(qt): v.tensor_tensor(alpha, m, t0c, ALU.subtract),
                     reads=[mrow], writes=[mrow])
                S.op("scalar", lambda s: s.activation(alpha, alpha, AF.Exp),
                     reads=[mrow], writes=[mrow])
                S.op("vector", lambda v, m=m_ap(qt): v.tensor_copy(m, t0c),
                     reads=[mrow], writes=[mrow])
                S.op("vector", lambda v: v.tensor_scalar(negm, t0c, -1.0, None, ALU.mult),
                     reads=[mrow], writes=[mrow])
                pb = pbf[qt % 2]
                S.op("scalar", lambda s, o=pb.ap, i=sb_s.ap: s.activation(
                     o, i, AF.Exp, bias=negm, accum_out=bm),
                     reads=[sb_s, mrow], writes=[pb, mrow])
                S.op("vector", lambda v, la=l_ap(qt): v.scalar_tensor_tensor(
                     la, la, alpha, bm, ALU.mult, ALU.add),
                     reads=[mrow], writes=[mrow])
                ps_tr = take(7, 8, bap[:, 0:512])
                pt = pT[qt % 2]
                for w in range(4):
                    S.op("tensor", lambda t, o=ps_tr.ap[:, w * 128:(w + 1) * 128],
                         i=pb.ap[:, w * 128:(w + 1) * 128]:
                         t.transpose(o, i, idb.ap),
                         reads=[pb, idb], writes=[ps_tr] if w in (0, 3) else [])
                S.op("vector", lambda v, o=pt.ap, i=ps_tr.ap: v.tensor_copy(o, i),
                     reads=[ps_tr], writes=[pt])
                for db in range(4):
                    ps_y = take(3 + db % 2, 4 + db % 2, pap[:, (3 + db % 2) * 512:(4 + db % 2) * 512])
                    for kc in range(4):
                        S.op("tensor", lambda t, o=ps_y.ap,
                             l=pt.ap[:, kc * 128:(kc + 1) * 128],
                             r=vbu.ap[:, kc * D + db * 512:kc * D + db * 512 + 512],
                             st=(kc == 0), sp=(kc == 3):
                             t.matmul(o, l, r, start=st, stop=sp),
                             reads=[pt, vbu], writes=_aw(kc, 4, ps_y))
                    S.op("vector", lambda v, ya=yac[qt].ap[:, db * 512:(db + 1) * 512],
                         i=ps_y.ap:
                         v.scalar_tensor_tensor(ya, ya, alpha, i, ALU.mult, ALU.add),
                         reads=[ps_y, mrow], writes=[yac[qt]])
        for qt in range(4):
            S.op("vector", lambda v, la=l_ap(qt): v.reciprocal(rl, la),
                 reads=[mrow], writes=[mrow])
            S.op("vector", lambda v, ya=yac[qt].ap: v.tensor_scalar(ya, ya, rl, None, ALU.mult),
                 reads=[mrow], writes=[yac[qt]])
            ps_tr = take(7, 8, bap[:, 0:512])
            for c in range(NCH):
                S.op("scalar", lambda s, o=pbf[0].ap[:, 0:128], i=yac[qt].ap[:, c * 128:(c + 1) * 128]:
                     s.copy(o, i),
                     reads=[yac[qt]], writes=[pbf[0]])
                S.op("tensor", lambda t, o=ps_tr.ap[:, 0:128], i=pbf[0].ap[:, 0:128]:
                     t.transpose(o, i, idb.ap),
                     reads=[pbf[0], idb], writes=[ps_tr])
                S.op("vector", lambda v, o=yT.ap[:, c * 512 + qt * 128:c * 512 + qt * 128 + 128],
                     i=ps_tr.ap[:, 0:128]:
                     v.tensor_copy(o, i),
                     reads=[ps_tr], writes=[yT])
        for m in range(NCH):
            w = wob[m % 2]
            wsrc = wo_d[:, m * 128:(m + 1) * 128].rearrange("(c p) n -> p c n", p=128)
            S.op("gpsimd", lambda g2, o=sb3(w.ap, 128), i=wsrc: g2.dma_start(out=o, in_=i),
                 writes=[w], dma=True)
            ps_o = take(5, 6, pap[:, 5 * 512:6 * 512])
            for c in range(NCH):
                S.op("tensor", lambda t, o=ps_o.ap, l=w.ap[:, c * 128:(c + 1) * 128],
                     r=yT.ap[:, c * 512:(c + 1) * 512],
                     st=(c == 0), sp=(c == NCH - 1):
                     t.matmul(o, l, r, start=st, stop=sp),
                     reads=[w, yT], writes=_aw(c, NCH, ps_o))
            sm = sumb[m % 2]
            off = g * 512
            S.op("sync", lambda s, o=sm.ap[:, 0:512], i=fm_chunk(x_d, m)[:, off:off + 512]:
                 s.dma_start(out=o, in_=i), writes=[sm], dma=True)
            S.op("sync", lambda s, o=sm.ap[:, 512:1024], i=fm_chunk(xn_d, m)[:, off:off + 512]:
                 s.dma_start(out=o, in_=i), writes=[sm], dma=True)
            S.op("sync", lambda s, o=sm.ap[:, 1024:1536], i=fm_chunk(ffn_d, m)[:, off:off + 512]:
                 s.dma_start(out=o, in_=i), writes=[sm], dma=True)
            e = evx[m % 2]
            S.op("vector", lambda v, o=e.ap, s2=sm.ap: v.tensor_tensor(
                 o, s2[:, 0:512], s2[:, 512:1024], ALU.add),
                 reads=[sm], writes=[e])
            S.op("vector", lambda v, o=e.ap, s2=sm.ap: v.tensor_tensor(
                 o, o, s2[:, 1024:1536], ALU.add),
                 reads=[sm], writes=[e])
            S.op("vector", lambda v, o=e.ap, i=ps_o.ap: v.tensor_tensor(o, o, i, ALU.add),
                 reads=[ps_o], writes=[e])
            S.op("scalar", lambda s, o=fm_chunk(xnew_o, m)[:, off:off + 512], i=e.ap:
                 s.dma_start(out=o, in_=i),
                 reads=[e], dma=True)

    S.fence_all("sync")
    S.build()
    return nc




def build_l1front():
    nc = bass.Bass("TRN2", target_bir_lowering=False, debug=False)
    x_d = nc.dram_tensor("x", [D, T], F32, kind="ExternalInput").ap()
    ipw_d = nc.dram_tensor("ipw", [5136, D], F32, kind="ExternalInput").ap()
    dtb_d = nc.dram_tensor("dtb", [16, 1], F32, kind="ExternalInput").ap()
    xn2_o = nc.dram_tensor("xn2", [D, T], F32, kind="ExternalOutput").ap()
    z_o = nc.dram_tensor("z", [D, T], BF16, kind="ExternalOutput").ap()
    xbc_o = nc.dram_tensor("xbc", [3072, T], F32, kind="ExternalOutput").ap()
    dts_o = nc.dram_tensor("dts", [16, T], F32, kind="ExternalOutput").ap()

    ipw_bf = Buf(nc.dram_tensor("ipw_bf", [5136, D], BF16, kind="Internal").ap())
    ones_row_f = nc.inline_tensor(np.ones((1, 128), dtype=np.float32), name="ones_row_f").ap()
    ones_col_bf = nc.inline_tensor(np.ones((128, 1), dtype=BF), name="ones_col_bf").ap()

    S = Sched(nc)
    x_sb = sb(nc, "x_sb", [128, NCH * T], F32)
    xn2b = sb(nc, "xn2b", [128, NCH * T], BF16)
    wT = [sb(nc, "wT0", [128, NCH * 512], BF16), sb(nc, "wT1", [128, NCH * 512], BF16)]
    scr = [sb(nc, "scr0", [128, T], BF16), sb(nc, "scr1", [128, T], BF16)]
    xc = [sb(nc, "xc0", [128, T], F32), sb(nc, "xc1", [128, T], F32)]
    bc0 = sb(nc, "bc0", [128, T], F32)
    bc1 = sb(nc, "bc1", [128, T], F32)
    rows = sb(nc, "rows", [1, 5 * T], F32)
    dtr = sb(nc, "dtr", [16, 6 * T], F32)
    ev = [sb(nc, "ev0", [128, T], F32), sb(nc, "ev1", [128, T], F32)]
    onesk1 = sb(nc, "onesk1", [1, 128], F32)
    onesb = sb(nc, "onesb", [128, 1], BF16)
    dtb_sb = sb(nc, "dtb_sb", [16, 1], F32)

    pool = nc.alloc_psum_tensor("pool", [128, 4096], F32)
    pap = pool.ap()
    bank_owner = {}

    def take(lo, hi, view):
        olds = [bank_owner[i] for i in range(lo, hi) if i in bank_owner]
        b = alias_over(view, *olds)
        for i in range(lo, hi):
            bank_owner[i] = b
        return b

    S.op("sync", lambda s: s.dma_start(out=sb3(x_sb.ap, T), in_=fm_full(x_d)),
         writes=[x_sb], dma=True)
    S.op("sync", lambda s: s.dma_start(out=onesk1.ap, in_=ones_row_f), writes=[onesk1], dma=True)
    S.op("sync", lambda s: s.dma_start(out=onesb.ap, in_=ones_col_bf), writes=[onesb], dma=True)
    S.op("sync", lambda s: s.dma_start(out=dtb_sb.ap, in_=dtb_d), writes=[dtb_sb], dma=True)
    for p in range(8):
        S.op("gpsimd", lambda g, o=ipw_bf.ap[p * 642:(p + 1) * 642, :],
             i=ipw_d[p * 642:(p + 1) * 642, :]: g.dma_start(out=o, in_=i),
             writes=[ipw_bf], dma=True)

    ps_sum = take(0, 2, pap[0:1, 0:1024])
    ps_sumsq = take(2, 4, pap[0:1, 1024:2048])
    for c in range(NCH):
        cb = scr[c % 2]
        xs = x_sb.ap[:, c * T:(c + 1) * T]
        S.op("vector", lambda v, o=cb.ap, i=xs: v.tensor_copy(o, i),
             reads=[x_sb], writes=[cb])
        for h in range(2):
            lo, hi_ = h * 512, (h + 1) * 512
            S.op("tensor", lambda t, o=ps_sum.ap[0:1, lo:hi_], l=onesb.ap,
                 r=cb.ap[:, lo:hi_], st=(c == 0), sp=(c == NCH - 1):
                 t.matmul(o, l, r, start=st, stop=sp),
                 reads=[cb, onesb], writes=_aw(c, NCH, ps_sum))
    for c in range(NCH):
        sq = scr[c % 2]
        xs = x_sb.ap[:, c * T:(c + 1) * T]
        S.op("scalar", lambda s, o=sq.ap, i=xs: s.activation(o, i, AF.Square),
             reads=[x_sb], writes=[sq])
        for h in range(2):
            lo, hi_ = h * 512, (h + 1) * 512
            S.op("tensor", lambda t, o=ps_sumsq.ap[0:1, lo:hi_], l=onesb.ap,
                 r=sq.ap[:, lo:hi_], st=(c == 0), sp=(c == NCH - 1):
                 t.matmul(o, l, r, start=st, stop=sp),
                 reads=[sq, onesb], writes=_aw(c, NCH, ps_sumsq))

    RW = rows.ap
    r_mu = RW[0:1, 0:T]
    r_var = RW[0:1, T:2 * T]
    r_t0 = RW[0:1, 2 * T:3 * T]
    r_rstd = RW[0:1, 3 * T:4 * T]
    r_mr = RW[0:1, 4 * T:5 * T]
    S.op("vector", lambda v: v.tensor_scalar(r_mu, ps_sum.ap, 1.0 / D, None, ALU.mult),
         reads=[ps_sum], writes=[rows])
    S.op("vector", lambda v: v.tensor_scalar(r_t0, ps_sumsq.ap, 1.0 / D, None, ALU.mult),
         reads=[ps_sumsq], writes=[rows])
    S.op("vector", lambda v: v.tensor_tensor(r_var, r_mu, r_mu, ALU.mult),
         reads=[rows], writes=[rows])
    S.op("vector", lambda v: v.tensor_tensor(r_var, r_t0, r_var, ALU.subtract),
         reads=[rows], writes=[rows])
    S.op("vector", lambda v: v.tensor_scalar(r_var, r_var, 1e-5, None, ALU.add),
         reads=[rows], writes=[rows])
    S.op("scalar", lambda s: s.activation(r_t0, r_var, AF.Sqrt),
         reads=[rows], writes=[rows])
    S.op("vector", lambda v: v.reciprocal(r_rstd, r_t0),
         reads=[rows], writes=[rows])
    S.op("vector", lambda v: v.tensor_tensor(r_mr, r_mu, r_rstd, ALU.mult),
         reads=[rows], writes=[rows])
    ps_bc = take(4, 5, pap[:, 4 * 512:5 * 512])
    bcast_row(S, rows, r_rstd, bc0, onesk1, ps_bc, T)
    bcast_row(S, rows, r_mr, bc1, onesk1, ps_bc, T)

    for c in range(NCH):
        xb = xc[c % 2]
        xs = x_sb.ap[:, c * T:(c + 1) * T]
        S.op("vector", lambda v, o=xb.ap, i=xs: v.tensor_tensor(o, i, bc0.ap, ALU.mult),
             reads=[x_sb, bc0], writes=[xb])
        S.op("vector", lambda v, o=xb.ap: v.tensor_tensor(o, o, bc1.ap, ALU.subtract),
             reads=[bc1], writes=[xb])
        S.op("sync", lambda s, o=fm_chunk(xn2_o, c), i=xb.ap: s.dma_start(out=o, in_=i),
             reads=[xb], dma=True)
        S.op("vector", lambda v, o=xn2b.ap[:, c * T:(c + 1) * T], i=xb.ap: v.tensor_copy(o, i),
             reads=[xb], writes=[xn2b])

    for blk in range(10):
        w = wT[blk % 2]
        for dc in range(NCH):
            S.op("sync", lambda s, o=w.ap[:, dc * 512:(dc + 1) * 512],
                 i=ipw_bf.ap[blk * 512:(blk + 1) * 512, dc * 128:(dc + 1) * 128]:
                 s.dma_start_transpose(o, i),
                 reads=[ipw_bf], writes=[w], dma=True)
        for m in range(4):
            oc = blk * 4 + m
            pm = take(5 + m % 2, 6 + m % 2, pap[:, (5 + m % 2) * 512:(6 + m % 2) * 512])
            e = ev[m % 2]
            for h in range(2):
                lo, hi_ = h * 512, (h + 1) * 512
                for dc in range(NCH):
                    S.op("tensor", lambda t, o=pm.ap, w=w,
                         l=w.ap[:, dc * 512 + m * 128:dc * 512 + m * 128 + 128],
                         r=xn2b.ap[:, dc * T + lo:dc * T + hi_],
                         st=(dc == 0), sp=(dc == NCH - 1):
                         t.matmul(o, l, r, start=st, stop=sp),
                         reads=[w, xn2b], writes=_aw(dc, NCH, pm))
                if oc < NCH:
                    S.op("vector", lambda v, o=scr[0].ap[:, lo:hi_], i=pm.ap: v.tensor_copy(o, i),
                         reads=[pm], writes=[scr[0]])
                else:
                    S.op("vector", lambda v, o=e.ap[:, lo:hi_], i=pm.ap: v.tensor_copy(o, i),
                         reads=[pm], writes=[e])
            if oc < NCH:
                S.op("scalar", lambda s, o=fm_chunk(z_o, oc), i=scr[0].ap:
                     s.dma_start(out=o, in_=i),
                     reads=[scr[0]], dma=True)
            else:
                S.op("scalar", lambda s, o=fm_chunk(xbc_o, oc - NCH), i=e.ap:
                     s.dma_start(out=o, in_=i),
                     reads=[e], dma=True)
    wdt = wT[0]
    for dc in range(NCH):
        S.op("sync", lambda s, o=wdt.ap[:, dc * 16:(dc + 1) * 16],
             i=ipw_bf.ap[5120:5136, dc * 128:(dc + 1) * 128]:
             s.dma_start_transpose(o, i),
             reads=[ipw_bf], writes=[wdt], dma=True)
    DTR = dtr.ap
    d_raw = DTR[0:16, 0:T]
    d_a = DTR[0:16, T:2 * T]
    d_e = DTR[0:16, 2 * T:3 * T]
    d_r = DTR[0:16, 3 * T:4 * T]
    d_s = DTR[0:16, 4 * T:5 * T]
    pm = take(5, 6, pap[:, 5 * 512:6 * 512])
    for h in range(2):
        lo, hi_ = h * 512, (h + 1) * 512
        for dc in range(NCH):
            S.op("tensor", lambda t, o=pm.ap[0:16, 0:512],
                 l=wdt.ap[:, dc * 16:(dc + 1) * 16],
                 r=xn2b.ap[:, dc * T + lo:dc * T + hi_],
                 st=(dc == 0), sp=(dc == NCH - 1):
                 t.matmul(o, l, r, start=st, stop=sp),
                 reads=[wdt, xn2b], writes=_aw(dc, NCH, pm))
        S.op("vector", lambda v, o=d_raw[0:16, lo:hi_], i=pm.ap[0:16, 0:512]:
             v.tensor_copy(o, i),
             reads=[pm], writes=[dtr])
    S.op("vector", lambda v: v.tensor_scalar(d_raw, d_raw, dtb_sb.ap, None, ALU.add),
         reads=[dtb_sb], writes=[dtr])
    S.op("scalar", lambda s: s.activation(d_a, d_raw, AF.Abs),
         reads=[dtr], writes=[dtr])
    S.op("scalar", lambda s: s.activation(d_e, d_a, AF.Exp, scale=-1.0),
         reads=[dtr], writes=[dtr])
    S.op("vector", lambda v: v.tensor_scalar(d_e, d_e, 1.0, None, ALU.add),
         reads=[dtr], writes=[dtr])
    S.op("scalar", lambda s: s.activation(d_e, d_e, AF.Ln),
         reads=[dtr], writes=[dtr])
    S.op("vector", lambda v: v.tensor_scalar(d_r, d_raw, 0.0, None, ALU.max),
         reads=[dtr], writes=[dtr])
    S.op("vector", lambda v: v.tensor_tensor(d_s, d_r, d_e, ALU.add),
         reads=[dtr], writes=[dtr])
    S.op("scalar", lambda s, o=dts_o: s.dma_start(out=o, in_=d_s),
         reads=[dtr], dma=True)

    S.fence_all("sync")
    S.build()
    return nc




def build_l1conv():
    nc = bass.Bass("TRN2", target_bir_lowering=False, debug=False)
    xbc_d = nc.dram_tensor("xbc", [3072, T], F32, kind="ExternalInput").ap()
    halo_d = nc.dram_tensor("halo", [3072, 3], F32, kind="ExternalInput").ap()
    cw_d = nc.dram_tensor("cw", [128, 96], F32, kind="ExternalInput").ap()
    cb_d = nc.dram_tensor("cb", [128, 24], F32, kind="ExternalInput").ap()
    dts_d = nc.dram_tensor("dts", [16, T], F32, kind="ExternalInput").ap()
    alog_d = nc.dram_tensor("alog", [16, 1], F32, kind="ExternalInput").ap()
    xbcp_o = nc.dram_tensor("xbcp", [3072, T], BF16, kind="ExternalOutput").ap()
    s_o = nc.dram_tensor("s", [128, 4 * 16 * 128], BF16, kind="ExternalOutput").ap()
    dec_o = nc.dram_tensor("dec", [1, 64], F32, kind="ExternalOutput").ap()

    ut_f = nc.inline_tensor(np.triu(np.ones((128, 128), dtype=np.float32)), name="ut_f").ap()
    on_f = nc.inline_tensor(np.ones((128, 128), dtype=np.float32), name="on_f").ap()
    ident_f = nc.inline_tensor(np.eye(128, dtype=np.float32), name="ident_f").ap()
    ident_b = nc.inline_tensor(np.eye(128, dtype=BF), name="ident_b").ap()
    ones_row_f = nc.inline_tensor(np.ones((1, 128), dtype=np.float32), name="ones_row_f").ap()

    S = Sched(nc)
    xbcp = sb(nc, "xbcp_sb", [128, 24 * T], BF16)
    xe = [sb(nc, "xe0", [128, T + 3], F32), sb(nc, "xe1", [128, T + 3], F32)]
    acc = [sb(nc, "acc0", [128, T], F32), sb(nc, "acc1", [128, T], F32)]
    cw_sb = sb(nc, "cw_sb", [128, 96], F32)
    cb_sb = sb(nc, "cb_sb", [128, 24], F32)
    dts_sb = sb(nc, "dts_sb", [16, T], F32)
    arow = sb(nc, "arow", [16, 8], F32)
    datm = sb(nc, "datm", [128, 8 * 16], F32)
    dttm = sb(nc, "dttm", [128, 8 * 16], F32)
    dxtm = sb(nc, "dxtm", [128, 8 * 16], F32)
    comb = sb(nc, "comb", [128, 8 * 16], F32)
    dafm = sb(nc, "dafm", [16, 256], F32)
    x_tm = sb(nc, "x_tm", [128, 8 * 128], BF16)
    b_tm = sb(nc, "b_tm", [128, 8 * 128], BF16)
    xdd = sb(nc, "xdd", [128, 2 * 128], BF16)
    sev = [sb(nc, "sev0", [128, 128], BF16), sb(nc, "sev1", [128, 128], BF16)]
    tot_row = sb(nc, "tot_row", [1, 16], F32)
    totb = sb(nc, "totb", [128, 16], F32)
    dec_row = sb(nc, "dec_row", [1, 64], F32)
    onesk1 = sb(nc, "onesk1", [1, 128], F32)
    idf = sb(nc, "idf", [128, 128], F32)
    idb = sb(nc, "idb", [128, 128], BF16)
    utf = sb(nc, "utf", [128, 128], F32)
    onf = sb(nc, "onf", [128, 128], F32)

    pool = nc.alloc_psum_tensor("pool", [128, 3072], F32)
    poolb = nc.alloc_psum_tensor("poolb", [128, 1024], BF16)
    pap = pool.ap()
    bap = poolb.ap()
    bank_owner = {}

    def take(lo, hi, view):
        olds = [bank_owner[i] for i in range(lo, hi) if i in bank_owner]
        b = alias_over(view, *olds)
        for i in range(lo, hi):
            bank_owner[i] = b
        return b

    for name, dst, srcap in (("cw", cw_sb, cw_d), ("cb", cb_sb, cb_d), ("dts", dts_sb, dts_d),
                             ("ok1", onesk1, ones_row_f), ("idf", idf, ident_f),
                             ("idb", idb, ident_b), ("utf", utf, ut_f), ("onf", onf, on_f)):
        S.op("sync", lambda s, o=dst.ap, i=srcap: s.dma_start(out=o, in_=i),
             writes=[dst], dma=True)
    S.op("sync", lambda s: s.dma_start(out=arow.ap[0:16, 0:1], in_=alog_d),
         writes=[arow], dma=True)
    S.op("scalar", lambda s: s.activation(arow.ap[0:16, 1:2], arow.ap[0:16, 0:1], AF.Exp),
         reads=[arow], writes=[arow])
    S.op("vector", lambda v: v.tensor_scalar(arow.ap[0:16, 1:2], arow.ap[0:16, 1:2],
                                             -1.0, None, ALU.mult),
         reads=[arow], writes=[arow])

    for cc in range(24):
        x = xe[cc % 2]
        a = acc[cc % 2]
        S.op("sync", lambda s, o=x.ap[:, 0:3], i=halo_d[cc * 128:(cc + 1) * 128, :]:
             s.dma_start(out=o, in_=i), writes=[x], dma=True)
        S.op("sync", lambda s, o=x.ap[:, 3:T + 3], i=xbc_d[cc * 128:(cc + 1) * 128, :]:
             s.dma_start(out=o, in_=i), writes=[x], dma=True)
        S.op("vector", lambda v, x=x, a=a, wc=cw_sb.ap[:, cc * 4:cc * 4 + 1]:
             v.tensor_scalar(a.ap, x.ap[:, 0:T], wc, None, ALU.mult),
             reads=[x, cw_sb], writes=[a])
        for k2 in range(1, 4):
            S.op("vector", lambda v, x=x, a=a, k2=k2, wc=cw_sb.ap[:, cc * 4 + k2:cc * 4 + k2 + 1]:
                 v.scalar_tensor_tensor(a.ap, x.ap[:, k2:T + k2], wc, a.ap, ALU.mult, ALU.add),
                 reads=[x, cw_sb], writes=[a])
        S.op("scalar", lambda s, a=a, o=xbcp.ap[:, cc * T:(cc + 1) * T],
             bc=cb_sb.ap[:, cc:cc + 1]:
             s.activation(o, a.ap, AF.Silu, bias=bc),
             reads=[a, cb_sb], writes=[xbcp])
        S.op("scalar", lambda s, o=xbcp_o[cc * 128:(cc + 1) * 128, :],
             i=xbcp.ap[:, cc * T:(cc + 1) * T]:
             s.dma_start(out=o, in_=i),
             reads=[xbcp], dma=True)

    for j in range(4):
        S.op("vector", lambda v, j=j: v.tensor_scalar(
             dafm.ap, dts_sb.ap[0:16, j * 256:(j + 1) * 256], arow.ap[0:16, 1:2], None, ALU.mult),
             reads=[dts_sb, arow], writes=[dafm])
        S.op("vector", lambda v, j=j: v.tensor_reduce(
             arow.ap[0:16, 2 + j:3 + j], dafm.ap, AX, ALU.add),
             reads=[dafm], writes=[arow])
        ps_t = take(4, 6, bap[:, 0:1024])
        for st2 in range(2):
            ps_f = take(0, 1, pap[:, 0:512])
            S.op("tensor", lambda t, o=ps_f.ap[:, 0:16],
                 i=dafm.ap[0:16, st2 * 128:(st2 + 1) * 128], st2=st2:
                 t.transpose(o, i, idf.ap[0:16, 0:16]),
                 reads=[dafm, idf], writes=[ps_f])
            S.op("vector", lambda v, o=datm.ap[:, (j * 2 + st2) * 16:(j * 2 + st2 + 1) * 16],
                 i=ps_f.ap[:, 0:16]: v.tensor_copy(o, i),
                 reads=[ps_f], writes=[datm])
            S.op("tensor", lambda t, o=ps_f.ap[:, 16:32],
                 i=dts_sb.ap[0:16, j * 256 + st2 * 128:j * 256 + (st2 + 1) * 128]:
                 t.transpose(o, i, idf.ap[0:16, 0:16]),
                 reads=[dts_sb, idf], writes=[ps_f])
            S.op("vector", lambda v, o=dttm.ap[:, (j * 2 + st2) * 16:(j * 2 + st2 + 1) * 16],
                 i=ps_f.ap[:, 16:32]: v.tensor_copy(o, i),
                 reads=[ps_f], writes=[dttm])
        ps_a = take(1, 2, pap[:, 512:1024])
        S.op("tensor", lambda t, o=ps_a.ap[:, 0:16], l=utf.ap,
             r=datm.ap[:, (j * 2) * 16:(j * 2 + 1) * 16]:
             t.matmul(o, l, r, start=True, stop=True),
             reads=[utf, datm], writes=[ps_a])
        S.op("tensor", lambda t, o=ps_a.ap[:, 16:32], l=onf.ap,
             r=datm.ap[:, (j * 2) * 16:(j * 2 + 1) * 16]:
             t.matmul(o, l, r, start=True, stop=False),
             reads=[onf, datm], writes=[ps_a])
        S.op("tensor", lambda t, o=ps_a.ap[:, 16:32], l=utf.ap,
             r=datm.ap[:, (j * 2 + 1) * 16:(j * 2 + 2) * 16]:
             t.matmul(o, l, r, start=False, stop=True),
             reads=[utf, datm], writes=[ps_a])
        ps_r = take(2, 3, pap[:, 1024:1536])
        S.op("tensor", lambda t, o=ps_r.ap[0:1, 0:16], i=arow.ap[0:16, 2 + j:3 + j]:
             t.transpose(o, i, idf.ap[0:16, 0:16]),
             reads=[arow, idf], writes=[ps_r])
        S.op("vector", lambda v: v.tensor_copy(tot_row.ap, ps_r.ap[0:1, 0:16]),
             reads=[ps_r], writes=[tot_row])
        ps_b = take(2, 3, pap[:, 1024:1536])
        S.op("tensor", lambda t, o=ps_b.ap[:, 0:16], l=onesk1.ap, r=tot_row.ap:
             t.matmul(o, l, r, start=True, stop=True),
             reads=[onesk1, tot_row], writes=[ps_b])
        S.op("vector", lambda v: v.tensor_copy(totb.ap, ps_b.ap[:, 0:16]),
             reads=[ps_b], writes=[totb])
        for st2 in range(2):
            S.op("vector", lambda v, st2=st2,
                 o=dxtm.ap[:, (j * 2 + st2) * 16:(j * 2 + st2 + 1) * 16],
                 i=ps_a.ap[:, st2 * 16:(st2 + 1) * 16]:
                 v.tensor_tensor(o, totb.ap, i, ALU.subtract),
                 reads=[ps_a, totb], writes=[dxtm])
            S.op("scalar", lambda s, st2=st2,
                 o=dxtm.ap[:, (j * 2 + st2) * 16:(j * 2 + st2 + 1) * 16]:
                 s.activation(o, o, AF.Exp),
                 reads=[dxtm], writes=[dxtm])
            S.op("vector", lambda v, st2=st2,
                 o=comb.ap[:, (j * 2 + st2) * 16:(j * 2 + st2 + 1) * 16],
                 a=dttm.ap[:, (j * 2 + st2) * 16:(j * 2 + st2 + 1) * 16],
                 b=dxtm.ap[:, (j * 2 + st2) * 16:(j * 2 + st2 + 1) * 16]:
                 v.tensor_tensor(o, a, b, ALU.mult),
                 reads=[dttm, dxtm], writes=[comb])
        for kv in range(4):
            for st2 in range(2):
                ps_tb = take(4, 6, bap[:, 0:1024])
                S.op("tensor", lambda t, o=ps_tb.ap[:, 0:128],
                     i=xbcp.ap[:, kv * T + j * 256 + st2 * 128:kv * T + j * 256 + (st2 + 1) * 128]:
                     t.transpose(o, i, idb.ap),
                     reads=[xbcp, idb], writes=[ps_tb])
                S.op("vector", lambda v, o=x_tm.ap[:, (kv * 2 + st2) * 128:(kv * 2 + st2 + 1) * 128],
                     i=ps_tb.ap[:, 0:128]: v.tensor_copy(o, i),
                     reads=[ps_tb], writes=[x_tm])
                S.op("tensor", lambda t, o=ps_tb.ap[:, 128:256],
                     i=xbcp.ap[:, (4 + kv) * T + j * 256 + st2 * 128:(4 + kv) * T + j * 256 + (st2 + 1) * 128]:
                     t.transpose(o, i, idb.ap),
                     reads=[xbcp, idb], writes=[ps_tb])
                S.op("vector", lambda v, o=b_tm.ap[:, (kv * 2 + st2) * 128:(kv * 2 + st2 + 1) * 128],
                     i=ps_tb.ap[:, 128:256]: v.tensor_copy(o, i),
                     reads=[ps_tb], writes=[b_tm])
        for h in range(16):
            kv = h // 4
            for st2 in range(2):
                S.op("vector", lambda v, st2=st2, kv=kv, h=h,
                     o=xdd.ap[:, st2 * 128:(st2 + 1) * 128],
                     xi=x_tm.ap[:, (kv * 2 + st2) * 128:(kv * 2 + st2 + 1) * 128],
                     cc2=comb.ap[:, (j * 2 + st2) * 16 + h:(j * 2 + st2) * 16 + h + 1]:
                     v.tensor_scalar(o, xi, cc2, None, ALU.mult),
                     reads=[x_tm, comb], writes=[xdd])
            ps_s = take(3 if h % 2 == 0 else 0, 4 if h % 2 == 0 else 1,
                        pap[:, (3 if h % 2 == 0 else 0) * 512:(4 if h % 2 == 0 else 1) * 512])
            for st2 in range(2):
                S.op("tensor", lambda t, st2=st2, kv=kv, o=ps_s.ap[:, 0:128],
                     l=b_tm.ap[:, (kv * 2 + st2) * 128:(kv * 2 + st2 + 1) * 128],
                     r=xdd.ap[:, st2 * 128:(st2 + 1) * 128],
                     st=(st2 == 0), sp=(st2 == 1):
                     t.matmul(o, l, r, start=st, stop=sp),
                     reads=[b_tm, xdd], writes=_aw(st2, 2, ps_s))
            e = sev[h % 2]
            S.op("vector", lambda v, o=e.ap, i=ps_s.ap[:, 0:128]: v.tensor_copy(o, i),
                 reads=[ps_s], writes=[e])
            S.op("scalar", lambda s, o=s_o[:, (j * 16 + h) * 128:(j * 16 + h + 1) * 128], i=e.ap:
                 s.dma_start(out=o, in_=i),
                 reads=[e], dma=True)
    for j in range(4):
        S.op("scalar", lambda s, j=j: s.activation(
             arow.ap[0:16, 6:7], arow.ap[0:16, 2 + j:3 + j], AF.Exp),
             reads=[arow], writes=[arow])
        ps_r = take(2, 3, pap[:, 1024:1536])
        S.op("tensor", lambda t, o=ps_r.ap[0:1, 0:16], i=arow.ap[0:16, 6:7]:
             t.transpose(o, i, idf.ap[0:16, 0:16]),
             reads=[arow, idf], writes=[ps_r])
        S.op("vector", lambda v, j=j: v.tensor_copy(dec_row.ap[0:1, j * 16:(j + 1) * 16],
                                                    ps_r.ap[0:1, 0:16]),
             reads=[ps_r], writes=[dec_row])
    S.op("sync", lambda s: s.dma_start(out=dec_o, in_=dec_row.ap),
         reads=[dec_row], dma=True)

    S.fence_all("sync")
    S.build()
    return nc




def build_l1ssd(debug=False):
    nc = bass.Bass("TRN2", target_bir_lowering=False, debug=False)
    xbcp_d = nc.dram_tensor("xbcp", [3072, T], BF16, kind="ExternalInput").ap()
    dts_d = nc.dram_tensor("dts", [16, T], F32, kind="ExternalInput").ap()
    z_d = nc.dram_tensor("z", [D, T], BF16, kind="ExternalInput").ap()
    sin_d = nc.dram_tensor("sin", [128, 32 * 2048], BF16, kind="ExternalInput").ap()
    din_d = nc.dram_tensor("din", [1, 512], F32, kind="ExternalInput").ap()
    sown_d = nc.dram_tensor("sown", [128, 8192], BF16, kind="ExternalInput").ap()
    dec_d = nc.dram_tensor("dec", [1, 64], F32, kind="ExternalInput").ap()
    alog_d = nc.dram_tensor("alog", [16, 1], F32, kind="ExternalInput").ap()
    dD_d = nc.dram_tensor("dD", [1, 16], F32, kind="ExternalInput").ap()
    nw_d = nc.dram_tensor("nw", [128, 16], F32, kind="ExternalInput").ap()
    opw_d = nc.dram_tensor("opw", [D, D], F32, kind="ExternalInput").ap()
    a1_o = nc.dram_tensor("attn1", [D, T], F32, kind="ExternalOutput").ap()
    if debug:
        ydbg_o = nc.dram_tensor("ydbg", [D, T], F32, kind="ExternalOutput").ap()
        gtd_o = nc.dram_tensor("gtdbg", [128, 512], F32, kind="ExternalOutput").ap()

    opw_bf = Buf(nc.dram_tensor("opw_bf", [D, D], BF16, kind="Internal").ap())
    p0 = (np.arange(256)[None, :] >= np.arange(128)[:, None]).astype(np.float32)
    p1 = (np.arange(256)[None, :] >= (128 + np.arange(128))[:, None]).astype(np.float32)
    P0 = nc.inline_tensor(p0, name="P0").ap()
    P1 = nc.inline_tensor(p1, name="P1").ap()
    bd = np.zeros((16, 16 * 128), dtype=np.float32)
    for h in range(16):
        bd[h, h * 128:(h + 1) * 128] = 1.0
    BD = nc.inline_tensor(bd, name="BD").ap()
    ident_f = nc.inline_tensor(np.eye(128, dtype=np.float32), name="ident_f").ap()
    ident_b = nc.inline_tensor(np.eye(128, dtype=BF), name="ident_b").ap()
    ones_row_f = nc.inline_tensor(np.ones((1, 128), dtype=np.float32), name="ones_row_f").ap()
    ones_col_bf = nc.inline_tensor(np.ones((128, 1), dtype=BF), name="ones_col_bf").ap()

    S = Sched(nc)
    xbcp = sb(nc, "xbcp_sb", [128, 24 * T], BF16)
    z_sb = sb(nc, "z_sb", [128, NCH * T], BF16)
    yn = sb(nc, "yn", [128, NCH * T], BF16)
    H = sb(nc, "H", [128, 2048], F32)
    sinb = [sb(nc, "sin0", [128, 2048], BF16), sb(nc, "sin1", [128, 2048], BF16)]
    sob = [sb(nc, "sob0", [128, 2048], BF16), sb(nc, "sob1", [128, 2048], BF16)]
    dts_sb = sb(nc, "dts_sb", [16, T], F32)
    dinb = sb(nc, "dinb", [128, 512], F32)
    cdb = sb(nc, "cdb", [128, 64], F32)
    Db = sb(nc, "Db", [128, 16], F32)
    nw_sb = sb(nc, "nw_sb", [128, 16], F32)
    arow = sb(nc, "arow", [16, 4], F32)
    dafm = sb(nc, "dafm", [16, 256], F32)
    acsfm = sb(nc, "acsfm", [16, 256], F32)
    acstm = sb(nc, "acstm", [128, 32], F32)
    dttm = sb(nc, "dttm", [128, 32], F32)
    x_tm = sb(nc, "x_tm", [128, 8 * 128], BF16)
    xdt = sb(nc, "xdt", [128, 2 * 128], BF16)
    gT = sb(nc, "gT", [128, 2 * 256], BF16)
    ce = sb(nc, "ce", [128, 256], BF16)
    eacs = sb(nc, "eacs", [128, 256], BF16)
    hbf = sb(nc, "hbf", [128, 128], BF16)
    scr1 = sb(nc, "scr1", [128, 256], F32)
    scr2 = sb(nc, "scr2", [128, 256], F32)
    ysb = sb(nc, "ysb", [128, 256], F32)
    gsb = sb(nc, "gsb", [128, 256], F32)
    szb = sb(nc, "szb", [128, 256], BF16)
    sqb = sb(nc, "sqb", [128, 256], BF16)
    rrow = sb(nc, "rrow", [1, 256], F32)
    wT = [sb(nc, "wT0", [128, NCH * 512], BF16), sb(nc, "wT1", [128, NCH * 512], BF16)]
    ev = [sb(nc, "ev0", [128, 512], F32), sb(nc, "ev1", [128, 512], F32)]
    bd_sb = sb(nc, "bd_sb", [16, 16 * 128], F32)
    p0_sb = sb(nc, "p0_sb", [128, 256], F32)
    p1_sb = sb(nc, "p1_sb", [128, 256], F32)
    onesk1 = sb(nc, "onesk1", [1, 128], F32)
    onesb = sb(nc, "onesb", [128, 1], BF16)
    idf = sb(nc, "idf", [128, 128], F32)
    idb = sb(nc, "idb", [128, 128], BF16)

    pool = nc.alloc_psum_tensor("pool", [128, 3584], F32)
    poolb = nc.alloc_psum_tensor("poolb", [128, 1024], BF16)
    pap = pool.ap()
    bap = poolb.ap()
    bank_owner = {}

    def take(lo, hi, view):
        olds = [bank_owner[i] for i in range(lo, hi) if i in bank_owner]
        b = alias_over(view, *olds)
        for i in range(lo, hi):
            bank_owner[i] = b
        return b

    S.op("sync", lambda s: s.dma_start(out=sb3(xbcp.ap, T), in_=fm_full(xbcp_d)),
         writes=[xbcp], dma=True)
    S.op("sync", lambda s: s.dma_start(out=sb3(z_sb.ap, T), in_=fm_full(z_d)),
         writes=[z_sb], dma=True)
    for dst, srcap in ((dts_sb, dts_d), (nw_sb, nw_d), (bd_sb, BD), (p0_sb, P0),
                       (p1_sb, P1), (onesk1, ones_row_f), (onesb, ones_col_bf),
                       (idf, ident_f), (idb, ident_b)):
        S.op("sync", lambda s, o=dst.ap, i=srcap: s.dma_start(out=o, in_=i),
             writes=[dst], dma=True)
    S.op("sync", lambda s: s.dma_start(out=arow.ap[0:16, 0:1], in_=alog_d),
         writes=[arow], dma=True)
    for p in range(4):
        S.op("gpsimd", lambda g, o=opw_bf.ap[p * 512:(p + 1) * 512, :],
             i=opw_d[p * 512:(p + 1) * 512, :]: g.dma_start(out=o, in_=i),
             writes=[opw_bf], dma=True)
    S.op("scalar", lambda s: s.activation(arow.ap[0:16, 1:2], arow.ap[0:16, 0:1], AF.Exp),
         reads=[arow], writes=[arow])
    S.op("vector", lambda v: v.tensor_scalar(arow.ap[0:16, 1:2], arow.ap[0:16, 1:2],
                                             -1.0, None, ALU.mult),
         reads=[arow], writes=[arow])
    dinr = sb(nc, "dinr", [1, 512], F32)
    decr = sb(nc, "decr", [1, 64], F32)
    dDr = sb(nc, "dDr", [1, 16], F32)
    for dst, srcap in ((dinr, din_d), (decr, dec_d), (dDr, dD_d)):
        S.op("sync", lambda s, o=dst.ap, i=srcap: s.dma_start(out=o, in_=i),
             writes=[dst], dma=True)
    ps_b = take(5, 6, pap[:, 5 * 512:6 * 512])
    S.op("tensor", lambda t: t.matmul(ps_b.ap, onesk1.ap, dinr.ap, start=True, stop=True),
         reads=[onesk1, dinr], writes=[ps_b])
    S.op("vector", lambda v: v.tensor_copy(dinb.ap, ps_b.ap), reads=[ps_b], writes=[dinb])
    ps_b2 = take(5, 6, pap[:, 5 * 512:6 * 512])
    S.op("tensor", lambda t: t.matmul(ps_b2.ap[:, 0:64], onesk1.ap, decr.ap, start=True, stop=True),
         reads=[onesk1, decr], writes=[ps_b2])
    S.op("vector", lambda v: v.tensor_copy(cdb.ap, ps_b2.ap[:, 0:64]), reads=[ps_b2], writes=[cdb])
    ps_b3 = take(5, 6, pap[:, 5 * 512:6 * 512])
    S.op("tensor", lambda t: t.matmul(ps_b3.ap[:, 0:16], onesk1.ap, dDr.ap, start=True, stop=True),
         reads=[onesk1, dDr], writes=[ps_b3])
    S.op("vector", lambda v: v.tensor_copy(Db.ap, ps_b3.ap[:, 0:16]), reads=[ps_b3], writes=[Db])

    S.op("vector", lambda v: v.memset(H.ap, 0.0), writes=[H])
    for c in range(32):
        sc = sinb[c % 2]
        S.op("sync", lambda s, o=sc.ap, i=sin_d[:, c * 2048:(c + 1) * 2048]:
             s.dma_start(out=o, in_=i),
             writes=[sc], dma=True)
        for h in range(16):
            S.op("vector", lambda v, c=c, h=h, sc=sc:
                 v.scalar_tensor_tensor(H.ap[:, h * 128:(h + 1) * 128],
                                        H.ap[:, h * 128:(h + 1) * 128],
                                        dinb.ap[:, c * 16 + h:c * 16 + h + 1],
                                        sc.ap[:, h * 128:(h + 1) * 128],
                                        ALU.mult, ALU.add),
                 reads=[sc, dinb], writes=[H])

    mask_sb = [p0_sb, p1_sb]
    for j in range(4):
        soj = sob[j % 2]
        S.op("sync", lambda s, o=soj.ap, i=sown_d[:, j * 2048:(j + 1) * 2048]:
             s.dma_start(out=o, in_=i),
             writes=[soj], dma=True)
        S.op("vector", lambda v, j=j: v.tensor_scalar(
             dafm.ap, dts_sb.ap[0:16, j * 256:(j + 1) * 256], arow.ap[0:16, 1:2], None, ALU.mult),
             reads=[dts_sb, arow], writes=[dafm])
        ps_f = take(6, 7, pap[:, 6 * 512:7 * 512])
        for st2 in range(2):
            S.op("tensor", lambda t, st2=st2, o=ps_f.ap[:, st2 * 16:(st2 + 1) * 16],
                 i=dafm.ap[0:16, st2 * 128:(st2 + 1) * 128]:
                 t.transpose(o, i, idf.ap[0:16, 0:16]),
                 reads=[dafm, idf], writes=[ps_f])
            S.op("tensor", lambda t, st2=st2, j=j, o=ps_f.ap[:, 32 + st2 * 16:32 + (st2 + 1) * 16],
                 i=dts_sb.ap[0:16, j * 256 + st2 * 128:j * 256 + (st2 + 1) * 128]:
                 t.transpose(o, i, idf.ap[0:16, 0:16]),
                 reads=[dts_sb, idf], writes=[ps_f])
        S.op("vector", lambda v: v.tensor_copy(acstm.ap[:, 0:32], ps_f.ap[:, 0:32]),
             reads=[ps_f], writes=[acstm])
        S.op("vector", lambda v: v.tensor_copy(dttm.ap, ps_f.ap[:, 32:64]),
             reads=[ps_f], writes=[dttm])
        ps_af = take(6, 7, pap[:, 6 * 512:7 * 512])
        S.op("tensor", lambda t, o=ps_af.ap[0:16, 0:256], l=acstm.ap[:, 0:16], r=p0_sb.ap:
             t.matmul(o, l, r, start=True, stop=False),
             reads=[acstm, p0_sb], writes=[ps_af])
        S.op("tensor", lambda t, o=ps_af.ap[0:16, 0:256], l=acstm.ap[:, 16:32], r=p1_sb.ap:
             t.matmul(o, l, r, start=False, stop=True),
             reads=[acstm, p1_sb], writes=[ps_af])
        S.op("vector", lambda v: v.tensor_copy(acsfm.ap, ps_af.ap[0:16, 0:256]),
             reads=[ps_af], writes=[acsfm])
        ps_f2 = take(6, 7, pap[:, 6 * 512:7 * 512])
        for st2 in range(2):
            S.op("tensor", lambda t, st2=st2, o=ps_f2.ap[:, st2 * 16:(st2 + 1) * 16],
                 i=acsfm.ap[0:16, st2 * 128:(st2 + 1) * 128]:
                 t.transpose(o, i, idf.ap[0:16, 0:16]),
                 reads=[acsfm, idf], writes=[ps_f2])
        S.op("vector", lambda v: v.tensor_copy(acstm.ap, ps_f2.ap[:, 0:32]),
             reads=[ps_f2], writes=[acstm])
        for kv in range(4):
            for st2 in range(2):
                ps_tb = take(7, 8, bap[:, 0:512])
                S.op("tensor", lambda t, kv=kv, st2=st2, j=j, o=ps_tb.ap[:, 0:128],
                     i=xbcp.ap[:, kv * T + j * 256 + st2 * 128:kv * T + j * 256 + (st2 + 1) * 128]:
                     t.transpose(o, i, idb.ap),
                     reads=[xbcp, idb], writes=[ps_tb])
                S.op("vector", lambda v, kv=kv, st2=st2,
                     o=x_tm.ap[:, (kv * 2 + st2) * 128:(kv * 2 + st2 + 1) * 128],
                     i=ps_tb.ap[:, 0:128]: v.tensor_copy(o, i),
                     reads=[ps_tb], writes=[x_tm])
        for h in range(16):
            kv = h // 4
            ps_ab = take(0, 1, pap[:, 0:512])
            S.op("tensor", lambda t, h=h, o=ps_ab.ap[:, 0:256],
                 l=bd_sb.ap[0:16, h * 128:(h + 1) * 128], r=acsfm.ap:
                 t.matmul(o, l, r, start=True, stop=True),
                 reads=[bd_sb, acsfm], writes=[ps_ab])
            S.op("scalar", lambda s, o=eacs.ap, i=ps_ab.ap[:, 0:256]:
                 s.activation(o, i, AF.Exp),
                 reads=[ps_ab], writes=[eacs])
            S.op("vector", lambda v, h=h, j=j, o=ce.ap,
                 i=xbcp.ap[:, (8 + h) * T + j * 256:(8 + h) * T + (j + 1) * 256]:
                 v.tensor_tensor(o, i, eacs.ap, ALU.mult),
                 reads=[xbcp, eacs], writes=[ce])
            ps_g = [take(1, 2, pap[:, 512:1024]), take(2, 3, pap[:, 1024:1536])]
            for st2 in range(2):
                S.op("tensor", lambda t, st2=st2, kv=kv, h=h, j=j, o=ps_g[st2].ap[:, 0:256],
                     l=xbcp.ap[:, (4 + kv) * T + j * 256 + st2 * 128:(4 + kv) * T + j * 256 + (st2 + 1) * 128],
                     r=xbcp.ap[:, (8 + h) * T + j * 256:(8 + h) * T + (j + 1) * 256]:
                     t.matmul(o, l, r, start=True, stop=True),
                     reads=[xbcp], writes=[ps_g[st2]])
            for st2 in range(2):
                S.op("vector", lambda v, st2=st2, h=h, o=scr1.ap, i=ps_ab.ap[:, 0:256]:
                     v.tensor_scalar(o, i, acstm.ap[:, st2 * 16 + h:st2 * 16 + h + 1], None, ALU.subtract),
                     reads=[ps_ab, acstm], writes=[scr1])
                S.op("vector", lambda v, st2=st2, o=scr1.ap, m=mask_sb[st2].ap:
                     v.tensor_tensor(o, o, m, ALU.mult),
                     reads=[mask_sb[st2]], writes=[scr1])
                S.op("scalar", lambda s: s.activation(scr2.ap, scr1.ap, AF.Exp),
                     reads=[scr1], writes=[scr2])
                S.op("vector", lambda v, st2=st2, h=h, o=scr2.ap, g2=ps_g[st2].ap[:, 0:256]:
                     v.scalar_tensor_tensor(o, o, dttm.ap[:, st2 * 16 + h:st2 * 16 + h + 1],
                                            g2, ALU.mult, ALU.mult),
                     reads=[ps_g[st2], dttm], writes=[scr2])
                S.op("vector", lambda v, st2=st2, o=gT.ap[:, st2 * 256:(st2 + 1) * 256],
                     m=mask_sb[st2].ap:
                     v.tensor_tensor(o, scr2.ap, m, ALU.mult),
                     reads=[scr2, mask_sb[st2]], writes=[gT])
            S.op("vector", lambda v, h=h: v.tensor_copy(hbf.ap, H.ap[:, h * 128:(h + 1) * 128]),
                 reads=[H], writes=[hbf])
            ps_y = take(3 + h % 2, 4 + h % 2, pap[:, (3 + h % 2) * 512:(4 + h % 2) * 512])
            for st2 in range(2):
                S.op("tensor", lambda t, st2=st2, kv=kv, o=ps_y.ap[:, 0:256],
                     l=x_tm.ap[:, (kv * 2 + st2) * 128:(kv * 2 + st2 + 1) * 128],
                     r=gT.ap[:, st2 * 256:(st2 + 1) * 256],
                     st=(st2 == 0):
                     t.matmul(o, l, r, start=st, stop=False),
                     reads=[x_tm, gT], writes=[ps_y] if st2 == 0 else [])
            S.op("tensor", lambda t, o=ps_y.ap[:, 0:256], l=hbf.ap, r=ce.ap:
                 t.matmul(o, l, r, start=False, stop=True),
                 reads=[hbf, ce], writes=[ps_y])
            S.op("vector", lambda v, h=h, j=j, kv=kv, o=ysb.ap,
                 xi=xbcp.ap[:, kv * T + j * 256:kv * T + (j + 1) * 256],
                 i=ps_y.ap[:, 0:256]:
                 v.scalar_tensor_tensor(o, xi, Db.ap[:, h:h + 1], i, ALU.mult, ALU.add),
                 reads=[xbcp, Db, ps_y], writes=[ysb])
            if debug:
                S.op("sync", lambda s, h=h, j=j, o=ydbg_o[h * 128:(h + 1) * 128, j * 256:(j + 1) * 256],
                     i=ysb.ap: s.dma_start(out=o, in_=i),
                     reads=[ysb], dma=True)
                if h == 0 and j == 0:
                    S.op("vector", lambda v: v.tensor_copy(scr1.ap, gT.ap[:, 0:256]),
                         reads=[gT], writes=[scr1])
                    S.op("vector", lambda v: v.tensor_copy(scr2.ap, gT.ap[:, 256:512]),
                         reads=[gT], writes=[scr2])
                    S.op("sync", lambda s: s.dma_start(out=gtd_o[:, 0:256], in_=scr1.ap),
                         reads=[scr1], dma=True)
                    S.op("sync", lambda s: s.dma_start(out=gtd_o[:, 256:512], in_=scr2.ap),
                         reads=[scr2], dma=True)
            S.op("scalar", lambda s, h=h, j=j, o=szb.ap,
                 i=z_sb.ap[:, h * T + j * 256:h * T + (j + 1) * 256]:
                 s.activation(o, i, AF.Silu),
                 reads=[z_sb], writes=[szb])
            S.op("vector", lambda v: v.tensor_tensor(gsb.ap, ysb.ap, szb.ap, ALU.mult),
                 reads=[ysb, szb], writes=[gsb])
            S.op("scalar", lambda s: s.activation(sqb.ap, gsb.ap, AF.Square),
                 reads=[gsb], writes=[sqb])
            ps_r = take(5, 6, pap[:, 5 * 512:6 * 512])
            S.op("tensor", lambda t, o=ps_r.ap[0:1, 0:256]: t.matmul(
                 o, onesb.ap, sqb.ap, start=True, stop=True),
                 reads=[onesb, sqb], writes=[ps_r])
            S.op("vector", lambda v: v.tensor_scalar(rrow.ap, ps_r.ap[0:1, 0:256],
                                                     1.0 / 128.0, 1e-5, ALU.mult, ALU.add),
                 reads=[ps_r], writes=[rrow])
            S.op("scalar", lambda s: s.activation(rrow.ap, rrow.ap, AF.Sqrt),
                 reads=[rrow], writes=[rrow])
            S.op("vector", lambda v: v.reciprocal(rrow.ap, rrow.ap),
                 reads=[rrow], writes=[rrow])
            ps_rb = take(5, 6, pap[:, 5 * 512:6 * 512])
            S.op("tensor", lambda t, o=ps_rb.ap[:, 0:256]: t.matmul(
                 o, onesk1.ap, rrow.ap, start=True, stop=True),
                 reads=[onesk1, rrow], writes=[ps_rb])
            S.op("vector", lambda v, h=h, j=j,
                 o=yn.ap[:, h * T + j * 256:h * T + (j + 1) * 256],
                 i=ps_rb.ap[:, 0:256]:
                 v.scalar_tensor_tensor(o, gsb.ap, nw_sb.ap[:, h:h + 1], i, ALU.mult, ALU.mult),
                 reads=[gsb, nw_sb, ps_rb], writes=[yn])
        for h in range(16):
            S.op("vector", lambda v, h=h, j=j, soj=soj:
                 v.scalar_tensor_tensor(H.ap[:, h * 128:(h + 1) * 128],
                                        H.ap[:, h * 128:(h + 1) * 128],
                                        cdb.ap[:, j * 16 + h:j * 16 + h + 1],
                                        soj.ap[:, h * 128:(h + 1) * 128],
                                        ALU.mult, ALU.add),
                 reads=[soj, cdb], writes=[H])

    for blk in range(4):
        w = wT[blk % 2]
        for dc in range(NCH):
            S.op("sync", lambda s, blk=blk, dc=dc, o=w.ap[:, dc * 512:(dc + 1) * 512],
                 i=opw_bf.ap[blk * 512:(blk + 1) * 512, dc * 128:(dc + 1) * 128]:
                 s.dma_start_transpose(o, i),
                 reads=[opw_bf], writes=[w], dma=True)
        for m in range(4):
            oc = blk * 4 + m
            pm = take(m % 2, m % 2 + 1, pap[:, (m % 2) * 512:(m % 2 + 1) * 512])
            for hh in range(2):
                lo, hi_ = hh * 512, (hh + 1) * 512
                e = ev[hh]
                for dc in range(NCH):
                    S.op("tensor", lambda t, pm=pm,
                         l=w.ap[:, dc * 512 + m * 128:dc * 512 + m * 128 + 128],
                         r=yn.ap[:, dc * T + lo:dc * T + hi_],
                         st=(dc == 0), sp=(dc == NCH - 1):
                         t.matmul(pm.ap, l, r, start=st, stop=sp),
                         reads=[w, yn], writes=_aw(dc, NCH, pm))
                S.op("vector", lambda v, e=e, pm=pm: v.tensor_copy(e.ap, pm.ap),
                     reads=[pm], writes=[e])
                S.op("scalar", lambda s, e=e, o=fm_chunk(a1_o, oc)[:, lo:hi_], i=e.ap:
                     s.dma_start(out=o, in_=i),
                     reads=[e], dma=True)

    S.fence_all("sync")
    S.build()
    return nc




def build_l1moe():
    nc = bass.Bass("TRN2", target_bir_lowering=False, debug=False)
    xnew_d = nc.dram_tensor("xnew", [D, T], F32, kind="ExternalInput").ap()
    xn2_d = nc.dram_tensor("xn2", [D, T], F32, kind="ExternalInput").ap()
    a1_d = nc.dram_tensor("attn1", [D, T], F32, kind="ExternalInput").ap()
    sw1_d = nc.dram_tensor("sw1", [2 * D, 1408], F32, kind="ExternalInput").ap()
    sb1_d = nc.dram_tensor("sb1", [128, 2 * 11], F32, kind="ExternalInput").ap()
    sw2_d = nc.dram_tensor("sw2", [2 * 1408, D], F32, kind="ExternalInput").ap()
    sb2_d = nc.dram_tensor("sb2", [128, 2 * 16], F32, kind="ExternalInput").ap()
    rw1_d = nc.dram_tensor("rw1", [16 * D, 1408], F32, kind="ExternalInput").ap()
    rb1_d = nc.dram_tensor("rb1", [128, 16 * 11], F32, kind="ExternalInput").ap()
    rw2_d = nc.dram_tensor("rw2", [16 * 1408, D], F32, kind="ExternalInput").ap()
    rb2_d = nc.dram_tensor("rb2", [16, D], F32, kind="ExternalInput").ap()
    cent_d = nc.dram_tensor("cent", [16, D], F32, kind="ExternalInput").ap()
    cbias_d = nc.dram_tensor("cbias", [1, 16], F32, kind="ExternalInput").ap()
    fin_o = nc.dram_tensor("fin", [D, T], F32, kind="ExternalOutput").ap()

    bd = np.zeros((16, 16 * 128), dtype=np.float32)
    for h in range(16):
        bd[h, h * 128:(h + 1) * 128] = 1.0
    BD = nc.inline_tensor(bd, name="BD").ap()
    ident_f = nc.inline_tensor(np.eye(128, dtype=np.float32), name="ident_f").ap()
    ones_row_f = nc.inline_tensor(np.ones((1, 128), dtype=np.float32), name="ones_row_f").ap()

    S = Sched(nc)
    xn2b = sb(nc, "xn2b", [128, NCH * T], BF16)
    acc = sb(nc, "acc", [128, NCH * T], F32)
    h_sb = sb(nc, "h_sb", [128, 11 * T], BF16)
    w1b = [sb(nc, "w1b0", [128, NCH * 128], BF16), sb(nc, "w1b1", [128, NCH * 128], BF16)]
    w2b = [sb(nc, "w2b0", [128, 11 * 128], BF16), sb(nc, "w2b1", [128, 11 * 128], BF16)]
    gb = sb(nc, "gb", [128, T], F32)
    xcf = [sb(nc, "xcf0", [128, T], F32), sb(nc, "xcf1", [128, T], F32)]
    centT = sb(nc, "centT", [128, 16 * 16], F32)
    cent_sb = sb(nc, "cent_sb", [16, D], F32)
    sc_t = sb(nc, "sc_t", [128, 16], F32)
    eqv = sb(nc, "eqv", [128, 16], F32)
    gv = sb(nc, "gv", [128, 16], F32)
    mrow = sb(nc, "mrow", [128, 4], F32)
    biasb = sb(nc, "biasb", [128, 16], F32)
    b2s = sb(nc, "b2s", [128, 16], F32)
    sb1c = sb(nc, "sb1c", [128, 2 * 11], F32)
    sb2c = sb(nc, "sb2c", [128, 2 * 16], F32)
    rb1c = sb(nc, "rb1c", [128, 16 * 11], F32)
    rb2b = sb(nc, "rb2b", [16, D], BF16)
    g_fm = sb(nc, "g_fm", [16, 8 * 128], F32)
    g_fmb = sb(nc, "g_fmb", [16, 8 * 128], BF16)
    cb_row = sb(nc, "cb_row", [1, 16], F32)
    sumb = [sb(nc, "sumb0", [128, 3 * 512], F32), sb(nc, "sumb1", [128, 3 * 512], F32)]
    evx = [sb(nc, "evx0", [128, 512], F32), sb(nc, "evx1", [128, 512], F32)]
    bd_sb = sb(nc, "bd_sb", [16, 16 * 128], F32)
    onesk1 = sb(nc, "onesk1", [1, 128], F32)
    idf = sb(nc, "idf", [128, 128], F32)

    pool = nc.alloc_psum_tensor("pool", [128, 4096], F32)
    pap = pool.ap()
    bank_owner = {}

    def take(lo, hi, view):
        olds = [bank_owner[i] for i in range(lo, hi) if i in bank_owner]
        b = alias_over(view, *olds)
        for i in range(lo, hi):
            bank_owner[i] = b
        return b

    for dst, srcap in ((sb1c, sb1_d), (sb2c, sb2_d), (rb1c, rb1_d),
                       (cb_row, cbias_d), (bd_sb, BD), (onesk1, ones_row_f), (idf, ident_f)):
        S.op("sync", lambda s, o=dst.ap, i=srcap: s.dma_start(out=o, in_=i),
             writes=[dst], dma=True)
    S.op("gpsimd", lambda g: g.dma_start(out=rb2b.ap, in_=rb2_d),
         writes=[rb2b], dma=True)
    S.op("sync", lambda s: s.dma_start(out=cent_sb.ap, in_=cent_d),
         writes=[cent_sb], dma=True)
    ps_ct = take(6, 7, pap[:, 6 * 512:7 * 512])
    for dc in range(NCH):
        S.op("tensor", lambda t, dc=dc, o=ps_ct.ap[:, 0:16],
             i=cent_sb.ap[0:16, dc * 128:(dc + 1) * 128]:
             t.transpose(o, i, idf.ap[0:16, 0:16]),
             reads=[cent_sb, idf], writes=[ps_ct])
        S.op("vector", lambda v, dc=dc, o=centT.ap[:, dc * 16:(dc + 1) * 16],
             i=ps_ct.ap[:, 0:16]: v.tensor_copy(o, i),
             reads=[ps_ct], writes=[centT])
    ps_sc = take(5, 6, pap[:, 5 * 512:6 * 512])
    for c in range(NCH):
        xf = xcf[c % 2]
        S.op("sync", lambda s, o=xf.ap, i=fm_chunk(xn2_d, c): s.dma_start(out=o, in_=i),
             writes=[xf], dma=True)
        S.op("vector", lambda v, c=c, xf=xf: v.tensor_copy(xn2b.ap[:, c * T:(c + 1) * T], xf.ap),
             reads=[xf], writes=[xn2b])
        for tt in range(8):
            S.op("tensor", lambda t, c=c, tt=tt, xf=xf, o=ps_sc.ap[:, tt * 16:(tt + 1) * 16],
                 l=xf.ap[:, tt * 128:(tt + 1) * 128],
                 r=centT.ap[:, c * 16:(c + 1) * 16],
                 st=(c == 0 and tt == 0), sp=(c == NCH - 1 and tt == 7):
                 t.matmul(o, l, r, start=st, stop=sp),
                 reads=[xf, centT],
                 writes=[ps_sc] if (c == 0 and tt == 0) or (c == NCH - 1 and tt == 7) else [])
    S.op("vector", lambda v: v.tensor_tensor(b2s.ap, sb2c.ap[:, 0:16], sb2c.ap[:, 16:32], ALU.add),
         reads=[sb2c], writes=[b2s])
    ps_b = take(4, 5, pap[:, 4 * 512:5 * 512])
    S.op("tensor", lambda t: t.matmul(ps_b.ap[:, 0:16], onesk1.ap, cb_row.ap, start=True, stop=True),
         reads=[onesk1, cb_row], writes=[ps_b])
    S.op("vector", lambda v: v.tensor_copy(biasb.ap, ps_b.ap[:, 0:16]),
         reads=[ps_b], writes=[biasb])

    for tt in range(8):
        S.op("scalar", lambda s, tt=tt: s.activation(sc_t.ap, ps_sc.ap[:, tt * 16:(tt + 1) * 16], AF.Sigmoid),
             reads=[ps_sc], writes=[sc_t])
        S.op("vector", lambda v: v.tensor_tensor(sc_t.ap, sc_t.ap, biasb.ap, ALU.add),
             reads=[biasb], writes=[sc_t])
        S.op("vector", lambda v: v.memset(gv.ap, 0.0), writes=[gv])
        S.op("vector", lambda v: v.memset(mrow.ap[:, 1:2], 0.0), writes=[mrow])
        for r in range(4):
            S.op("vector", lambda v: v.tensor_reduce(mrow.ap[:, 0:1], sc_t.ap, AX, ALU.max),
                 reads=[sc_t], writes=[mrow])
            S.op("vector", lambda v: v.tensor_scalar(eqv.ap, sc_t.ap, mrow.ap[:, 0:1], None, ALU.subtract),
                 reads=[sc_t, mrow], writes=[eqv])
            S.op("vector", lambda v: v.tensor_scalar(eqv.ap, eqv.ap, 1e12, 1.0, ALU.mult, ALU.add),
                 reads=[], writes=[eqv])
            S.op("vector", lambda v: v.tensor_scalar(eqv.ap, eqv.ap, 0.0, None, ALU.max),
                 reads=[], writes=[eqv])
            S.op("vector", lambda v: v.scalar_tensor_tensor(gv.ap, eqv.ap, mrow.ap[:, 0:1],
                                                            gv.ap, ALU.mult, ALU.add),
                 reads=[eqv, mrow], writes=[gv])
            S.op("vector", lambda v: v.scalar_tensor_tensor(sc_t.ap, eqv.ap, -1e4,
                                                            sc_t.ap, ALU.mult, ALU.add),
                 reads=[eqv], writes=[sc_t])
            S.op("vector", lambda v: v.tensor_tensor(mrow.ap[:, 1:2], mrow.ap[:, 1:2],
                                                     mrow.ap[:, 0:1], ALU.add),
                 reads=[], writes=[mrow])
        S.op("vector", lambda v: v.tensor_scalar(mrow.ap[:, 2:3], mrow.ap[:, 1:2], 1e-8, None, ALU.max),
             reads=[], writes=[mrow])
        S.op("vector", lambda v: v.reciprocal(mrow.ap[:, 3:4], mrow.ap[:, 2:3]),
             reads=[], writes=[mrow])
        S.op("vector", lambda v: v.tensor_scalar(gv.ap, gv.ap, mrow.ap[:, 3:4], None, ALU.mult),
             reads=[mrow], writes=[gv])
        ps_t = take(6, 7, pap[:, 6 * 512:7 * 512])
        S.op("tensor", lambda t, o=ps_t.ap[0:16, 0:128]: t.transpose(o, gv.ap, idf.ap),
             reads=[gv, idf], writes=[ps_t])
        S.op("vector", lambda v, tt=tt: v.tensor_copy(g_fm.ap[0:16, tt * 128:(tt + 1) * 128],
                                                      ps_t.ap[0:16, 0:128]),
             reads=[ps_t], writes=[g_fm])
    S.op("vector", lambda v: v.tensor_copy(g_fmb.ap, g_fm.ap),
         reads=[g_fm], writes=[g_fmb])

    def expert(eidx, w1ap, w2ap, b1col_off, b1buf, gated, j, first):
        for m in range(11):
            w = w1b[m % 2]
            wsrc = w1ap[:, m * 128:(m + 1) * 128].rearrange("(c p) n -> p c n", p=128)
            S.op("gpsimd", lambda g, o=sb3(w.ap, 128), i=wsrc: g.dma_start(out=o, in_=i),
                 writes=[w], dma=True)
            pm = take(0 + m % 2, 1 + m % 2, pap[:, (m % 2) * 512:(m % 2 + 1) * 512])
            for hh in range(2):
                lo, hi_ = hh * 512, (hh + 1) * 512
                for dc in range(NCH):
                    S.op("tensor", lambda t, pm=pm, w=w,
                         l=w.ap[:, dc * 128:(dc + 1) * 128],
                         r=xn2b.ap[:, dc * T + lo:dc * T + hi_],
                         st=(dc == 0), sp=(dc == NCH - 1):
                         t.matmul(pm.ap, l, r, start=st, stop=sp),
                         reads=[w, xn2b], writes=_aw(dc, NCH, pm))
                S.op("scalar", lambda s, pm=pm, m=m, lo=lo, hi_=hi_,
                     o=h_sb.ap[:, m * T + lo:m * T + hi_],
                     bc=b1buf.ap[:, b1col_off + m:b1col_off + m + 1]:
                     s.activation(o, pm.ap, AF.Relu, bias=bc),
                     reads=[pm, b1buf], writes=[h_sb])
        if gated:
            ps_g = take(2, 3, pap[:, 2 * 512:3 * 512])
            for hh in range(2):
                lo, hi_ = hh * 512, (hh + 1) * 512
                S.op("tensor", lambda t, j=j, lo=lo, hi_=hi_, o=ps_g.ap,
                     l=bd_sb.ap[0:16, j * 128:(j + 1) * 128], r=g_fm.ap[0:16, lo:hi_]:
                     t.matmul(o, l, r, start=True, stop=True),
                     reads=[bd_sb, g_fm], writes=[ps_g])
                S.op("vector", lambda v, lo=lo, hi_=hi_, o=gb.ap[:, lo:hi_], i=ps_g.ap:
                     v.tensor_copy(o, i),
                     reads=[ps_g], writes=[gb])
            for m in range(11):
                S.op("vector", lambda v, m=m, o=h_sb.ap[:, m * T:(m + 1) * T]:
                     v.tensor_tensor(o, o, gb.ap, ALU.mult),
                     reads=[gb], writes=[h_sb])
        for mo in range(NCH):
            w = w2b[mo % 2]
            wsrc = w2ap[:, mo * 128:(mo + 1) * 128].rearrange("(c p) n -> p c n", p=128)
            S.op("gpsimd", lambda g, o=w.ap.rearrange("p (c n) -> p c n", n=128), i=wsrc:
                 g.dma_start(out=o, in_=i),
                 writes=[w], dma=True)
            pm = take(0 + mo % 2, 1 + mo % 2, pap[:, (mo % 2) * 512:(mo % 2 + 1) * 512])
            for hh in range(2):
                lo, hi_ = hh * 512, (hh + 1) * 512
                for dc in range(11):
                    S.op("tensor", lambda t, pm=pm, w=w,
                         l=w.ap[:, dc * 128:(dc + 1) * 128],
                         r=h_sb.ap[:, dc * T + lo:dc * T + hi_],
                         st=(dc == 0), sp=(dc == 10):
                         t.matmul(pm.ap, l, r, start=st, stop=sp),
                         reads=[w, h_sb], writes=_aw(dc, 11, pm))
                if first:
                    S.op("vector", lambda v, pm=pm, mo=mo, lo=lo, hi_=hi_,
                         o=acc.ap[:, mo * T + lo:mo * T + hi_],
                         bc=b2s.ap[:, mo:mo + 1]:
                         v.tensor_scalar(o, pm.ap, bc, None, ALU.add),
                         reads=[pm, b2s], writes=[acc])
                else:
                    S.op("vector", lambda v, pm=pm, mo=mo, lo=lo, hi_=hi_,
                         o=acc.ap[:, mo * T + lo:mo * T + hi_]:
                         v.tensor_tensor(o, o, pm.ap, ALU.add),
                         reads=[pm], writes=[acc])

    expert(0, sw1_d[0:D, :], sw2_d[0:1408, :], 0, sb1c, False, 0, True)
    expert(1, sw1_d[D:2 * D, :], sw2_d[1408:2 * 1408, :], 11, sb1c, False, 0, False)
    for j in range(16):
        expert(2 + j, rw1_d[j * D:(j + 1) * D, :], rw2_d[j * 1408:(j + 1) * 1408, :],
               j * 11, rb1c, True, j, False)

    for mo in range(NCH):
        pm = take(0 + mo % 2, 1 + mo % 2, pap[:, (mo % 2) * 512:(mo % 2 + 1) * 512])
        for hh in range(2):
            lo, hi_ = hh * 512, (hh + 1) * 512
            S.op("tensor", lambda t, pm=pm, mo=mo, lo=lo, hi_=hi_,
                 l=rb2b.ap[0:16, mo * 128:(mo + 1) * 128], r=g_fmb.ap[0:16, lo:hi_]:
                 t.matmul(pm.ap, l, r, start=True, stop=True),
                 reads=[rb2b, g_fmb], writes=[pm])
            S.op("vector", lambda v, pm=pm, mo=mo, lo=lo, hi_=hi_,
                 o=acc.ap[:, mo * T + lo:mo * T + hi_]:
                 v.tensor_tensor(o, o, pm.ap, ALU.add),
                 reads=[pm], writes=[acc])

    for c in range(NCH):
        for hh in range(2):
            lo, hi_ = hh * 512, (hh + 1) * 512
            sm = sumb[(c * 2 + hh) % 2]
            e = evx[(c * 2 + hh) % 2]
            S.op("sync", lambda s, o=sm.ap[:, 0:512], i=fm_chunk(xnew_d, c)[:, lo:hi_]:
                 s.dma_start(out=o, in_=i), writes=[sm], dma=True)
            S.op("sync", lambda s, o=sm.ap[:, 512:1024], i=fm_chunk(xn2_d, c)[:, lo:hi_]:
                 s.dma_start(out=o, in_=i), writes=[sm], dma=True)
            S.op("sync", lambda s, o=sm.ap[:, 1024:1536], i=fm_chunk(a1_d, c)[:, lo:hi_]:
                 s.dma_start(out=o, in_=i), writes=[sm], dma=True)
            S.op("vector", lambda v, sm=sm, e=e: v.tensor_tensor(
                 e.ap, sm.ap[:, 0:512], sm.ap[:, 512:1024], ALU.add),
                 reads=[sm], writes=[e])
            S.op("vector", lambda v, sm=sm, e=e: v.tensor_tensor(
                 e.ap, e.ap, sm.ap[:, 1024:1536], ALU.add),
                 reads=[sm], writes=[e])
            S.op("vector", lambda v, c=c, lo=lo, hi_=hi_, e=e: v.tensor_tensor(
                 e.ap, e.ap, acc.ap[:, c * T + lo:c * T + hi_], ALU.add),
                 reads=[acc], writes=[e])
            S.op("scalar", lambda s, c=c, lo=lo, hi_=hi_, e=e,
                 o=fm_chunk(fin_o, c)[:, lo:hi_]:
                 s.dma_start(out=o, in_=e.ap),
                 reads=[e], dma=True)

    S.fence_all("sync")
    S.build()
    return nc


_NC_CACHE = {}


def _get_nc(name, builder):
    if name not in _NC_CACHE:
        _NC_CACHE[name] = builder()
    return _NC_CACHE[name]


def kernel(**d):
    cores = list(range(NCORE))
    x2d = d["L0_attn_x"][0]

    nc1 = _get_nc("l0qkv", build_l0qkv)
    im = [{"x": fm_shard(x2d, i), "qkvw": d["L0_attn_qkv"],
           "gamma": col128(d["L0_attn_gamma"]), "beta": col128(d["L0_attn_beta"])}
          for i in cores]
    r1 = run_bass_kernel_spmd(nc1, im, core_ids=cores).results

    nc2 = _get_nc("l0ffn", build_l0ffn)
    im = [{"xn": r1[i]["xn"], "wfc": d["L0_ffn_w_fc"], "wpr": d["L0_ffn_w_proj"]}
          for i in cores]
    r2 = run_bass_kernel_spmd(nc2, im, core_ids=cores).results

    nc3 = _get_nc("l0attn", build_l0attn)
    kf = np.ascontiguousarray(np.concatenate([r1[i]["k"] for i in cores], axis=1))
    vf = np.ascontiguousarray(np.concatenate([r1[i]["v"].T for i in cores], axis=0))
    im = []
    for i in cores:
        qidx = (np.arange(128)[:, None] + i * T +
                (np.arange(8)[None, :] * 128)).astype(np.float32)
        im.append({"q": r1[i]["q"], "kf": kf, "vf": vf, "x": fm_shard(x2d, i),
                   "xn": r1[i]["xn"], "ffn": r2[i]["ffn"],
                   "wo": d["L0_attn_o_proj"], "qidx": qidx})
    r3 = run_bass_kernel_spmd(nc3, im, core_ids=cores).results

    nc4 = _get_nc("l1front", build_l1front)
    im = [{"x": r3[i]["xnew"], "ipw": d["L1_attn_in_proj_w"],
           "dtb": np.ascontiguousarray(d["L1_attn_dt_bias"].reshape(16, 1))}
          for i in cores]
    r4 = run_bass_kernel_spmd(nc4, im, core_ids=cores).results

    nc5 = _get_nc("l1conv", build_l1conv)
    cw = np.ascontiguousarray(d["L1_attn_conv_w"].reshape(24, 128, 4).transpose(1, 0, 2).reshape(128, 96))
    cb = np.ascontiguousarray(d["L1_attn_conv_b"].reshape(24, 128).T)
    alog = np.ascontiguousarray(d["L1_attn_A_log"].reshape(16, 1))
    im = []
    for i in cores:
        halo = (np.zeros((3072, 3), np.float32) if i == 0
                else np.ascontiguousarray(r4[i - 1]["xbc"][:, T - 3:T]))
        im.append({"xbc": r4[i]["xbc"], "halo": halo, "cw": cw, "cb": cb,
                   "dts": r4[i]["dts"], "alog": alog})
    r5 = run_bass_kernel_spmd(nc5, im, core_ids=cores).results

    nc6 = _get_nc("l1ssd", build_l1ssd)
    s_full = np.concatenate([r5[i]["s"] for i in cores], axis=1)
    dec_all = np.stack([r5[i]["dec"][0] for i in cores], axis=0)
    im = []
    for i in cores:
        sin = s_full.copy()
        sin[:, i * 4 * 2048:] = 0
        din = np.ones((1, 512), np.float32)
        din[0, :i * 4 * 16] = dec_all[:i].reshape(-1)[:i * 4 * 16]
        im.append({"xbcp": r5[i]["xbcp"], "dts": r4[i]["dts"], "z": r4[i]["z"],
                   "sin": sin, "din": din, "sown": r5[i]["s"], "dec": r5[i]["dec"],
                   "alog": alog,
                   "dD": np.ascontiguousarray(d["L1_attn_D"].reshape(1, 16)),
                   "nw": col128(d["L1_attn_norm_w"]),
                   "opw": d["L1_attn_out_proj_w"]})
    r6 = run_bass_kernel_spmd(nc6, im, core_ids=cores).results

    nc7 = _get_nc("l1moe", build_l1moe)
    moe_w = {
        "sw1": np.ascontiguousarray(d["L1_ffn_shared_w1"].reshape(2 * D, 1408)),
        "sb1": np.ascontiguousarray(d["L1_ffn_shared_b1"].reshape(2, 11, 128).transpose(2, 0, 1).reshape(128, 22)),
        "sw2": np.ascontiguousarray(d["L1_ffn_shared_w2"].reshape(2 * 1408, D)),
        "sb2": np.ascontiguousarray(d["L1_ffn_shared_b2"].reshape(2, 16, 128).transpose(2, 0, 1).reshape(128, 32)),
        "rw1": np.ascontiguousarray(d["L1_ffn_routed_w1"].reshape(16 * D, 1408)),
        "rb1": np.ascontiguousarray(d["L1_ffn_routed_b1"].reshape(16, 11, 128).transpose(2, 0, 1).reshape(128, 176)),
        "rw2": np.ascontiguousarray(d["L1_ffn_routed_w2"].reshape(16 * 1408, D)),
        "rb2": d["L1_ffn_routed_b2"],
        "cent": d["L1_ffn_centroids"],
        "cbias": np.ascontiguousarray(d["L1_ffn_bias"].reshape(1, 16)),
    }
    im = [dict(moe_w, xnew=r3[i]["xnew"], xn2=r4[i]["xn2"], attn1=r6[i]["attn1"])
          for i in cores]
    r7 = run_bass_kernel_spmd(nc7, im, core_ids=cores).results

    out = np.concatenate([r7[i]["fin"].T for i in cores], axis=0)
    return np.ascontiguousarray(out.reshape(1, 8 * T, D), dtype=np.float32)


# version 05 (final)
# speedup vs baseline: 1.1389x; 1.1353x over previous
import numpy as np
import ml_dtypes
import concourse.bass as bass
import concourse.mybir as mybir
from concourse.bass_utils import run_bass_kernel_spmd

F32 = mybir.dt.float32
BF16 = mybir.dt.bfloat16
AF = mybir.ActivationFunctionType
ALU = mybir.AluOpType
AX = mybir.AxisListType.X
BF = ml_dtypes.bfloat16

NCORE = 8
T = 1024
D = 2048
NCH = D // 128

ENGS = ("sync", "tensor", "vector", "scalar", "gpsimd")


class Buf:
    __slots__ = ("ap", "w", "r")

    def __init__(self, ap):
        self.ap = ap
        self.w = None
        self.r = []


class Sched:
    NSLOT = 8

    def __init__(self, nc, serial=False):
        self.nc = nc
        self.serial = serial
        self.items = {e: [] for e in ENGS}
        self.val = {e: 0 for e in ENGS}
        self.duse = {e: [0] * self.NSLOT for e in ENGS}
        self.dnext = {e: 0 for e in ENGS}

    def op(self, eng, build, reads=(), writes=(), extra=(), dma=False):
        raw = []
        for b in reads:
            if b.w:
                raw.append(b.w)
        for b in writes:
            if b.w:
                raw.append(b.w)
            raw.extend(b.r)
        raw.extend(extra)
        deps = list(raw)
        if self.serial:
            deps = [(e2, v) for e2, v in self.val.items() if v > 0]
            deps += [("D", e2, s, 16 * u) for e2 in ENGS
                     for s, u in enumerate(self.duse[e2]) if u > 0]
        if dma:
            s = self.dnext[eng]
            self.dnext[eng] = (s + 1) % self.NSLOT
            u = self.duse[eng][s] + 1
            if u >= 2:
                deps.append(("D", eng, s, 16 * (u - 1)))
            self.duse[eng][s] = u
            tok = ("D", eng, s, 16 * u)
            self.items[eng].append((tuple(deps), build, ("D", eng, s)))
        else:
            self.val[eng] += 1
            tok = (eng, self.val[eng])
            self.items[eng].append((tuple(deps), build, eng))
        for b in reads:
            b.r.append(tok)
        for b in writes:
            b.w = tok
            b.r = []
        return tok

    def fence_all(self, eng="sync"):
        extra = [(e, self.val[e]) for e in ENGS if e != eng and self.val[e] > 0]
        extra += [("D", e, s, 16 * u) for e in ENGS
                  for s, u in enumerate(self.duse[e]) if u > 0]
        self.op(eng, lambda s: s.nop(), extra=extra)

    def build(self):
        nc = self.nc
        sems = {e: nc.alloc_semaphore(f"sched_{e}") for e in ENGS}
        dsems = {e: [nc.alloc_semaphore(f"dma_{e}_{s}") for s in range(self.NSLOT)]
                 for e in ENGS if any(self.duse[e])}
        with nc.Block() as block:
            for e in ENGS:
                items = self.items[e]

                def body(engobj, items=items, myname=e):
                    hi = {}
                    done = 0
                    for deps, build, inckey in items:
                        for t in deps:
                            if t[0] == "D":
                                _, e2, s2, v2 = t
                                key = ("D", e2, s2)
                                if hi.get(key, 0) < v2:
                                    engobj.wait_ge(dsems[e2][s2], v2)
                                    hi[key] = v2
                            else:
                                e2, v2 = t
                                if e2 == myname:
                                    if myname == "tensor" or v2 <= done:
                                        continue
                                    engobj.wait_ge(sems[e2], v2)
                                    done = v2
                                    continue
                                if hi.get(e2, 0) < v2:
                                    engobj.wait_ge(sems[e2], v2)
                                    hi[e2] = v2
                        ins = build(engobj)
                        if inckey[0] == "D":
                            ins.then_inc(dsems[inckey[1]][inckey[2]], 16)
                        else:
                            ins.then_inc(sems[inckey], 1)
                getattr(block, e)(body)



def fm_shard(arr2d, core):
    return np.ascontiguousarray(arr2d[core * T:(core + 1) * T, :].T)


def col128(vec):
    return np.ascontiguousarray(vec.reshape(-1, 128).T)


def fm_full(dram_ap):
    return dram_ap.rearrange("(c p) t -> p c t", p=128)


def sb3(sb_ap, tok):
    return sb_ap.rearrange("p (c t) -> p c t", t=tok)


def fm_chunk(dram_ap, c):
    return dram_ap[c * 128:(c + 1) * 128, :]



def _aw(c, n, buf):
    return [buf] if (c == 0 or c == n - 1) else []


def ln_stats(S, x_sb, xsq_scr, ps_sum, ps_sumsq, ones_bf, nch, tok, cast_scr):
    nhalf = (tok + 511) // 512
    for c in range(nch):
        cb = cast_scr[c % 2]
        sq = xsq_scr[c % 2]
        xs = x_sb.ap[:, c * tok:(c + 1) * tok]
        S.op("vector", lambda v, o=cb.ap, i=xs: v.tensor_copy(o, i),
             reads=[x_sb], writes=[cb])
        S.op("scalar", lambda s, o=sq.ap, i=xs: s.activation(o, i, AF.Square),
             reads=[x_sb], writes=[sq])
        for h in range(nhalf):
            lo, hi_ = h * 512, min((h + 1) * 512, tok)
            S.op("tensor", lambda t, o=ps_sum.ap[0:1, lo:hi_], l=ones_bf.ap,
                 r=cb.ap[:, lo:hi_], st=(c == 0), sp=(c == nch - 1):
                 t.matmul(o, l, r, start=st, stop=sp),
                 reads=[cb, ones_bf], writes=_aw(c, nch, ps_sum))
            S.op("tensor", lambda t, o=ps_sumsq.ap[0:1, lo:hi_], l=ones_bf.ap,
                 r=sq.ap[:, lo:hi_], st=(c == 0), sp=(c == nch - 1):
                 t.matmul(o, l, r, start=st, stop=sp),
                 reads=[sq, ones_bf], writes=_aw(c, nch, ps_sumsq))


def bcast_row(S, row_buf, row_slice, out_sb, onesk1, ps_bc, tok):
    nhalf = (tok + 511) // 512
    for h in range(nhalf):
        lo, hi_ = h * 512, min((h + 1) * 512, tok)
        S.op("tensor", lambda t, o=ps_bc.ap[:, 0:hi_ - lo], l=onesk1.ap,
             r=row_slice[0:1, lo:hi_]:
             t.matmul(o, l, r, start=True, stop=True),
             reads=[row_buf, onesk1], writes=[ps_bc])
        S.op("vector", lambda v, o=out_sb.ap[:, lo:hi_], i=ps_bc.ap[:, 0:hi_ - lo]:
             v.tensor_copy(o, i),
             reads=[ps_bc], writes=[out_sb])



def sb(nc, name, shape, dtype):
    return Buf(nc.alloc_sbuf_tensor(name, shape, dtype).ap())


def psb(nc, name, shape, dtype=F32):
    return Buf(nc.alloc_psum_tensor(name, shape, dtype).ap())



def alias_over(ap, *olds):
    b = Buf(ap)
    rs = []
    for o in olds:
        if o.w:
            rs.append(o.w)
        rs.extend(o.r)
    b.r = rs
    return b



def build_l0qkv(debug=False, serial=False):
    nc = bass.Bass("TRN2", target_bir_lowering=False, debug=False)
    x_d = nc.dram_tensor("x", [D, T], F32, kind="ExternalInput").ap()
    qkvw_d = nc.dram_tensor("qkvw", [D, 3 * D], F32, kind="ExternalInput").ap()
    gamma_d = nc.dram_tensor("gamma", [128, NCH], F32, kind="ExternalInput").ap()
    beta_d = nc.dram_tensor("beta", [128, NCH], F32, kind="ExternalInput").ap()
    xn_o = nc.dram_tensor("xn", [D, T], F32, kind="ExternalOutput").ap()
    q_o = nc.dram_tensor("q", [D, T], BF16, kind="ExternalOutput").ap()
    k_o = nc.dram_tensor("k", [D, T], BF16, kind="ExternalOutput").ap()
    v_o = nc.dram_tensor("v", [D, T], BF16, kind="ExternalOutput").ap()

    if debug:
        dbg_rows = nc.dram_tensor("dbg_rows", [1, 16 * T], F32, kind="ExternalOutput").ap()
        dbg_bc = nc.dram_tensor("dbg_bc", [128, 2 * T], F32, kind="ExternalOutput").ap()
    ones_col_bf = nc.inline_tensor(np.ones((128, 1), dtype=BF), name="ones_col_bf").ap()
    ones_row_f = nc.inline_tensor(np.ones((1, 128), dtype=np.float32), name="ones_row_f").ap()

    S = Sched(nc, serial=serial)
    x_sb = sb(nc, "x_sb", [128, NCH * T], F32)
    xnorm = sb(nc, "xnorm_bf", [128, NCH * T], BF16)
    scr = [sb(nc, "scr_a", [128, T], BF16), sb(nc, "scr_b", [128, T], BF16)]
    sqr = [sb(nc, "scr_c", [128, T], BF16), sb(nc, "scr_d", [128, T], BF16)]
    xnc = [sb(nc, "xn_chunk0", [128, T], F32), sb(nc, "xn_chunk1", [128, T], F32)]
    rows = sb(nc, "rows", [1, 16 * T], F32)
    bc0 = sb(nc, "bc0", [128, T], F32)
    bc1 = sb(nc, "bc1", [128, T], F32)
    onesk1 = sb(nc, "onesk1", [1, 128], F32)
    onesb = sb(nc, "onesb", [128, 1], BF16)
    gb = sb(nc, "gb_sb", [128, 2 * NCH], F32)
    wb = [sb(nc, "w0", [128, NCH * 128], BF16), sb(nc, "w1", [128, NCH * 128], BF16)]
    ev = [sb(nc, "ev0", [128, T], BF16), sb(nc, "ev1", [128, T], BF16)]
    ps_sum = psb(nc, "ps_sum", [1, T])
    ps_sumsq = psb(nc, "ps_sumsq", [1, T])
    ps_bc = psb(nc, "ps_bc", [128, 512])
    ps_mm = [psb(nc, "ps_mm0", [128, 512]), psb(nc, "ps_mm1", [128, 512])]
    if True:
        ones_col_bf_b = Buf(ones_col_bf)
        ones_row_f_b = Buf(ones_row_f)

        S.op("sync", lambda s: s.dma_start(out=sb3(x_sb.ap, T), in_=fm_full(x_d)),
             writes=[x_sb], dma=True)
        S.op("sync", lambda s: s.dma_start(out=gb.ap[:, 0:NCH], in_=gamma_d),
             writes=[gb], dma=True)
        S.op("sync", lambda s: s.dma_start(out=gb.ap[:, NCH:2 * NCH], in_=beta_d),
             writes=[gb], dma=True)
        S.op("sync", lambda s: s.dma_start(out=onesb.ap, in_=ones_col_bf),
             reads=[ones_col_bf_b], writes=[onesb], dma=True)
        S.op("sync", lambda s: s.dma_start(out=onesk1.ap, in_=ones_row_f),
             reads=[ones_row_f_b], writes=[onesk1], dma=True)

        ln_stats(S, x_sb, sqr, ps_sum, ps_sumsq, onesb, NCH, T, scr)

        RW = rows.ap
        r_mu1 = RW[0:1, 0:T]
        r_ex2 = RW[0:1, T:2 * T]
        r_var1 = RW[0:1, 2 * T:3 * T]
        r_rstd1 = RW[0:1, 3 * T:4 * T]
        r_t0 = RW[0:1, 4 * T:5 * T]
        r_mu2 = RW[0:1, 5 * T:6 * T]
        r_ex22 = RW[0:1, 6 * T:7 * T]
        r_var2 = RW[0:1, 7 * T:8 * T]
        r_rstd2 = RW[0:1, 8 * T:9 * T]
        r_c1 = RW[0:1, 9 * T:10 * T]
        r_c2 = RW[0:1, 10 * T:11 * T]
        r_m1r1 = RW[0:1, 11 * T:12 * T]

        S.op("vector", lambda v: v.tensor_scalar(r_mu1, ps_sum.ap, 1.0 / D, None, ALU.mult),
             reads=[ps_sum], writes=[rows])
        S.op("vector", lambda v: v.tensor_scalar(r_ex2, ps_sumsq.ap, 1.0 / D, None, ALU.mult),
             reads=[ps_sumsq], writes=[rows])
        S.op("vector", lambda v: v.tensor_tensor(r_t0, r_mu1, r_mu1, ALU.mult),
             reads=[rows], writes=[rows])
        S.op("vector", lambda v: v.tensor_tensor(r_var1, r_ex2, r_t0, ALU.subtract),
             reads=[rows], writes=[rows])
        S.op("vector", lambda v: v.tensor_scalar(r_var1, r_var1, 1e-5, None, ALU.add),
             reads=[rows], writes=[rows])
        S.op("scalar", lambda s: s.activation(r_t0, r_var1, AF.Sqrt),
             reads=[rows], writes=[rows])
        S.op("vector", lambda v: v.reciprocal(r_rstd1, r_t0),
             reads=[rows], writes=[rows])
        S.op("vector", lambda v: v.tensor_tensor(r_m1r1, r_mu1, r_rstd1, ALU.mult),
             reads=[rows], writes=[rows])

        bcast_row(S, rows, r_rstd1, bc0, onesk1, ps_bc, T)
        bcast_row(S, rows, r_m1r1, bc1, onesk1, ps_bc, T)

        if debug:
            S.op("sync", lambda s: s.dma_start(out=dbg_bc[:, 0:T], in_=bc0.ap),
                 reads=[bc0], dma=True)
            S.op("sync", lambda s: s.dma_start(out=dbg_bc[:, T:2 * T], in_=bc1.ap),
                 reads=[bc1], dma=True)

        for c in range(NCH):
            xb = xnc[c % 2]
            xs = x_sb.ap[:, c * T:(c + 1) * T]
            S.op("vector", lambda v, o=xb.ap, i=xs: v.tensor_tensor(o, i, bc0.ap, ALU.mult),
                 reads=[x_sb, bc0], writes=[xb])
            S.op("vector", lambda v, o=xb.ap: v.tensor_tensor(o, o, bc1.ap, ALU.subtract),
                 reads=[bc1], writes=[xb])
            S.op("sync", lambda s, o=fm_chunk(xn_o, c), i=xb.ap:
                 s.dma_start(out=o, in_=i),
                 reads=[xb], dma=True)
            cb = scr[c % 2]
            sq = sqr[c % 2]
            S.op("vector", lambda v, o=cb.ap, i=xb.ap: v.tensor_copy(o, i),
                 reads=[xb], writes=[cb])
            S.op("scalar", lambda s, o=sq.ap, i=xb.ap: s.activation(o, i, AF.Square),
                 reads=[xb], writes=[sq])
            for h in range(2):
                lo, hi_ = h * 512, (h + 1) * 512
                S.op("tensor", lambda t, o=ps_sum.ap[0:1, lo:hi_], l=onesb.ap,
                     r=cb.ap[:, lo:hi_], st=(c == 0), sp=(c == NCH - 1):
                     t.matmul(o, l, r, start=st, stop=sp),
                     reads=[cb, onesb], writes=_aw(c, NCH, ps_sum))
                S.op("tensor", lambda t, o=ps_sumsq.ap[0:1, lo:hi_], l=onesb.ap,
                     r=sq.ap[:, lo:hi_], st=(c == 0), sp=(c == NCH - 1):
                     t.matmul(o, l, r, start=st, stop=sp),
                     reads=[sq, onesb], writes=_aw(c, NCH, ps_sumsq))

        S.op("vector", lambda v: v.tensor_scalar(r_mu2, ps_sum.ap, 1.0 / D, None, ALU.mult),
             reads=[ps_sum], writes=[rows])
        S.op("vector", lambda v: v.tensor_scalar(r_ex22, ps_sumsq.ap, 1.0 / D, None, ALU.mult),
             reads=[ps_sumsq], writes=[rows])
        S.op("vector", lambda v: v.tensor_tensor(r_t0, r_mu2, r_mu2, ALU.mult),
             reads=[rows], writes=[rows])
        S.op("vector", lambda v: v.tensor_tensor(r_var2, r_ex22, r_t0, ALU.subtract),
             reads=[rows], writes=[rows])
        S.op("vector", lambda v: v.tensor_scalar(r_var2, r_var2, 1e-5, None, ALU.add),
             reads=[rows], writes=[rows])
        S.op("scalar", lambda s: s.activation(r_t0, r_var2, AF.Sqrt),
             reads=[rows], writes=[rows])
        S.op("vector", lambda v: v.reciprocal(r_rstd2, r_t0),
             reads=[rows], writes=[rows])
        S.op("vector", lambda v: v.tensor_tensor(r_c1, r_rstd1, r_rstd2, ALU.mult),
             reads=[rows], writes=[rows])
        S.op("vector", lambda v: v.tensor_tensor(r_t0, r_m1r1, r_mu2, ALU.add),
             reads=[rows], writes=[rows])
        S.op("vector", lambda v: v.tensor_tensor(r_c2, r_t0, r_rstd2, ALU.mult),
             reads=[rows], writes=[rows])

        bcast_row(S, rows, r_c1, bc0, onesk1, ps_bc, T)
        bcast_row(S, rows, r_c2, bc1, onesk1, ps_bc, T)

        for c in range(NCH):
            xb = xnc[c % 2]
            xs = x_sb.ap[:, c * T:(c + 1) * T]
            xo = xnorm.ap[:, c * T:(c + 1) * T]
            S.op("vector", lambda v, o=xb.ap, i=xs: v.tensor_tensor(o, i, bc0.ap, ALU.mult),
                 reads=[x_sb, bc0], writes=[xb])
            S.op("vector", lambda v, o=xb.ap: v.tensor_tensor(o, o, bc1.ap, ALU.subtract),
                 reads=[bc1], writes=[xb])
            S.op("vector", lambda v, o=xo, i=xb.ap, g=gb.ap[:, c:c + 1],
                 b=gb.ap[:, NCH + c:NCH + c + 1]:
                 v.tensor_scalar(o, i, g, b, ALU.mult, ALU.add),
                 reads=[xb, gb], writes=[xnorm])

        for m in range(3 * NCH):
            w = wb[m % 2]
            wsrc = qkvw_d[:, m * 128:(m + 1) * 128].rearrange("(c p) n -> p c n", p=128)
            S.op("gpsimd", lambda g, o=sb3(w.ap, 128), i=wsrc: g.dma_start(out=o, in_=i),
                 writes=[w], dma=True)
            pm = ps_mm[m % 2]
            e = ev[m % 2]
            for h in range(2):
                lo, hi_ = h * 512, (h + 1) * 512
                for c in range(NCH):
                    S.op("tensor", lambda t, o=pm.ap, l=w.ap[:, c * 128:(c + 1) * 128],
                         r=xnorm.ap[:, c * T + lo:c * T + hi_],
                         st=(c == 0), sp=(c == NCH - 1):
                         t.matmul(o, l, r, start=st, stop=sp),
                         reads=[w, xnorm], writes=_aw(c, NCH, pm))
                S.op("vector", lambda v, o=e.ap[:, lo:hi_], i=pm.ap: v.tensor_copy(o, i),
                     reads=[pm], writes=[e])
            if m < NCH:
                dst = fm_chunk(q_o, m)
            elif m < 2 * NCH:
                dst = fm_chunk(k_o, m - NCH)
            else:
                dst = fm_chunk(v_o, m - 2 * NCH)
            S.op("scalar", lambda s, o=dst, i=e.ap: s.dma_start(out=o, in_=i),
                 reads=[e], dma=True)

        if debug:
            S.op("sync", lambda s: s.dma_start(out=dbg_rows, in_=rows.ap),
                 reads=[rows], dma=True)
        S.fence_all("sync")
        S.build()
    return nc





def build_l0ffn():
    nc = bass.Bass("TRN2", target_bir_lowering=False, debug=False)
    xn_d = nc.dram_tensor("xn", [D, T], F32, kind="ExternalInput").ap()
    wfc_d = nc.dram_tensor("wfc", [4 * D, D], F32, kind="ExternalInput").ap()
    wpr_d = nc.dram_tensor("wpr", [D, 4 * D], F32, kind="ExternalInput").ap()
    ffn_o = nc.dram_tensor("ffn", [D, T], F32, kind="ExternalOutput").ap()

    sfcT_d = nc.dram_tensor("sfcT", [D, 4 * D], BF16, kind="Internal").ap()
    sprT_d = nc.dram_tensor("sprT", [4 * D, D], BF16, kind="Internal").ap()
    sfcT_pb = [Buf(sfcT_d[:, b * 512:(b + 1) * 512]) for b in range(16)]
    sprT_pb = [[Buf(sprT_d[cb * 1024:(cb + 1) * 1024, g2 * 1024:(g2 + 1) * 1024])
                for g2 in range(2)] for cb in range(8)]

    ones_row_f = nc.inline_tensor(np.ones((1, 128), dtype=np.float32), name="ones_row_f").ap()
    ones_col_bf = nc.inline_tensor(np.ones((128, 1), dtype=BF), name="ones_col_bf").ap()
    ident_f = nc.inline_tensor(np.eye(128, dtype=np.float32), name="ident_f").ap()

    S = Sched(nc)
    xn_sb = sb(nc, "xn_sb", [128, NCH * T], F32)
    xnc = [Buf(xn_sb.ap[:, c * T:(c + 1) * T]) for c in range(NCH)]
    ghc = [Buf(xn_sb.ap.bitcast(BF16)[:, k * 512:(k + 1) * 512]) for k in range(64)]
    xhat = sb(nc, "xhat", [128, NCH * T], BF16)
    xhc = [Buf(xhat.ap[:, c * T:(c + 1) * T]) for c in range(NCH)]
    wT = [sb(nc, "wT0", [128, NCH * 512], BF16), sb(nc, "wT1", [128, NCH * 512], BF16)]
    wT2 = [sb(nc, "wT2a", [128, T], BF16), sb(nc, "wT2b", [128, T], BF16)]
    PW = 1024
    wf32 = [sb(nc, "wf32a", [128, PW], F32), sb(nc, "wf32b", [128, PW], F32),
            sb(nc, "wf32c", [128, PW], F32)]
    wc32 = [sb(nc, "wc32a", [128, PW], F32), sb(nc, "wc32b", [128, PW], F32)]
    wsf_t = sb(nc, "wsfa", [128, PW], F32)
    wsf = [wsf_t, wsf_t]
    wsc = [sb(nc, "wsca", [128, PW], BF16), sb(nc, "wscb", [128, PW], BF16)]
    bc_mu = sb(nc, "bc_mu", [128, T], F32)
    bc_c1 = sb(nc, "bc_c1", [128, T], F32)
    bc_c2 = sb(nc, "bc_c2", [128, T], F32)
    tmax = sb(nc, "tmax", [128, T], F32)
    tmin = sb(nc, "tmin", [128, T], F32)
    scrv = sb(nc, "scrv", [128, 512], F32)
    sqs = [sb(nc, "sqs0", [128, T], BF16), sb(nc, "sqs1", [128, T], BF16)]
    rows = sb(nc, "rows", [1, 5 * T + 64], F32)
    amax_tm = sb(nc, "amax_tm", [128, 2], F32)
    cols = sb(nc, "cols", [128, 8], F32)
    onesk1 = sb(nc, "onesk1", [1, 128], F32)
    onesb = sb(nc, "onesb", [128, 1], BF16)
    idf = sb(nc, "idf", [128, 128], F32)

    pool = nc.alloc_psum_tensor("pool", [128, 4096], F32)
    pap = pool.ap()
    bank_owner = {}

    def bank(i, j=None):
        j = i + 1 if j is None else j
        return pap[:, i * 512:j * 512]

    def take(lo, hi, view):
        olds = [bank_owner[i] for i in range(lo, hi) if i in bank_owner]
        b = alias_over(view, *olds)
        for i in range(lo, hi):
            bank_owner[i] = b
        return b

    S.op("sync", lambda s: s.dma_start(out=onesk1.ap, in_=ones_row_f), writes=[onesk1], dma=True)
    S.op("sync", lambda s: s.dma_start(out=onesb.ap, in_=ones_col_bf), writes=[onesb], dma=True)
    S.op("sync", lambda s: s.dma_start(out=idf.ap, in_=ident_f), writes=[idf], dma=True)
    for c in range(NCH):
        S.op("sync", lambda s, c=c, o=xnc[c].ap, i=fm_chunk(xn_d, c): s.dma_start(out=o, in_=i),
             writes=[xnc[c]], dma=True)

    RW = rows.ap
    r_mu = RW[0:1, 0:T]
    r_var = RW[0:1, T:2 * T]
    r_t0 = RW[0:1, 2 * T:3 * T]
    r_rstd = RW[0:1, 3 * T:4 * T]
    r_amax = RW[0:1, 4 * T:5 * T]
    r_c1 = RW[0:1, T:2 * T]
    r_c2 = RW[0:1, 2 * T:3 * T]
    r_g1 = RW[0:1, 5 * T:5 * T + 16]
    r_g2 = RW[0:1, 5 * T + 16:5 * T + 32]

    gctr = [0]

    def gamma_gen(w_d, nrows, ncols, r_g, cidx, gbank):
        ps_g = take(gbank, gbank + 1, bank(gbank))
        plist = [(i, cb) for i in range(nrows // 128) for cb in range(ncols // PW)]
        first = [True]
        for pi, (i, cb) in enumerate(plist):
            pc = gctr[0]
            gctr[0] += 1
            wf = wf32[pc % 3]
            ws = wsc[pc % 2]
            q = ("sync", "gpsimd")[pc % 2]
            S.op(q, lambda s, wf=wf, i2=w_d[i * 128:(i + 1) * 128, cb * PW:(cb + 1) * PW]:
                 s.dma_start(out=wf.ap, in_=i2),
                 writes=[wf], dma=True)
            S.op("scalar", lambda s, wf=wf, ws=ws: s.activation(ws.ap, wf.ap, AF.Abs),
                 reads=[wf], writes=[ws])
            for h in range(PW // 512):
                lo, hi_ = h * 512, (h + 1) * 512
                st = first[0]
                first[0] = False
                sp = (pi == len(plist) - 1) and (h == PW // 512 - 1)
                S.op("tensor", lambda t, ws=ws, o=ps_g.ap[0:1, 0:512], l=onesb.ap,
                     r=ws.ap[:, lo:hi_], st=st, sp=sp:
                     t.matmul(o, l, r, start=st, stop=sp, skip_group_check=True),
                     reads=[ws, onesb],
                     writes=[ps_g] if (st or sp) else [])
            yield
        S.op("vector", lambda v: v.tensor_reduce(r_g[0:1, 0:1], ps_g.ap[0:1, 0:512], AX, ALU.add),
             reads=[ps_g], writes=[rows])
        S.op("vector", lambda v, n=float(nrows * ncols): v.tensor_scalar(
             r_g[0:1, 0:1], r_g[0:1, 0:1], 1.0 / n, None, ALU.mult),
             reads=[rows], writes=[rows])
        S.op("vector", lambda v: v.tensor_scalar(r_g[0:1, 1:2], r_g[0:1, 0:1],
                                                 1e-6, 0.5, ALU.add, ALU.mult),
             reads=[rows], writes=[rows])
        S.op("vector", lambda v: v.tensor_scalar(r_g[0:1, 2:3], r_g[0:1, 1:2],
                                                 -1.0, None, ALU.mult),
             reads=[rows], writes=[rows])
        ps_c = take(gbank, gbank + 1, bank(gbank))
        S.op("tensor", lambda t, o=ps_c.ap[:, 0:3], l=onesk1.ap, r=r_g[0:1, 0:3]:
             t.matmul(o, l, r, start=True, stop=True),
             reads=[rows, onesk1], writes=[ps_c])
        S.op("vector", lambda v, o=cols.ap[:, cidx:cidx + 3], i=ps_c.ap[:, 0:3]:
             v.tensor_copy(o, i),
             reads=[ps_c], writes=[cols])

    def tern_gen(w_d, outT_d, out_pb, nrows, ncols, cidx, cb_major=False):
        if cb_major:
            plist = [(i, cb) for cb in range(ncols // PW) for i in range(nrows // 128)]
        else:
            plist = [(i, cb) for i in range(nrows // 128) for cb in range(ncols // PW)]
        for pi, (i, cb) in enumerate(plist):
            wf = wf32[pi % 3]
            r2 = wc32[pi % 2]
            rr = wsf[0]
            st_ = wsc[pi % 2]
            q = ("sync", "gpsimd")[pi % 2]
            S.op(q, lambda s, wf=wf, i2=w_d[i * 128:(i + 1) * 128, cb * PW:(cb + 1) * PW]:
                 s.dma_start(out=wf.ap, in_=i2),
                 writes=[wf], dma=True)
            S.op("vector", lambda v, wf=wf, r2=r2, t1=cols.ap[:, cidx + 1:cidx + 2],
                 t2=cols.ap[:, cidx + 2:cidx + 3]:
                 v.tensor_scalar(r2.ap, wf.ap, t1, t2, ALU.min, ALU.max),
                 reads=[wf, cols], writes=[r2])
            S.op("gpsimd",
                 lambda g, wf=wf, r2=r2: g.tensor_tensor(r2.ap, wf.ap, r2.ap, ALU.subtract),
                 reads=[wf], writes=[r2])
            S.op("scalar", lambda s, rr=rr, r2=r2: s.activation(rr.ap, r2.ap, AF.Sign),
                 reads=[r2], writes=[rr])
            for hp in range(2):
                ps_tr = take(6 + hp, 7 + hp, bank(6 + hp))
                for k in range(4):
                    S.op("tensor", lambda t, o=ps_tr.ap[:, k * 128:(k + 1) * 128],
                         i2=rr.ap[:, hp * 512 + k * 128:hp * 512 + (k + 1) * 128]:
                         t.transpose(o, i2, idf.ap),
                         reads=[rr, idf], writes=[ps_tr] if k in (0, 3) else [])
                S.op("scalar", lambda s, st_=st_, hp=hp, ps_tr=ps_tr:
                     s.copy(st_.ap[:, hp * 512:(hp + 1) * 512], ps_tr.ap[:, 0:512]),
                     reads=[ps_tr], writes=[st_])
            pb = out_pb(i, cb)
            S.op(("sync", "gpsimd")[pi % 2], lambda s, st_=st_,
                 o=outT_d[cb * PW:(cb + 1) * PW, i * 128:(i + 1) * 128].rearrange(
                     "(k p) n -> p k n", p=128),
                 i2=st_.ap.rearrange("p (k n) -> p k n", n=128):
                 s.dma_start(out=o, in_=i2),
                 reads=[st_], writes=[pb], dma=True)
            yield

    def stats_gen():
        ps_sum = take(0, 2, pap[0:1, 0:1024])
        ps_sumsq = take(2, 4, pap[0:1, 1024:2048])
        for c in range(NCH):
            cb2 = sqs[c % 2]
            S.op("vector", lambda v, c=c, o=cb2.ap: v.tensor_copy(o, xnc[c].ap),
                 reads=[xnc[c]], writes=[cb2])
            for h in range(2):
                lo, hi_ = h * 512, (h + 1) * 512
                S.op("tensor", lambda t, cb2=cb2, o=ps_sum.ap[0:1, lo:hi_], l=onesb.ap,
                     r=cb2.ap[:, lo:hi_], st=(c == 0), sp=(c == NCH - 1):
                     t.matmul(o, l, r, start=st, stop=sp),
                     reads=[cb2, onesb], writes=_aw(c, NCH, ps_sum))
            if c == 0:
                S.op("vector", lambda v: v.tensor_copy(tmax.ap, xnc[0].ap),
                     reads=[xnc[0]], writes=[tmax])
                S.op("vector", lambda v: v.tensor_copy(tmin.ap, xnc[0].ap),
                     reads=[xnc[0]], writes=[tmin])
            else:
                S.op("vector", lambda v, c=c: v.tensor_tensor(tmax.ap, tmax.ap, xnc[c].ap, ALU.max),
                     reads=[xnc[c]], writes=[tmax])
                S.op("vector", lambda v, c=c: v.tensor_tensor(tmin.ap, tmin.ap, xnc[c].ap, ALU.min),
                     reads=[xnc[c]], writes=[tmin])
            yield
        for c in range(NCH):
            sq = sqs[c % 2]
            S.op("scalar", lambda s, c=c, o=sq.ap: s.activation(o, xnc[c].ap, AF.Square),
                 reads=[xnc[c]], writes=[sq])
            for h in range(2):
                lo, hi_ = h * 512, (h + 1) * 512
                S.op("tensor", lambda t, sq=sq, o=ps_sumsq.ap[0:1, lo:hi_], l=onesb.ap,
                     r=sq.ap[:, lo:hi_], st=(c == 0), sp=(c == NCH - 1):
                     t.matmul(o, l, r, start=st, stop=sp),
                     reads=[sq, onesb], writes=_aw(c, NCH, ps_sumsq))
            yield
        stat_rows(float(D), 1e-6, T, ps_sum, ps_sumsq)
        yield
        amax_rows(tmax.ap, tmin.ap, T)
        yield
        c_rows()
        ps_bc = take(4, 5, bank(4))
        bcast_row(S, rows, r_mu, bc_mu, onesk1, ps_bc, T)
        bcast_row(S, rows, r_c1, bc_c1, onesk1, ps_bc, T)
        bcast_row(S, rows, r_c2, bc_c2, onesk1, ps_bc, T)
        yield
        for c in range(NCH):
            for h in range(2):
                lo, hi_ = h * 512, (h + 1) * 512
                S.op("vector", lambda v, c=c, lo=lo, hi_=hi_: v.tensor_tensor(
                     scrv.ap, xnc[c].ap[:, lo:hi_], bc_mu.ap[:, lo:hi_], ALU.subtract),
                     reads=[xnc[c], bc_mu], writes=[scrv])
                S.op("vector", lambda v, lo=lo, hi_=hi_: v.tensor_tensor(
                     scrv.ap, scrv.ap, bc_c1.ap[:, lo:hi_], ALU.mult),
                     reads=[bc_c1], writes=[scrv])
                S.op("vector", lambda v: v.tensor_scalar(
                     scrv.ap, scrv.ap, 128.0, -128.0, ALU.min, ALU.max),
                     reads=[], writes=[scrv])
                S.op("vector", lambda v, c=c, lo=lo, hi_=hi_: v.tensor_tensor(
                     xhc[c].ap[:, lo:hi_], scrv.ap, bc_c2.ap[:, lo:hi_], ALU.mult),
                     reads=[scrv, bc_c2], writes=[xhc[c]])
            yield

    def stat_rows(dD_, eps, width, pss, psq):
        mu, var, t0, rstd = (r_mu[0:1, 0:width], r_var[0:1, 0:width],
                             r_t0[0:1, 0:width], r_rstd[0:1, 0:width])
        S.op("vector", lambda v, pss=pss: v.tensor_scalar(mu, pss.ap, 1.0 / dD_, None, ALU.mult),
             reads=[pss], writes=[rows])
        S.op("vector", lambda v, psq=psq: v.tensor_scalar(t0, psq.ap, 1.0 / dD_, None, ALU.mult),
             reads=[psq], writes=[rows])
        S.op("vector", lambda v: v.tensor_tensor(var, mu, mu, ALU.mult),
             reads=[rows], writes=[rows])
        S.op("vector", lambda v: v.tensor_tensor(var, t0, var, ALU.subtract),
             reads=[rows], writes=[rows])
        S.op("vector", lambda v, e=eps: v.tensor_scalar(var, var, e, None, ALU.add),
             reads=[rows], writes=[rows])
        S.op("scalar", lambda s: s.activation(t0, var, AF.Sqrt),
             reads=[rows], writes=[rows])
        S.op("vector", lambda v: v.reciprocal(rstd, t0),
             reads=[rows], writes=[rows])

    def amax_rows(tmaxap, tminap, width):
        psA = take(4, 5, bank(4))
        for tt in range(width // 128):
            S.op("tensor", lambda t, tt=tt, i=tmaxap[:, tt * 128:(tt + 1) * 128],
                 o=psA.ap[:, 0:128]:
                 t.transpose(o, i, idf.ap),
                 reads=[tmax, idf], writes=[psA])
            S.op("tensor", lambda t, tt=tt, i=tminap[:, tt * 128:(tt + 1) * 128],
                 o=psA.ap[:, 128:256]:
                 t.transpose(o, i, idf.ap),
                 reads=[tmin, idf], writes=[psA])
            S.op("vector", lambda v: v.tensor_reduce(amax_tm.ap[:, 0:1], psA.ap[:, 0:128], AX, ALU.max),
                 reads=[psA], writes=[amax_tm])
            S.op("vector", lambda v: v.tensor_reduce(amax_tm.ap[:, 1:2], psA.ap[:, 128:256], AX, ALU.min),
                 reads=[psA], writes=[amax_tm])
            S.op("tensor", lambda t, o=psA.ap[0:1, 256:384], i=amax_tm.ap[:, 0:1]:
                 t.transpose(o, i, idf.ap),
                 reads=[amax_tm, idf], writes=[psA])
            S.op("tensor", lambda t, o=psA.ap[0:1, 384:512], i=amax_tm.ap[:, 1:2]:
                 t.transpose(o, i, idf.ap),
                 reads=[amax_tm, idf], writes=[psA])
            S.op("vector", lambda v, tt=tt: v.tensor_tensor(
                 r_t0[0:1, tt * 128:(tt + 1) * 128], psA.ap[0:1, 256:384],
                 r_mu[0:1, tt * 128:(tt + 1) * 128], ALU.subtract),
                 reads=[psA, rows], writes=[rows])
            S.op("vector", lambda v, tt=tt: v.tensor_tensor(
                 r_amax[0:1, tt * 128:(tt + 1) * 128],
                 r_mu[0:1, tt * 128:(tt + 1) * 128], psA.ap[0:1, 384:512], ALU.subtract),
                 reads=[psA, rows], writes=[rows])
            S.op("vector", lambda v, tt=tt: v.tensor_tensor(
                 r_amax[0:1, tt * 128:(tt + 1) * 128],
                 r_amax[0:1, tt * 128:(tt + 1) * 128],
                 r_t0[0:1, tt * 128:(tt + 1) * 128], ALU.max),
                 reads=[rows], writes=[rows])

    def c_rows(width=T):
        c1, c2, am, rstd = (r_c1[0:1, 0:width], r_c2[0:1, 0:width],
                            r_amax[0:1, 0:width], r_rstd[0:1, 0:width])
        S.op("vector", lambda v: v.reciprocal(c1, am), reads=[rows], writes=[rows])
        S.op("vector", lambda v: v.tensor_scalar(c1, c1, 128.0, None, ALU.mult),
             reads=[rows], writes=[rows])
        S.op("vector", lambda v: v.tensor_tensor(c2, rstd, am, ALU.mult),
             reads=[rows], writes=[rows])
        S.op("vector", lambda v: v.tensor_scalar(c2, c2, 1.0 / 128.0, None, ALU.mult),
             reads=[rows], writes=[rows])

    def drive(specs):
        done = [False] * len(specs)
        while not all(done):
            for k, (g, n) in enumerate(specs):
                if done[k]:
                    continue
                for _ in range(n):
                    try:
                        next(g)
                    except StopIteration:
                        done[k] = True
                        break

    drive([(stats_gen(), 1),
           (gamma_gen(wfc_d, 4 * D, D, r_g1, 0, 6), 3),
           (gamma_gen(wpr_d, D, 4 * D, r_g2, 3, 5), 3)])

    ev_t = alias_over(bc_c2.ap[:, 512:1024], bc_c2)
    ev = [ev_t, ev_t]

    for half in range(2):
        toff = half * 512

        if half == 0:
            t1 = tern_gen(wfc_d, sfcT_d, lambda i, cb: sfcT_pb[i // 4], 4 * D, D, 0)
            t2 = tern_gen(wpr_d, sprT_d, lambda i, cb: sprT_pb[cb][i // 8], D, 4 * D, 3,
                          cb_major=True)
            for _ in t1:
                pass
            for _ in t2:
                pass

        ps_h = [take(i, i + 1, bank(i)) for i in range(4)]
        ps_sum = take(4, 5, pap[0:1, 4 * 512:5 * 512])
        ps_sumsq = take(5, 6, pap[0:1, 5 * 512:6 * 512])
        for blk in range(16):
            w = wT[blk % 2]
            S.op(("sync", "gpsimd")[blk % 2], lambda s, blk=blk, w=w,
                 o=sb3(w.ap, 512),
                 i=sfcT_d[:, blk * 512:(blk + 1) * 512].rearrange("(c p) n -> p c n", p=128):
                 s.dma_start(out=o, in_=i),
                 reads=[sfcT_pb[blk]], writes=[w], dma=True)
            for m in range(4):
                pm = ps_h[m]
                gci = ghc[blk * 4 + m]
                for dc in range(NCH):
                    S.op("tensor", lambda t, w=w, pm=pm, m=m, dc=dc,
                         l=w.ap[:, dc * 512 + m * 128:dc * 512 + m * 128 + 128],
                         r=xhc[dc].ap[:, toff:toff + 512],
                         st=(dc == 0), sp=(dc == NCH - 1):
                         t.matmul(pm.ap, l, r, start=st, stop=sp),
                         reads=[w, xhc[dc]], writes=_aw(dc, NCH, pm))
                S.op("scalar", lambda s, pm=pm, gci=gci:
                     s.activation(gci.ap, pm.ap, AF.Gelu, scale=cols.ap[:, 0:1]),
                     reads=[pm, cols], writes=[gci])
                k = blk * 4 + m
                S.op("tensor", lambda t, gci=gci, o=ps_sum.ap[0:1, 0:512], l=onesb.ap,
                     st=(k == 0), sp=(k == 63):
                     t.matmul(o, l, gci.ap, start=st, stop=sp),
                     reads=[gci, onesb], writes=_aw(k, 64, ps_sum))
                sq = sqs[k % 2]
                S.op("scalar", lambda s, gci=gci, sq=sq: s.activation(sq.ap[:, 0:512], gci.ap, AF.Square),
                     reads=[gci], writes=[sq])
                S.op("tensor", lambda t, sq=sq, o=ps_sumsq.ap[0:1, 0:512], l=onesb.ap,
                     st=(k == 0), sp=(k == 63):
                     t.matmul(o, l, sq.ap[:, 0:512], start=st, stop=sp),
                     reads=[sq, onesb], writes=_aw(k, 64, ps_sumsq))
                if k == 0:
                    S.op("vector", lambda v, gci=gci: v.tensor_copy(tmax.ap[:, 0:512], gci.ap),
                         reads=[gci], writes=[tmax])
                    S.op("vector", lambda v, gci=gci: v.tensor_copy(tmin.ap[:, 0:512], gci.ap),
                         reads=[gci], writes=[tmin])
                else:
                    S.op("vector", lambda v, gci=gci: v.tensor_tensor(
                         tmax.ap[:, 0:512], tmax.ap[:, 0:512], gci.ap, ALU.max),
                         reads=[gci], writes=[tmax])
                    S.op("vector", lambda v, gci=gci: v.tensor_tensor(
                         tmin.ap[:, 0:512], tmin.ap[:, 0:512], gci.ap, ALU.min),
                         reads=[gci], writes=[tmin])

        stat_rows(float(4 * D), 1e-6, 512, ps_sum, ps_sumsq)
        amax_rows(tmax.ap[:, 0:512], tmin.ap[:, 0:512], 512)
        c_rows(512)
        ps_bc = take(4, 5, bank(4))
        bcast_row(S, rows, r_mu[0:1, 0:512], bc_mu, onesk1, ps_bc, 512)
        bcast_row(S, rows, r_c1[0:1, 0:512], bc_c1, onesk1, ps_bc, 512)
        bcast_row(S, rows, r_c2[0:1, 0:512], bc_c2, onesk1, ps_bc, 512)

        for k in range(64):
            gci = ghc[k]
            S.op("vector", lambda v, gci=gci: v.tensor_tensor(
                 scrv.ap[:, 0:512], gci.ap, bc_mu.ap[:, 0:512], ALU.subtract),
                 reads=[gci, bc_mu], writes=[scrv])
            S.op("vector", lambda v: v.tensor_tensor(
                 scrv.ap[:, 0:512], scrv.ap[:, 0:512], bc_c1.ap[:, 0:512], ALU.mult),
                 reads=[bc_c1], writes=[scrv])
            S.op("vector", lambda v: v.tensor_scalar(
                 scrv.ap[:, 0:512], scrv.ap[:, 0:512], 128.0, -128.0, ALU.min, ALU.max),
                 reads=[], writes=[scrv])
            S.op("vector", lambda v, gci=gci: v.tensor_tensor(
                 gci.ap, scrv.ap[:, 0:512], bc_c2.ap[:, 0:512], ALU.mult),
                 reads=[scrv, bc_c2], writes=[gci])

        for grp in range(2):
            ps8 = [take(i, i + 1, bank(i)) for i in range(8)]
            for dc in range(64):
                w2 = wT2[dc % 2]
                S.op(("sync", "gpsimd")[dc % 2], lambda s, w2=w2, grp=grp, dc=dc, o=w2.ap,
                     i=sprT_d[dc * 128:(dc + 1) * 128, grp * 1024:(grp + 1) * 1024]:
                     s.dma_start(out=o, in_=i),
                     reads=[sprT_pb[dc // 8][grp]], writes=[w2], dma=True)
                gci = ghc[dc]
                for m in range(8):
                    S.op("tensor", lambda t, w2=w2, m=m, gci=gci, ps=ps8[m],
                         l=w2.ap[:, m * 128:(m + 1) * 128],
                         st=(dc == 0), sp=(dc == 63):
                         t.matmul(ps.ap, l, gci.ap, start=st, stop=sp),
                         reads=[w2, gci], writes=_aw(dc, 64, ps8[m]))
            for m in range(8):
                e = ev[m % 2]
                S.op("vector", lambda v, e=e, ps=ps8[m]:
                     v.tensor_scalar(e.ap, ps.ap, cols.ap[:, 3:4], None, ALU.mult),
                     reads=[ps8[m], cols], writes=[e])
                oc = grp * 8 + m
                S.op("scalar", lambda s, e=e, o=fm_chunk(ffn_o, oc)[:, toff:toff + 512]:
                     s.dma_start(out=o, in_=e.ap),
                     reads=[e], dma=True)

    S.fence_all("sync")
    S.build()
    return nc



SCALE0 = 1.0 / float(np.sqrt(2048.0))


def build_l0attn():
    nc = bass.Bass("TRN2", target_bir_lowering=False, debug=False)
    q_d = nc.dram_tensor("q", [D, T], BF16, kind="ExternalInput").ap()
    kf_d = nc.dram_tensor("kf", [D, 8 * T], BF16, kind="ExternalInput").ap()
    vf_d = nc.dram_tensor("vf", [8 * T, D], BF16, kind="ExternalInput").ap()
    x_d = nc.dram_tensor("x", [D, T], F32, kind="ExternalInput").ap()
    xn_d = nc.dram_tensor("xn", [D, T], F32, kind="ExternalInput").ap()
    ffn_d = nc.dram_tensor("ffn", [D, T], F32, kind="ExternalInput").ap()
    wo_d = nc.dram_tensor("wo", [D, D], F32, kind="ExternalInput").ap()
    qidx_d = nc.dram_tensor("qidx", [128, 8], F32, kind="ExternalInput").ap()
    xnew_o = nc.dram_tensor("xnew", [D, T], F32, kind="ExternalOutput").ap()

    iota512 = nc.inline_tensor(np.arange(512, dtype=np.float32).reshape(1, 512),
                               name="iota512").ap()
    ones_row_f = nc.inline_tensor(np.ones((1, 128), dtype=np.float32), name="ones_row_f").ap()
    ident_b = nc.inline_tensor(np.eye(128, dtype=BF), name="ident_b").ap()

    S = Sched(nc)
    q_sb = sb(nc, "q_sb", [128, NCH * T], BF16)
    yac = [sb(nc, f"yac{i}", [128, D], F32) for i in range(4)]
    kblk = [sb(nc, "kblk0", [128, NCH * 512], BF16), sb(nc, "kblk1", [128, NCH * 512], BF16)]
    vblk = [sb(nc, "vblk0", [128, 4 * D], BF16), sb(nc, "vblk1", [128, 4 * D], BF16)]
    yT = sb(nc, "yT", [128, NCH * 512], BF16)
    ssb = [sb(nc, "ssb0", [128, 512], F32), sb(nc, "ssb1", [128, 512], F32)]
    pen = [sb(nc, "pen0", [128, 512], F32), sb(nc, "pen1", [128, 512], F32)]
    pT = [sb(nc, "pT0", [128, 512], BF16), sb(nc, "pT1", [128, 512], BF16)]
    pbf = [sb(nc, "pbf0", [128, 512], BF16), sb(nc, "pbf1", [128, 512], BF16)]
    wob = [sb(nc, "wob0", [128, NCH * 128], BF16), sb(nc, "wob1", [128, NCH * 128], BF16)]
    sumb = [sb(nc, "sumb0", [128, 3 * 512], F32), sb(nc, "sumb1", [128, 3 * 512], F32)]
    evx = [sb(nc, "evx0", [128, 512], F32), sb(nc, "evx1", [128, 512], F32)]
    mrow = sb(nc, "mrow", [128, 16], F32)
    qidx_sb = sb(nc, "qidx_sb", [128, 8], F32)
    io_sb = sb(nc, "io_sb", [1, 512], F32)
    jrow = sb(nc, "jrow", [128, 512], F32)
    onesk1 = sb(nc, "onesk1", [1, 128], F32)
    idb = sb(nc, "idb", [128, 128], BF16)

    pool = nc.alloc_psum_tensor("pool", [128, 3584], F32)
    poolb = nc.alloc_psum_tensor("poolb", [128, 1024], BF16)
    pap = pool.ap()
    bap = poolb.ap()
    bank_owner = {}

    def take(lo, hi, view):
        olds = [bank_owner[i] for i in range(lo, hi) if i in bank_owner]
        b = alias_over(view, *olds)
        for i in range(lo, hi):
            bank_owner[i] = b
        return b

    S.op("sync", lambda s: s.dma_start(out=sb3(q_sb.ap, T), in_=fm_full(q_d)),
         writes=[q_sb], dma=True)
    S.op("sync", lambda s: s.dma_start(out=qidx_sb.ap, in_=qidx_d),
         writes=[qidx_sb], dma=True)
    S.op("sync", lambda s: s.dma_start(out=onesk1.ap, in_=ones_row_f),
         writes=[onesk1], dma=True)
    S.op("sync", lambda s: s.dma_start(out=idb.ap, in_=ident_b),
         writes=[idb], dma=True)
    S.op("sync", lambda s: s.dma_start(out=io_sb.ap, in_=iota512),
         writes=[io_sb], dma=True)
    for i in range(4):
        S.op("vector", lambda v, i2=yac[i].ap: v.memset(i2, 0.0), writes=[yac[i]])
    ps_j = take(2, 3, pap[:, 2 * 512:3 * 512])
    S.op("tensor", lambda t: t.matmul(ps_j.ap, onesk1.ap, io_sb.ap, start=True, stop=True),
         reads=[onesk1, io_sb], writes=[ps_j])
    S.op("vector", lambda v: v.tensor_copy(jrow.ap, ps_j.ap),
         reads=[ps_j], writes=[jrow])

    MR = mrow.ap

    def m_ap(qt):
        return MR[:, qt:qt + 1]

    def l_ap(qt):
        return MR[:, 4 + qt:5 + qt]
    negm = MR[:, 8:9]
    alpha = MR[:, 9:10]
    t0c = MR[:, 10:11]
    rl = MR[:, 11:12]
    bm = MR[:, 12:13]

    for g in range(2):
        nkb = (8, 16)[g]
        for qt in range(4):
            S.op("vector", lambda v, a=m_ap(qt): v.memset(a, -3e38), writes=[mrow])
            S.op("vector", lambda v, a=l_ap(qt): v.memset(a, 0.0), writes=[mrow])
        if g == 1:
            for i in range(4):
                S.op("vector", lambda v, i2=yac[i].ap: v.memset(i2, 0.0), writes=[yac[i]])
        for kb in range(nkb):
            kbu = kblk[kb % 2]
            vbu = vblk[kb % 2]
            S.op("sync", lambda s, o=sb3(kbu.ap, 512), i=fm_full(kf_d[:, kb * 512:(kb + 1) * 512]):
                 s.dma_start(out=o, in_=i),
                 writes=[kbu], dma=True)
            S.op("sync", lambda s, o=vbu.ap.rearrange("p (w d) -> p w d", d=D),
                 i=vf_d[kb * 512:(kb + 1) * 512, :].rearrange("(w p) d -> p w d", p=128):
                 s.dma_start(out=o, in_=i),
                 writes=[vbu], dma=True)
            for qt in range(4):
                ps_s = take(0 + qt % 2, 1 + qt % 2, pap[:, (qt % 2) * 512:(qt % 2 + 1) * 512])
                for c in range(NCH):
                    S.op("tensor", lambda t, o=ps_s.ap,
                         l=q_sb.ap[:, c * T + (g * 4 + qt) * 128:c * T + (g * 4 + qt) * 128 + 128],
                         r=kbu.ap[:, c * 512:(c + 1) * 512],
                         st=(c == 0), sp=(c == NCH - 1):
                         t.matmul(o, l, r, start=st, stop=sp),
                         reads=[q_sb, kbu], writes=_aw(c, NCH, ps_s))
                pe = pen[qt % 2]
                S.op("vector", lambda v, o=pe.ap, qc=qidx_sb.ap[:, g * 4 + qt:g * 4 + qt + 1],
                     kb512=float(kb * 512):
                     v.tensor_scalar(o, jrow.ap, qc, kb512, ALU.subtract, ALU.add),
                     reads=[jrow, qidx_sb], writes=[pe])
                S.op("vector", lambda v, o=pe.ap: v.tensor_scalar(o, o, 0.0, -60000.0, ALU.max, ALU.mult),
                     reads=[], writes=[pe])
                sb_s = ssb[qt % 2]
                S.op("vector", lambda v, o=sb_s.ap, i=ps_s.ap, p=pe.ap:
                     v.scalar_tensor_tensor(o, i, SCALE0, p, ALU.mult, ALU.add),
                     reads=[ps_s, pe], writes=[sb_s])
                S.op("vector", lambda v, i=sb_s.ap: v.tensor_reduce(bm, i, AX, ALU.max),
                     reads=[sb_s], writes=[mrow])
                S.op("vector", lambda v, m=m_ap(qt): v.tensor_tensor(t0c, m, bm, ALU.max),
                     reads=[mrow], writes=[mrow])
                S.op("vector", lambda v, m=m_ap(qt): v.tensor_tensor(alpha, m, t0c, ALU.subtract),
                     reads=[mrow], writes=[mrow])
                S.op("scalar", lambda s: s.activation(alpha, alpha, AF.Exp),
                     reads=[mrow], writes=[mrow])
                S.op("vector", lambda v, m=m_ap(qt): v.tensor_copy(m, t0c),
                     reads=[mrow], writes=[mrow])
                S.op("vector", lambda v: v.tensor_scalar(negm, t0c, -1.0, None, ALU.mult),
                     reads=[mrow], writes=[mrow])
                pb = pbf[qt % 2]
                S.op("scalar", lambda s, o=pb.ap, i=sb_s.ap: s.activation(
                     o, i, AF.Exp, bias=negm, accum_out=bm),
                     reads=[sb_s, mrow], writes=[pb, mrow])
                S.op("vector", lambda v, la=l_ap(qt): v.scalar_tensor_tensor(
                     la, la, alpha, bm, ALU.mult, ALU.add),
                     reads=[mrow], writes=[mrow])
                ps_tr = take(7, 8, bap[:, 0:512])
                pt = pT[qt % 2]
                for w in range(4):
                    S.op("tensor", lambda t, o=ps_tr.ap[:, w * 128:(w + 1) * 128],
                         i=pb.ap[:, w * 128:(w + 1) * 128]:
                         t.transpose(o, i, idb.ap),
                         reads=[pb, idb], writes=[ps_tr] if w in (0, 3) else [])
                S.op("vector", lambda v, o=pt.ap, i=ps_tr.ap: v.tensor_copy(o, i),
                     reads=[ps_tr], writes=[pt])
                for db in range(4):
                    ps_y = take(3 + db % 2, 4 + db % 2, pap[:, (3 + db % 2) * 512:(4 + db % 2) * 512])
                    for kc in range(4):
                        S.op("tensor", lambda t, o=ps_y.ap,
                             l=pt.ap[:, kc * 128:(kc + 1) * 128],
                             r=vbu.ap[:, kc * D + db * 512:kc * D + db * 512 + 512],
                             st=(kc == 0), sp=(kc == 3):
                             t.matmul(o, l, r, start=st, stop=sp),
                             reads=[pt, vbu], writes=_aw(kc, 4, ps_y))
                    S.op("vector", lambda v, ya=yac[qt].ap[:, db * 512:(db + 1) * 512],
                         i=ps_y.ap:
                         v.scalar_tensor_tensor(ya, ya, alpha, i, ALU.mult, ALU.add),
                         reads=[ps_y, mrow], writes=[yac[qt]])
        for qt in range(4):
            S.op("vector", lambda v, la=l_ap(qt): v.reciprocal(rl, la),
                 reads=[mrow], writes=[mrow])
            S.op("vector", lambda v, ya=yac[qt].ap: v.tensor_scalar(ya, ya, rl, None, ALU.mult),
                 reads=[mrow], writes=[yac[qt]])
            ps_tr = take(7, 8, bap[:, 0:512])
            for c in range(NCH):
                S.op("scalar", lambda s, o=pbf[0].ap[:, 0:128], i=yac[qt].ap[:, c * 128:(c + 1) * 128]:
                     s.copy(o, i),
                     reads=[yac[qt]], writes=[pbf[0]])
                S.op("tensor", lambda t, o=ps_tr.ap[:, 0:128], i=pbf[0].ap[:, 0:128]:
                     t.transpose(o, i, idb.ap),
                     reads=[pbf[0], idb], writes=[ps_tr])
                S.op("vector", lambda v, o=yT.ap[:, c * 512 + qt * 128:c * 512 + qt * 128 + 128],
                     i=ps_tr.ap[:, 0:128]:
                     v.tensor_copy(o, i),
                     reads=[ps_tr], writes=[yT])
        for m in range(NCH):
            w = wob[m % 2]
            wsrc = wo_d[:, m * 128:(m + 1) * 128].rearrange("(c p) n -> p c n", p=128)
            S.op("gpsimd", lambda g2, o=sb3(w.ap, 128), i=wsrc: g2.dma_start(out=o, in_=i),
                 writes=[w], dma=True)
            ps_o = take(5, 6, pap[:, 5 * 512:6 * 512])
            for c in range(NCH):
                S.op("tensor", lambda t, o=ps_o.ap, l=w.ap[:, c * 128:(c + 1) * 128],
                     r=yT.ap[:, c * 512:(c + 1) * 512],
                     st=(c == 0), sp=(c == NCH - 1):
                     t.matmul(o, l, r, start=st, stop=sp),
                     reads=[w, yT], writes=_aw(c, NCH, ps_o))
            sm = sumb[m % 2]
            off = g * 512
            S.op("sync", lambda s, o=sm.ap[:, 0:512], i=fm_chunk(x_d, m)[:, off:off + 512]:
                 s.dma_start(out=o, in_=i), writes=[sm], dma=True)
            S.op("sync", lambda s, o=sm.ap[:, 512:1024], i=fm_chunk(xn_d, m)[:, off:off + 512]:
                 s.dma_start(out=o, in_=i), writes=[sm], dma=True)
            S.op("sync", lambda s, o=sm.ap[:, 1024:1536], i=fm_chunk(ffn_d, m)[:, off:off + 512]:
                 s.dma_start(out=o, in_=i), writes=[sm], dma=True)
            e = evx[m % 2]
            S.op("vector", lambda v, o=e.ap, s2=sm.ap: v.tensor_tensor(
                 o, s2[:, 0:512], s2[:, 512:1024], ALU.add),
                 reads=[sm], writes=[e])
            S.op("vector", lambda v, o=e.ap, s2=sm.ap: v.tensor_tensor(
                 o, o, s2[:, 1024:1536], ALU.add),
                 reads=[sm], writes=[e])
            S.op("vector", lambda v, o=e.ap, i=ps_o.ap: v.tensor_tensor(o, o, i, ALU.add),
                 reads=[ps_o], writes=[e])
            S.op("scalar", lambda s, o=fm_chunk(xnew_o, m)[:, off:off + 512], i=e.ap:
                 s.dma_start(out=o, in_=i),
                 reads=[e], dma=True)

    S.fence_all("sync")
    S.build()
    return nc




def build_l1front():
    nc = bass.Bass("TRN2", target_bir_lowering=False, debug=False)
    x_d = nc.dram_tensor("x", [D, T], F32, kind="ExternalInput").ap()
    ipw_d = nc.dram_tensor("ipw", [5136, D], F32, kind="ExternalInput").ap()
    dtb_d = nc.dram_tensor("dtb", [16, 1], F32, kind="ExternalInput").ap()
    xn2_o = nc.dram_tensor("xn2", [D, T], F32, kind="ExternalOutput").ap()
    z_o = nc.dram_tensor("z", [D, T], BF16, kind="ExternalOutput").ap()
    xbc_o = nc.dram_tensor("xbc", [3072, T], F32, kind="ExternalOutput").ap()
    dts_o = nc.dram_tensor("dts", [16, T], F32, kind="ExternalOutput").ap()

    ipw_bf = Buf(nc.dram_tensor("ipw_bf", [5136, D], BF16, kind="Internal").ap())
    ones_row_f = nc.inline_tensor(np.ones((1, 128), dtype=np.float32), name="ones_row_f").ap()
    ones_col_bf = nc.inline_tensor(np.ones((128, 1), dtype=BF), name="ones_col_bf").ap()

    S = Sched(nc)
    x_sb = sb(nc, "x_sb", [128, NCH * T], F32)
    xn2b = sb(nc, "xn2b", [128, NCH * T], BF16)
    wT = [sb(nc, "wT0", [128, NCH * 512], BF16), sb(nc, "wT1", [128, NCH * 512], BF16)]
    scr = [sb(nc, "scr0", [128, T], BF16), sb(nc, "scr1", [128, T], BF16)]
    xc = [sb(nc, "xc0", [128, T], F32), sb(nc, "xc1", [128, T], F32)]
    bc0 = sb(nc, "bc0", [128, T], F32)
    bc1 = sb(nc, "bc1", [128, T], F32)
    rows = sb(nc, "rows", [1, 5 * T], F32)
    dtr = sb(nc, "dtr", [16, 6 * T], F32)
    ev = [sb(nc, "ev0", [128, T], F32), sb(nc, "ev1", [128, T], F32)]
    onesk1 = sb(nc, "onesk1", [1, 128], F32)
    onesb = sb(nc, "onesb", [128, 1], BF16)
    dtb_sb = sb(nc, "dtb_sb", [16, 1], F32)

    pool = nc.alloc_psum_tensor("pool", [128, 4096], F32)
    pap = pool.ap()
    bank_owner = {}

    def take(lo, hi, view):
        olds = [bank_owner[i] for i in range(lo, hi) if i in bank_owner]
        b = alias_over(view, *olds)
        for i in range(lo, hi):
            bank_owner[i] = b
        return b

    S.op("sync", lambda s: s.dma_start(out=sb3(x_sb.ap, T), in_=fm_full(x_d)),
         writes=[x_sb], dma=True)
    S.op("sync", lambda s: s.dma_start(out=onesk1.ap, in_=ones_row_f), writes=[onesk1], dma=True)
    S.op("sync", lambda s: s.dma_start(out=onesb.ap, in_=ones_col_bf), writes=[onesb], dma=True)
    S.op("sync", lambda s: s.dma_start(out=dtb_sb.ap, in_=dtb_d), writes=[dtb_sb], dma=True)
    for p in range(8):
        S.op("gpsimd", lambda g, o=ipw_bf.ap[p * 642:(p + 1) * 642, :],
             i=ipw_d[p * 642:(p + 1) * 642, :]: g.dma_start(out=o, in_=i),
             writes=[ipw_bf], dma=True)

    ps_sum = take(0, 2, pap[0:1, 0:1024])
    ps_sumsq = take(2, 4, pap[0:1, 1024:2048])
    for c in range(NCH):
        cb = scr[c % 2]
        xs = x_sb.ap[:, c * T:(c + 1) * T]
        S.op("vector", lambda v, o=cb.ap, i=xs: v.tensor_copy(o, i),
             reads=[x_sb], writes=[cb])
        for h in range(2):
            lo, hi_ = h * 512, (h + 1) * 512
            S.op("tensor", lambda t, o=ps_sum.ap[0:1, lo:hi_], l=onesb.ap,
                 r=cb.ap[:, lo:hi_], st=(c == 0), sp=(c == NCH - 1):
                 t.matmul(o, l, r, start=st, stop=sp),
                 reads=[cb, onesb], writes=_aw(c, NCH, ps_sum))
    for c in range(NCH):
        sq = scr[c % 2]
        xs = x_sb.ap[:, c * T:(c + 1) * T]
        S.op("scalar", lambda s, o=sq.ap, i=xs: s.activation(o, i, AF.Square),
             reads=[x_sb], writes=[sq])
        for h in range(2):
            lo, hi_ = h * 512, (h + 1) * 512
            S.op("tensor", lambda t, o=ps_sumsq.ap[0:1, lo:hi_], l=onesb.ap,
                 r=sq.ap[:, lo:hi_], st=(c == 0), sp=(c == NCH - 1):
                 t.matmul(o, l, r, start=st, stop=sp),
                 reads=[sq, onesb], writes=_aw(c, NCH, ps_sumsq))

    RW = rows.ap
    r_mu = RW[0:1, 0:T]
    r_var = RW[0:1, T:2 * T]
    r_t0 = RW[0:1, 2 * T:3 * T]
    r_rstd = RW[0:1, 3 * T:4 * T]
    r_mr = RW[0:1, 4 * T:5 * T]
    S.op("vector", lambda v: v.tensor_scalar(r_mu, ps_sum.ap, 1.0 / D, None, ALU.mult),
         reads=[ps_sum], writes=[rows])
    S.op("vector", lambda v: v.tensor_scalar(r_t0, ps_sumsq.ap, 1.0 / D, None, ALU.mult),
         reads=[ps_sumsq], writes=[rows])
    S.op("vector", lambda v: v.tensor_tensor(r_var, r_mu, r_mu, ALU.mult),
         reads=[rows], writes=[rows])
    S.op("vector", lambda v: v.tensor_tensor(r_var, r_t0, r_var, ALU.subtract),
         reads=[rows], writes=[rows])
    S.op("vector", lambda v: v.tensor_scalar(r_var, r_var, 1e-5, None, ALU.add),
         reads=[rows], writes=[rows])
    S.op("scalar", lambda s: s.activation(r_t0, r_var, AF.Sqrt),
         reads=[rows], writes=[rows])
    S.op("vector", lambda v: v.reciprocal(r_rstd, r_t0),
         reads=[rows], writes=[rows])
    S.op("vector", lambda v: v.tensor_tensor(r_mr, r_mu, r_rstd, ALU.mult),
         reads=[rows], writes=[rows])
    ps_bc = take(4, 5, pap[:, 4 * 512:5 * 512])
    bcast_row(S, rows, r_rstd, bc0, onesk1, ps_bc, T)
    bcast_row(S, rows, r_mr, bc1, onesk1, ps_bc, T)

    for c in range(NCH):
        xb = xc[c % 2]
        xs = x_sb.ap[:, c * T:(c + 1) * T]
        S.op("vector", lambda v, o=xb.ap, i=xs: v.tensor_tensor(o, i, bc0.ap, ALU.mult),
             reads=[x_sb, bc0], writes=[xb])
        S.op("vector", lambda v, o=xb.ap: v.tensor_tensor(o, o, bc1.ap, ALU.subtract),
             reads=[bc1], writes=[xb])
        S.op("sync", lambda s, o=fm_chunk(xn2_o, c), i=xb.ap: s.dma_start(out=o, in_=i),
             reads=[xb], dma=True)
        S.op("vector", lambda v, o=xn2b.ap[:, c * T:(c + 1) * T], i=xb.ap: v.tensor_copy(o, i),
             reads=[xb], writes=[xn2b])

    for blk in range(10):
        w = wT[blk % 2]
        for dc in range(NCH):
            S.op("sync", lambda s, o=w.ap[:, dc * 512:(dc + 1) * 512],
                 i=ipw_bf.ap[blk * 512:(blk + 1) * 512, dc * 128:(dc + 1) * 128]:
                 s.dma_start_transpose(o, i),
                 reads=[ipw_bf], writes=[w], dma=True)
        for m in range(4):
            oc = blk * 4 + m
            pm = take(5 + m % 2, 6 + m % 2, pap[:, (5 + m % 2) * 512:(6 + m % 2) * 512])
            e = ev[m % 2]
            for h in range(2):
                lo, hi_ = h * 512, (h + 1) * 512
                for dc in range(NCH):
                    S.op("tensor", lambda t, o=pm.ap, w=w,
                         l=w.ap[:, dc * 512 + m * 128:dc * 512 + m * 128 + 128],
                         r=xn2b.ap[:, dc * T + lo:dc * T + hi_],
                         st=(dc == 0), sp=(dc == NCH - 1):
                         t.matmul(o, l, r, start=st, stop=sp),
                         reads=[w, xn2b], writes=_aw(dc, NCH, pm))
                if oc < NCH:
                    S.op("vector", lambda v, o=scr[0].ap[:, lo:hi_], i=pm.ap: v.tensor_copy(o, i),
                         reads=[pm], writes=[scr[0]])
                else:
                    S.op("vector", lambda v, o=e.ap[:, lo:hi_], i=pm.ap: v.tensor_copy(o, i),
                         reads=[pm], writes=[e])
            if oc < NCH:
                S.op("scalar", lambda s, o=fm_chunk(z_o, oc), i=scr[0].ap:
                     s.dma_start(out=o, in_=i),
                     reads=[scr[0]], dma=True)
            else:
                S.op("scalar", lambda s, o=fm_chunk(xbc_o, oc - NCH), i=e.ap:
                     s.dma_start(out=o, in_=i),
                     reads=[e], dma=True)
    wdt = wT[0]
    for dc in range(NCH):
        S.op("sync", lambda s, o=wdt.ap[:, dc * 16:(dc + 1) * 16],
             i=ipw_bf.ap[5120:5136, dc * 128:(dc + 1) * 128]:
             s.dma_start_transpose(o, i),
             reads=[ipw_bf], writes=[wdt], dma=True)
    DTR = dtr.ap
    d_raw = DTR[0:16, 0:T]
    d_a = DTR[0:16, T:2 * T]
    d_e = DTR[0:16, 2 * T:3 * T]
    d_r = DTR[0:16, 3 * T:4 * T]
    d_s = DTR[0:16, 4 * T:5 * T]
    pm = take(5, 6, pap[:, 5 * 512:6 * 512])
    for h in range(2):
        lo, hi_ = h * 512, (h + 1) * 512
        for dc in range(NCH):
            S.op("tensor", lambda t, o=pm.ap[0:16, 0:512],
                 l=wdt.ap[:, dc * 16:(dc + 1) * 16],
                 r=xn2b.ap[:, dc * T + lo:dc * T + hi_],
                 st=(dc == 0), sp=(dc == NCH - 1):
                 t.matmul(o, l, r, start=st, stop=sp),
                 reads=[wdt, xn2b], writes=_aw(dc, NCH, pm))
        S.op("vector", lambda v, o=d_raw[0:16, lo:hi_], i=pm.ap[0:16, 0:512]:
             v.tensor_copy(o, i),
             reads=[pm], writes=[dtr])
    S.op("vector", lambda v: v.tensor_scalar(d_raw, d_raw, dtb_sb.ap, None, ALU.add),
         reads=[dtb_sb], writes=[dtr])
    S.op("scalar", lambda s: s.activation(d_a, d_raw, AF.Abs),
         reads=[dtr], writes=[dtr])
    S.op("scalar", lambda s: s.activation(d_e, d_a, AF.Exp, scale=-1.0),
         reads=[dtr], writes=[dtr])
    S.op("vector", lambda v: v.tensor_scalar(d_e, d_e, 1.0, None, ALU.add),
         reads=[dtr], writes=[dtr])
    S.op("scalar", lambda s: s.activation(d_e, d_e, AF.Ln),
         reads=[dtr], writes=[dtr])
    S.op("vector", lambda v: v.tensor_scalar(d_r, d_raw, 0.0, None, ALU.max),
         reads=[dtr], writes=[dtr])
    S.op("vector", lambda v: v.tensor_tensor(d_s, d_r, d_e, ALU.add),
         reads=[dtr], writes=[dtr])
    S.op("scalar", lambda s, o=dts_o: s.dma_start(out=o, in_=d_s),
         reads=[dtr], dma=True)

    S.fence_all("sync")
    S.build()
    return nc




def build_l1conv():
    nc = bass.Bass("TRN2", target_bir_lowering=False, debug=False)
    xbc_d = nc.dram_tensor("xbc", [3072, T], F32, kind="ExternalInput").ap()
    halo_d = nc.dram_tensor("halo", [3072, 3], F32, kind="ExternalInput").ap()
    cw_d = nc.dram_tensor("cw", [128, 96], F32, kind="ExternalInput").ap()
    cb_d = nc.dram_tensor("cb", [128, 24], F32, kind="ExternalInput").ap()
    dts_d = nc.dram_tensor("dts", [16, T], F32, kind="ExternalInput").ap()
    alog_d = nc.dram_tensor("alog", [16, 1], F32, kind="ExternalInput").ap()
    xbcp_o = nc.dram_tensor("xbcp", [3072, T], BF16, kind="ExternalOutput").ap()
    s_o = nc.dram_tensor("s", [128, 4 * 16 * 128], BF16, kind="ExternalOutput").ap()
    dec_o = nc.dram_tensor("dec", [1, 64], F32, kind="ExternalOutput").ap()

    ut_f = nc.inline_tensor(np.triu(np.ones((128, 128), dtype=np.float32)), name="ut_f").ap()
    on_f = nc.inline_tensor(np.ones((128, 128), dtype=np.float32), name="on_f").ap()
    ident_f = nc.inline_tensor(np.eye(128, dtype=np.float32), name="ident_f").ap()
    ident_b = nc.inline_tensor(np.eye(128, dtype=BF), name="ident_b").ap()
    ones_row_f = nc.inline_tensor(np.ones((1, 128), dtype=np.float32), name="ones_row_f").ap()

    S = Sched(nc)
    xbcp = sb(nc, "xbcp_sb", [128, 24 * T], BF16)
    xe = [sb(nc, "xe0", [128, T + 3], F32), sb(nc, "xe1", [128, T + 3], F32)]
    acc = [sb(nc, "acc0", [128, T], F32), sb(nc, "acc1", [128, T], F32)]
    cw_sb = sb(nc, "cw_sb", [128, 96], F32)
    cb_sb = sb(nc, "cb_sb", [128, 24], F32)
    dts_sb = sb(nc, "dts_sb", [16, T], F32)
    arow = sb(nc, "arow", [16, 8], F32)
    datm = sb(nc, "datm", [128, 8 * 16], F32)
    dttm = sb(nc, "dttm", [128, 8 * 16], F32)
    dxtm = sb(nc, "dxtm", [128, 8 * 16], F32)
    comb = sb(nc, "comb", [128, 8 * 16], F32)
    dafm = sb(nc, "dafm", [16, 256], F32)
    x_tm = sb(nc, "x_tm", [128, 8 * 128], BF16)
    b_tm = sb(nc, "b_tm", [128, 8 * 128], BF16)
    xdd = sb(nc, "xdd", [128, 2 * 128], BF16)
    sev = [sb(nc, "sev0", [128, 128], BF16), sb(nc, "sev1", [128, 128], BF16)]
    tot_row = sb(nc, "tot_row", [1, 16], F32)
    totb = sb(nc, "totb", [128, 16], F32)
    dec_row = sb(nc, "dec_row", [1, 64], F32)
    onesk1 = sb(nc, "onesk1", [1, 128], F32)
    idf = sb(nc, "idf", [128, 128], F32)
    idb = sb(nc, "idb", [128, 128], BF16)
    utf = sb(nc, "utf", [128, 128], F32)
    onf = sb(nc, "onf", [128, 128], F32)

    pool = nc.alloc_psum_tensor("pool", [128, 3072], F32)
    poolb = nc.alloc_psum_tensor("poolb", [128, 1024], BF16)
    pap = pool.ap()
    bap = poolb.ap()
    bank_owner = {}

    def take(lo, hi, view):
        olds = [bank_owner[i] for i in range(lo, hi) if i in bank_owner]
        b = alias_over(view, *olds)
        for i in range(lo, hi):
            bank_owner[i] = b
        return b

    for name, dst, srcap in (("cw", cw_sb, cw_d), ("cb", cb_sb, cb_d), ("dts", dts_sb, dts_d),
                             ("ok1", onesk1, ones_row_f), ("idf", idf, ident_f),
                             ("idb", idb, ident_b), ("utf", utf, ut_f), ("onf", onf, on_f)):
        S.op("sync", lambda s, o=dst.ap, i=srcap: s.dma_start(out=o, in_=i),
             writes=[dst], dma=True)
    S.op("sync", lambda s: s.dma_start(out=arow.ap[0:16, 0:1], in_=alog_d),
         writes=[arow], dma=True)
    S.op("scalar", lambda s: s.activation(arow.ap[0:16, 1:2], arow.ap[0:16, 0:1], AF.Exp),
         reads=[arow], writes=[arow])
    S.op("vector", lambda v: v.tensor_scalar(arow.ap[0:16, 1:2], arow.ap[0:16, 1:2],
                                             -1.0, None, ALU.mult),
         reads=[arow], writes=[arow])

    for cc in range(24):
        x = xe[cc % 2]
        a = acc[cc % 2]
        S.op("sync", lambda s, o=x.ap[:, 0:3], i=halo_d[cc * 128:(cc + 1) * 128, :]:
             s.dma_start(out=o, in_=i), writes=[x], dma=True)
        S.op("sync", lambda s, o=x.ap[:, 3:T + 3], i=xbc_d[cc * 128:(cc + 1) * 128, :]:
             s.dma_start(out=o, in_=i), writes=[x], dma=True)
        S.op("vector", lambda v, x=x, a=a, wc=cw_sb.ap[:, cc * 4:cc * 4 + 1]:
             v.tensor_scalar(a.ap, x.ap[:, 0:T], wc, None, ALU.mult),
             reads=[x, cw_sb], writes=[a])
        for k2 in range(1, 4):
            S.op("vector", lambda v, x=x, a=a, k2=k2, wc=cw_sb.ap[:, cc * 4 + k2:cc * 4 + k2 + 1]:
                 v.scalar_tensor_tensor(a.ap, x.ap[:, k2:T + k2], wc, a.ap, ALU.mult, ALU.add),
                 reads=[x, cw_sb], writes=[a])
        S.op("scalar", lambda s, a=a, o=xbcp.ap[:, cc * T:(cc + 1) * T],
             bc=cb_sb.ap[:, cc:cc + 1]:
             s.activation(o, a.ap, AF.Silu, bias=bc),
             reads=[a, cb_sb], writes=[xbcp])
        S.op("scalar", lambda s, o=xbcp_o[cc * 128:(cc + 1) * 128, :],
             i=xbcp.ap[:, cc * T:(cc + 1) * T]:
             s.dma_start(out=o, in_=i),
             reads=[xbcp], dma=True)

    for j in range(4):
        S.op("vector", lambda v, j=j: v.tensor_scalar(
             dafm.ap, dts_sb.ap[0:16, j * 256:(j + 1) * 256], arow.ap[0:16, 1:2], None, ALU.mult),
             reads=[dts_sb, arow], writes=[dafm])
        S.op("vector", lambda v, j=j: v.tensor_reduce(
             arow.ap[0:16, 2 + j:3 + j], dafm.ap, AX, ALU.add),
             reads=[dafm], writes=[arow])
        ps_t = take(4, 6, bap[:, 0:1024])
        for st2 in range(2):
            ps_f = take(0, 1, pap[:, 0:512])
            S.op("tensor", lambda t, o=ps_f.ap[:, 0:16],
                 i=dafm.ap[0:16, st2 * 128:(st2 + 1) * 128], st2=st2:
                 t.transpose(o, i, idf.ap[0:16, 0:16]),
                 reads=[dafm, idf], writes=[ps_f])
            S.op("vector", lambda v, o=datm.ap[:, (j * 2 + st2) * 16:(j * 2 + st2 + 1) * 16],
                 i=ps_f.ap[:, 0:16]: v.tensor_copy(o, i),
                 reads=[ps_f], writes=[datm])
            S.op("tensor", lambda t, o=ps_f.ap[:, 16:32],
                 i=dts_sb.ap[0:16, j * 256 + st2 * 128:j * 256 + (st2 + 1) * 128]:
                 t.transpose(o, i, idf.ap[0:16, 0:16]),
                 reads=[dts_sb, idf], writes=[ps_f])
            S.op("vector", lambda v, o=dttm.ap[:, (j * 2 + st2) * 16:(j * 2 + st2 + 1) * 16],
                 i=ps_f.ap[:, 16:32]: v.tensor_copy(o, i),
                 reads=[ps_f], writes=[dttm])
        ps_a = take(1, 2, pap[:, 512:1024])
        S.op("tensor", lambda t, o=ps_a.ap[:, 0:16], l=utf.ap,
             r=datm.ap[:, (j * 2) * 16:(j * 2 + 1) * 16]:
             t.matmul(o, l, r, start=True, stop=True),
             reads=[utf, datm], writes=[ps_a])
        S.op("tensor", lambda t, o=ps_a.ap[:, 16:32], l=onf.ap,
             r=datm.ap[:, (j * 2) * 16:(j * 2 + 1) * 16]:
             t.matmul(o, l, r, start=True, stop=False),
             reads=[onf, datm], writes=[ps_a])
        S.op("tensor", lambda t, o=ps_a.ap[:, 16:32], l=utf.ap,
             r=datm.ap[:, (j * 2 + 1) * 16:(j * 2 + 2) * 16]:
             t.matmul(o, l, r, start=False, stop=True),
             reads=[utf, datm], writes=[ps_a])
        ps_r = take(2, 3, pap[:, 1024:1536])
        S.op("tensor", lambda t, o=ps_r.ap[0:1, 0:16], i=arow.ap[0:16, 2 + j:3 + j]:
             t.transpose(o, i, idf.ap[0:16, 0:16]),
             reads=[arow, idf], writes=[ps_r])
        S.op("vector", lambda v: v.tensor_copy(tot_row.ap, ps_r.ap[0:1, 0:16]),
             reads=[ps_r], writes=[tot_row])
        ps_b = take(2, 3, pap[:, 1024:1536])
        S.op("tensor", lambda t, o=ps_b.ap[:, 0:16], l=onesk1.ap, r=tot_row.ap:
             t.matmul(o, l, r, start=True, stop=True),
             reads=[onesk1, tot_row], writes=[ps_b])
        S.op("vector", lambda v: v.tensor_copy(totb.ap, ps_b.ap[:, 0:16]),
             reads=[ps_b], writes=[totb])
        for st2 in range(2):
            S.op("vector", lambda v, st2=st2,
                 o=dxtm.ap[:, (j * 2 + st2) * 16:(j * 2 + st2 + 1) * 16],
                 i=ps_a.ap[:, st2 * 16:(st2 + 1) * 16]:
                 v.tensor_tensor(o, totb.ap, i, ALU.subtract),
                 reads=[ps_a, totb], writes=[dxtm])
            S.op("scalar", lambda s, st2=st2,
                 o=dxtm.ap[:, (j * 2 + st2) * 16:(j * 2 + st2 + 1) * 16]:
                 s.activation(o, o, AF.Exp),
                 reads=[dxtm], writes=[dxtm])
            S.op("vector", lambda v, st2=st2,
                 o=comb.ap[:, (j * 2 + st2) * 16:(j * 2 + st2 + 1) * 16],
                 a=dttm.ap[:, (j * 2 + st2) * 16:(j * 2 + st2 + 1) * 16],
                 b=dxtm.ap[:, (j * 2 + st2) * 16:(j * 2 + st2 + 1) * 16]:
                 v.tensor_tensor(o, a, b, ALU.mult),
                 reads=[dttm, dxtm], writes=[comb])
        for kv in range(4):
            for st2 in range(2):
                ps_tb = take(4, 6, bap[:, 0:1024])
                S.op("tensor", lambda t, o=ps_tb.ap[:, 0:128],
                     i=xbcp.ap[:, kv * T + j * 256 + st2 * 128:kv * T + j * 256 + (st2 + 1) * 128]:
                     t.transpose(o, i, idb.ap),
                     reads=[xbcp, idb], writes=[ps_tb])
                S.op("vector", lambda v, o=x_tm.ap[:, (kv * 2 + st2) * 128:(kv * 2 + st2 + 1) * 128],
                     i=ps_tb.ap[:, 0:128]: v.tensor_copy(o, i),
                     reads=[ps_tb], writes=[x_tm])
                S.op("tensor", lambda t, o=ps_tb.ap[:, 128:256],
                     i=xbcp.ap[:, (4 + kv) * T + j * 256 + st2 * 128:(4 + kv) * T + j * 256 + (st2 + 1) * 128]:
                     t.transpose(o, i, idb.ap),
                     reads=[xbcp, idb], writes=[ps_tb])
                S.op("vector", lambda v, o=b_tm.ap[:, (kv * 2 + st2) * 128:(kv * 2 + st2 + 1) * 128],
                     i=ps_tb.ap[:, 128:256]: v.tensor_copy(o, i),
                     reads=[ps_tb], writes=[b_tm])
        for h in range(16):
            kv = h // 4
            for st2 in range(2):
                S.op("vector", lambda v, st2=st2, kv=kv, h=h,
                     o=xdd.ap[:, st2 * 128:(st2 + 1) * 128],
                     xi=x_tm.ap[:, (kv * 2 + st2) * 128:(kv * 2 + st2 + 1) * 128],
                     cc2=comb.ap[:, (j * 2 + st2) * 16 + h:(j * 2 + st2) * 16 + h + 1]:
                     v.tensor_scalar(o, xi, cc2, None, ALU.mult),
                     reads=[x_tm, comb], writes=[xdd])
            ps_s = take(3 if h % 2 == 0 else 0, 4 if h % 2 == 0 else 1,
                        pap[:, (3 if h % 2 == 0 else 0) * 512:(4 if h % 2 == 0 else 1) * 512])
            for st2 in range(2):
                S.op("tensor", lambda t, st2=st2, kv=kv, o=ps_s.ap[:, 0:128],
                     l=b_tm.ap[:, (kv * 2 + st2) * 128:(kv * 2 + st2 + 1) * 128],
                     r=xdd.ap[:, st2 * 128:(st2 + 1) * 128],
                     st=(st2 == 0), sp=(st2 == 1):
                     t.matmul(o, l, r, start=st, stop=sp),
                     reads=[b_tm, xdd], writes=_aw(st2, 2, ps_s))
            e = sev[h % 2]
            S.op("vector", lambda v, o=e.ap, i=ps_s.ap[:, 0:128]: v.tensor_copy(o, i),
                 reads=[ps_s], writes=[e])
            S.op("scalar", lambda s, o=s_o[:, (j * 16 + h) * 128:(j * 16 + h + 1) * 128], i=e.ap:
                 s.dma_start(out=o, in_=i),
                 reads=[e], dma=True)
    for j in range(4):
        S.op("scalar", lambda s, j=j: s.activation(
             arow.ap[0:16, 6:7], arow.ap[0:16, 2 + j:3 + j], AF.Exp),
             reads=[arow], writes=[arow])
        ps_r = take(2, 3, pap[:, 1024:1536])
        S.op("tensor", lambda t, o=ps_r.ap[0:1, 0:16], i=arow.ap[0:16, 6:7]:
             t.transpose(o, i, idf.ap[0:16, 0:16]),
             reads=[arow, idf], writes=[ps_r])
        S.op("vector", lambda v, j=j: v.tensor_copy(dec_row.ap[0:1, j * 16:(j + 1) * 16],
                                                    ps_r.ap[0:1, 0:16]),
             reads=[ps_r], writes=[dec_row])
    S.op("sync", lambda s: s.dma_start(out=dec_o, in_=dec_row.ap),
         reads=[dec_row], dma=True)

    S.fence_all("sync")
    S.build()
    return nc




def build_l1ssd(debug=False):
    nc = bass.Bass("TRN2", target_bir_lowering=False, debug=False)
    xbcp_d = nc.dram_tensor("xbcp", [3072, T], BF16, kind="ExternalInput").ap()
    dts_d = nc.dram_tensor("dts", [16, T], F32, kind="ExternalInput").ap()
    z_d = nc.dram_tensor("z", [D, T], BF16, kind="ExternalInput").ap()
    sin_d = nc.dram_tensor("sin", [128, 32 * 2048], BF16, kind="ExternalInput").ap()
    din_d = nc.dram_tensor("din", [1, 512], F32, kind="ExternalInput").ap()
    sown_d = nc.dram_tensor("sown", [128, 8192], BF16, kind="ExternalInput").ap()
    dec_d = nc.dram_tensor("dec", [1, 64], F32, kind="ExternalInput").ap()
    alog_d = nc.dram_tensor("alog", [16, 1], F32, kind="ExternalInput").ap()
    dD_d = nc.dram_tensor("dD", [1, 16], F32, kind="ExternalInput").ap()
    nw_d = nc.dram_tensor("nw", [128, 16], F32, kind="ExternalInput").ap()
    opw_d = nc.dram_tensor("opw", [D, D], F32, kind="ExternalInput").ap()
    a1_o = nc.dram_tensor("attn1", [D, T], F32, kind="ExternalOutput").ap()
    if debug:
        ydbg_o = nc.dram_tensor("ydbg", [D, T], F32, kind="ExternalOutput").ap()
        gtd_o = nc.dram_tensor("gtdbg", [128, 512], F32, kind="ExternalOutput").ap()

    opw_bf = Buf(nc.dram_tensor("opw_bf", [D, D], BF16, kind="Internal").ap())
    p0 = (np.arange(256)[None, :] >= np.arange(128)[:, None]).astype(np.float32)
    p1 = (np.arange(256)[None, :] >= (128 + np.arange(128))[:, None]).astype(np.float32)
    P0 = nc.inline_tensor(p0, name="P0").ap()
    P1 = nc.inline_tensor(p1, name="P1").ap()
    bd = np.zeros((16, 16 * 128), dtype=np.float32)
    for h in range(16):
        bd[h, h * 128:(h + 1) * 128] = 1.0
    BD = nc.inline_tensor(bd, name="BD").ap()
    ident_f = nc.inline_tensor(np.eye(128, dtype=np.float32), name="ident_f").ap()
    ident_b = nc.inline_tensor(np.eye(128, dtype=BF), name="ident_b").ap()
    ones_row_f = nc.inline_tensor(np.ones((1, 128), dtype=np.float32), name="ones_row_f").ap()
    ones_col_bf = nc.inline_tensor(np.ones((128, 1), dtype=BF), name="ones_col_bf").ap()

    S = Sched(nc)
    xbcp = sb(nc, "xbcp_sb", [128, 24 * T], BF16)
    z_sb = sb(nc, "z_sb", [128, NCH * T], BF16)
    yn = sb(nc, "yn", [128, NCH * T], BF16)
    H = sb(nc, "H", [128, 2048], F32)
    sinb = [sb(nc, "sin0", [128, 2048], BF16), sb(nc, "sin1", [128, 2048], BF16)]
    sob = [sb(nc, "sob0", [128, 2048], BF16), sb(nc, "sob1", [128, 2048], BF16)]
    dts_sb = sb(nc, "dts_sb", [16, T], F32)
    dinb = sb(nc, "dinb", [128, 512], F32)
    cdb = sb(nc, "cdb", [128, 64], F32)
    Db = sb(nc, "Db", [128, 16], F32)
    nw_sb = sb(nc, "nw_sb", [128, 16], F32)
    arow = sb(nc, "arow", [16, 4], F32)
    dafm = sb(nc, "dafm", [16, 256], F32)
    acsfm = sb(nc, "acsfm", [16, 256], F32)
    acstm = sb(nc, "acstm", [128, 32], F32)
    dttm = sb(nc, "dttm", [128, 32], F32)
    x_tm = sb(nc, "x_tm", [128, 8 * 128], BF16)
    xdt = sb(nc, "xdt", [128, 2 * 128], BF16)
    gT = sb(nc, "gT", [128, 2 * 256], BF16)
    ce = sb(nc, "ce", [128, 256], BF16)
    eacs = sb(nc, "eacs", [128, 256], BF16)
    hbf = sb(nc, "hbf", [128, 128], BF16)
    scr1 = sb(nc, "scr1", [128, 256], F32)
    scr2 = sb(nc, "scr2", [128, 256], F32)
    ysb = sb(nc, "ysb", [128, 256], F32)
    gsb = sb(nc, "gsb", [128, 256], F32)
    szb = sb(nc, "szb", [128, 256], BF16)
    sqb = sb(nc, "sqb", [128, 256], BF16)
    rrow = sb(nc, "rrow", [1, 256], F32)
    wT = [sb(nc, "wT0", [128, NCH * 512], BF16), sb(nc, "wT1", [128, NCH * 512], BF16)]
    ev = [sb(nc, "ev0", [128, 512], F32), sb(nc, "ev1", [128, 512], F32)]
    p0_sb = sb(nc, "p0_sb", [128, 256], F32)
    p1_sb = sb(nc, "p1_sb", [128, 256], F32)
    bd_sb = sb(nc, "bd_sb", [16, 16 * 128], F32)
    onesk1 = sb(nc, "onesk1", [1, 128], F32)
    onesb = sb(nc, "onesb", [128, 1], BF16)
    idf = sb(nc, "idf", [128, 128], F32)
    idb = sb(nc, "idb", [128, 128], BF16)

    pool = nc.alloc_psum_tensor("pool", [128, 3584], F32)
    poolb = nc.alloc_psum_tensor("poolb", [128, 1024], BF16)
    pap = pool.ap()
    bap = poolb.ap()
    bank_owner = {}

    def take(lo, hi, view):
        olds = [bank_owner[i] for i in range(lo, hi) if i in bank_owner]
        b = alias_over(view, *olds)
        for i in range(lo, hi):
            bank_owner[i] = b
        return b

    S.op("sync", lambda s: s.dma_start(out=sb3(xbcp.ap, T), in_=fm_full(xbcp_d)),
         writes=[xbcp], dma=True)
    S.op("sync", lambda s: s.dma_start(out=sb3(z_sb.ap, T), in_=fm_full(z_d)),
         writes=[z_sb], dma=True)
    for dst, srcap in ((dts_sb, dts_d), (nw_sb, nw_d), (bd_sb, BD), (p0_sb, P0),
                       (p1_sb, P1), (onesk1, ones_row_f), (onesb, ones_col_bf),
                       (idf, ident_f), (idb, ident_b)):
        S.op("sync", lambda s, o=dst.ap, i=srcap: s.dma_start(out=o, in_=i),
             writes=[dst], dma=True)
    S.op("sync", lambda s: s.dma_start(out=arow.ap[0:16, 0:1], in_=alog_d),
         writes=[arow], dma=True)
    for p in range(4):
        S.op("gpsimd", lambda g, o=opw_bf.ap[p * 512:(p + 1) * 512, :],
             i=opw_d[p * 512:(p + 1) * 512, :]: g.dma_start(out=o, in_=i),
             writes=[opw_bf], dma=True)
    S.op("scalar", lambda s: s.activation(arow.ap[0:16, 1:2], arow.ap[0:16, 0:1], AF.Exp),
         reads=[arow], writes=[arow])
    S.op("vector", lambda v: v.tensor_scalar(arow.ap[0:16, 1:2], arow.ap[0:16, 1:2],
                                             -1.0, None, ALU.mult),
         reads=[arow], writes=[arow])
    dinr = sb(nc, "dinr", [1, 512], F32)
    decr = sb(nc, "decr", [1, 64], F32)
    dDr = sb(nc, "dDr", [1, 16], F32)
    for dst, srcap in ((dinr, din_d), (decr, dec_d), (dDr, dD_d)):
        S.op("sync", lambda s, o=dst.ap, i=srcap: s.dma_start(out=o, in_=i),
             writes=[dst], dma=True)
    ps_b = take(5, 6, pap[:, 5 * 512:6 * 512])
    S.op("tensor", lambda t: t.matmul(ps_b.ap, onesk1.ap, dinr.ap, start=True, stop=True),
         reads=[onesk1, dinr], writes=[ps_b])
    S.op("vector", lambda v: v.tensor_copy(dinb.ap, ps_b.ap), reads=[ps_b], writes=[dinb])
    ps_b2 = take(5, 6, pap[:, 5 * 512:6 * 512])
    S.op("tensor", lambda t: t.matmul(ps_b2.ap[:, 0:64], onesk1.ap, decr.ap, start=True, stop=True),
         reads=[onesk1, decr], writes=[ps_b2])
    S.op("vector", lambda v: v.tensor_copy(cdb.ap, ps_b2.ap[:, 0:64]), reads=[ps_b2], writes=[cdb])
    ps_b3 = take(5, 6, pap[:, 5 * 512:6 * 512])
    S.op("tensor", lambda t: t.matmul(ps_b3.ap[:, 0:16], onesk1.ap, dDr.ap, start=True, stop=True),
         reads=[onesk1, dDr], writes=[ps_b3])
    S.op("vector", lambda v: v.tensor_copy(Db.ap, ps_b3.ap[:, 0:16]), reads=[ps_b3], writes=[Db])

    S.op("vector", lambda v: v.memset(H.ap, 0.0), writes=[H])
    for c in range(32):
        sc = sinb[c % 2]
        S.op("sync", lambda s, o=sc.ap, i=sin_d[:, c * 2048:(c + 1) * 2048]:
             s.dma_start(out=o, in_=i),
             writes=[sc], dma=True)
        for h in range(16):
            S.op("vector", lambda v, c=c, h=h, sc=sc:
                 v.scalar_tensor_tensor(H.ap[:, h * 128:(h + 1) * 128],
                                        H.ap[:, h * 128:(h + 1) * 128],
                                        dinb.ap[:, c * 16 + h:c * 16 + h + 1],
                                        sc.ap[:, h * 128:(h + 1) * 128],
                                        ALU.mult, ALU.add),
                 reads=[sc, dinb], writes=[H])

    mask_sb = [p0_sb, p1_sb]
    for j in range(4):
        soj = sob[j % 2]
        S.op("sync", lambda s, o=soj.ap, i=sown_d[:, j * 2048:(j + 1) * 2048]:
             s.dma_start(out=o, in_=i),
             writes=[soj], dma=True)
        S.op("vector", lambda v, j=j: v.tensor_scalar(
             dafm.ap, dts_sb.ap[0:16, j * 256:(j + 1) * 256], arow.ap[0:16, 1:2], None, ALU.mult),
             reads=[dts_sb, arow], writes=[dafm])
        ps_f = take(6, 7, pap[:, 6 * 512:7 * 512])
        for st2 in range(2):
            S.op("tensor", lambda t, st2=st2, o=ps_f.ap[:, st2 * 16:(st2 + 1) * 16],
                 i=dafm.ap[0:16, st2 * 128:(st2 + 1) * 128]:
                 t.transpose(o, i, idf.ap[0:16, 0:16]),
                 reads=[dafm, idf], writes=[ps_f])
            S.op("tensor", lambda t, st2=st2, j=j, o=ps_f.ap[:, 32 + st2 * 16:32 + (st2 + 1) * 16],
                 i=dts_sb.ap[0:16, j * 256 + st2 * 128:j * 256 + (st2 + 1) * 128]:
                 t.transpose(o, i, idf.ap[0:16, 0:16]),
                 reads=[dts_sb, idf], writes=[ps_f])
        S.op("vector", lambda v: v.tensor_copy(acstm.ap[:, 0:32], ps_f.ap[:, 0:32]),
             reads=[ps_f], writes=[acstm])
        S.op("vector", lambda v: v.tensor_copy(dttm.ap, ps_f.ap[:, 32:64]),
             reads=[ps_f], writes=[dttm])
        ps_af = take(6, 7, pap[:, 6 * 512:7 * 512])
        S.op("tensor", lambda t, o=ps_af.ap[0:16, 0:256], l=acstm.ap[:, 0:16], r=p0_sb.ap:
             t.matmul(o, l, r, start=True, stop=False),
             reads=[acstm, p0_sb], writes=[ps_af])
        S.op("tensor", lambda t, o=ps_af.ap[0:16, 0:256], l=acstm.ap[:, 16:32], r=p1_sb.ap:
             t.matmul(o, l, r, start=False, stop=True),
             reads=[acstm, p1_sb], writes=[ps_af])
        S.op("vector", lambda v: v.tensor_copy(acsfm.ap, ps_af.ap[0:16, 0:256]),
             reads=[ps_af], writes=[acsfm])
        ps_f2 = take(6, 7, pap[:, 6 * 512:7 * 512])
        for st2 in range(2):
            S.op("tensor", lambda t, st2=st2, o=ps_f2.ap[:, st2 * 16:(st2 + 1) * 16],
                 i=acsfm.ap[0:16, st2 * 128:(st2 + 1) * 128]:
                 t.transpose(o, i, idf.ap[0:16, 0:16]),
                 reads=[acsfm, idf], writes=[ps_f2])
        S.op("vector", lambda v: v.tensor_copy(acstm.ap, ps_f2.ap[:, 0:32]),
             reads=[ps_f2], writes=[acstm])
        for kv in range(4):
            for st2 in range(2):
                ps_tb = take(7, 8, bap[:, 0:512])
                S.op("tensor", lambda t, kv=kv, st2=st2, j=j, o=ps_tb.ap[:, 0:128],
                     i=xbcp.ap[:, kv * T + j * 256 + st2 * 128:kv * T + j * 256 + (st2 + 1) * 128]:
                     t.transpose(o, i, idb.ap),
                     reads=[xbcp, idb], writes=[ps_tb])
                S.op("vector", lambda v, kv=kv, st2=st2,
                     o=x_tm.ap[:, (kv * 2 + st2) * 128:(kv * 2 + st2 + 1) * 128],
                     i=ps_tb.ap[:, 0:128]: v.tensor_copy(o, i),
                     reads=[ps_tb], writes=[x_tm])
        for h in range(16):
            kv = h // 4
            ps_ab = take(0, 1, pap[:, 0:512])
            S.op("tensor", lambda t, h=h, o=ps_ab.ap[:, 0:256],
                 l=bd_sb.ap[0:16, h * 128:(h + 1) * 128], r=acsfm.ap:
                 t.matmul(o, l, r, start=True, stop=True),
                 reads=[bd_sb, acsfm], writes=[ps_ab])
            S.op("scalar", lambda s, o=eacs.ap, i=ps_ab.ap[:, 0:256]:
                 s.activation(o, i, AF.Exp),
                 reads=[ps_ab], writes=[eacs])
            S.op("vector", lambda v, h=h, j=j, o=ce.ap,
                 i=xbcp.ap[:, (8 + h) * T + j * 256:(8 + h) * T + (j + 1) * 256]:
                 v.tensor_tensor(o, i, eacs.ap, ALU.mult),
                 reads=[xbcp, eacs], writes=[ce])
            ps_g = [take(1, 2, pap[:, 512:1024]), take(2, 3, pap[:, 1024:1536])]
            for st2 in range(2):
                S.op("tensor", lambda t, st2=st2, kv=kv, h=h, j=j, o=ps_g[st2].ap[:, 0:256],
                     l=xbcp.ap[:, (4 + kv) * T + j * 256 + st2 * 128:(4 + kv) * T + j * 256 + (st2 + 1) * 128],
                     r=xbcp.ap[:, (8 + h) * T + j * 256:(8 + h) * T + (j + 1) * 256]:
                     t.matmul(o, l, r, start=True, stop=True),
                     reads=[xbcp], writes=[ps_g[st2]])
            for st2 in range(2):
                S.op("vector", lambda v, st2=st2, h=h, o=scr1.ap, i=ps_ab.ap[:, 0:256]:
                     v.tensor_scalar(o, i, acstm.ap[:, st2 * 16 + h:st2 * 16 + h + 1], None, ALU.subtract),
                     reads=[ps_ab, acstm], writes=[scr1])
                S.op("vector", lambda v, st2=st2, o=scr1.ap, m=mask_sb[st2].ap:
                     v.tensor_tensor(o, o, m, ALU.mult),
                     reads=[mask_sb[st2]], writes=[scr1])
                S.op("scalar", lambda s: s.activation(scr2.ap, scr1.ap, AF.Exp),
                     reads=[scr1], writes=[scr2])
                S.op("vector", lambda v, st2=st2, h=h, o=scr2.ap, g2=ps_g[st2].ap[:, 0:256]:
                     v.scalar_tensor_tensor(o, o, dttm.ap[:, st2 * 16 + h:st2 * 16 + h + 1],
                                            g2, ALU.mult, ALU.mult),
                     reads=[ps_g[st2], dttm], writes=[scr2])
                S.op("vector", lambda v, st2=st2, o=gT.ap[:, st2 * 256:(st2 + 1) * 256],
                     m=mask_sb[st2].ap:
                     v.tensor_tensor(o, scr2.ap, m, ALU.mult),
                     reads=[scr2, mask_sb[st2]], writes=[gT])
            S.op("vector", lambda v, h=h: v.tensor_copy(hbf.ap, H.ap[:, h * 128:(h + 1) * 128]),
                 reads=[H], writes=[hbf])
            ps_y = take(3 + h % 2, 4 + h % 2, pap[:, (3 + h % 2) * 512:(4 + h % 2) * 512])
            for st2 in range(2):
                S.op("tensor", lambda t, st2=st2, kv=kv, o=ps_y.ap[:, 0:256],
                     l=x_tm.ap[:, (kv * 2 + st2) * 128:(kv * 2 + st2 + 1) * 128],
                     r=gT.ap[:, st2 * 256:(st2 + 1) * 256],
                     st=(st2 == 0):
                     t.matmul(o, l, r, start=st, stop=False),
                     reads=[x_tm, gT], writes=[ps_y] if st2 == 0 else [])
            S.op("tensor", lambda t, o=ps_y.ap[:, 0:256], l=hbf.ap, r=ce.ap:
                 t.matmul(o, l, r, start=False, stop=True),
                 reads=[hbf, ce], writes=[ps_y])
            S.op("vector", lambda v, h=h, j=j, kv=kv, o=ysb.ap,
                 xi=xbcp.ap[:, kv * T + j * 256:kv * T + (j + 1) * 256],
                 i=ps_y.ap[:, 0:256]:
                 v.scalar_tensor_tensor(o, xi, Db.ap[:, h:h + 1], i, ALU.mult, ALU.add),
                 reads=[xbcp, Db, ps_y], writes=[ysb])
            if debug:
                S.op("sync", lambda s, h=h, j=j, o=ydbg_o[h * 128:(h + 1) * 128, j * 256:(j + 1) * 256],
                     i=ysb.ap: s.dma_start(out=o, in_=i),
                     reads=[ysb], dma=True)
                if h == 0 and j == 0:
                    S.op("vector", lambda v: v.tensor_copy(scr1.ap, gT.ap[:, 0:256]),
                         reads=[gT], writes=[scr1])
                    S.op("vector", lambda v: v.tensor_copy(scr2.ap, gT.ap[:, 256:512]),
                         reads=[gT], writes=[scr2])
                    S.op("sync", lambda s: s.dma_start(out=gtd_o[:, 0:256], in_=scr1.ap),
                         reads=[scr1], dma=True)
                    S.op("sync", lambda s: s.dma_start(out=gtd_o[:, 256:512], in_=scr2.ap),
                         reads=[scr2], dma=True)
            S.op("scalar", lambda s, h=h, j=j, o=szb.ap,
                 i=z_sb.ap[:, h * T + j * 256:h * T + (j + 1) * 256]:
                 s.activation(o, i, AF.Silu),
                 reads=[z_sb], writes=[szb])
            S.op("vector", lambda v: v.tensor_tensor(gsb.ap, ysb.ap, szb.ap, ALU.mult),
                 reads=[ysb, szb], writes=[gsb])
            S.op("scalar", lambda s: s.activation(sqb.ap, gsb.ap, AF.Square),
                 reads=[gsb], writes=[sqb])
            ps_r = take(5, 6, pap[:, 5 * 512:6 * 512])
            S.op("tensor", lambda t, o=ps_r.ap[0:1, 0:256]: t.matmul(
                 o, onesb.ap, sqb.ap, start=True, stop=True),
                 reads=[onesb, sqb], writes=[ps_r])
            S.op("vector", lambda v: v.tensor_scalar(rrow.ap, ps_r.ap[0:1, 0:256],
                                                     1.0 / 128.0, 1e-5, ALU.mult, ALU.add),
                 reads=[ps_r], writes=[rrow])
            S.op("scalar", lambda s: s.activation(rrow.ap, rrow.ap, AF.Sqrt),
                 reads=[rrow], writes=[rrow])
            S.op("vector", lambda v: v.reciprocal(rrow.ap, rrow.ap),
                 reads=[rrow], writes=[rrow])
            ps_rb = take(5, 6, pap[:, 5 * 512:6 * 512])
            S.op("tensor", lambda t, o=ps_rb.ap[:, 0:256]: t.matmul(
                 o, onesk1.ap, rrow.ap, start=True, stop=True),
                 reads=[onesk1, rrow], writes=[ps_rb])
            S.op("vector", lambda v, h=h, j=j,
                 o=yn.ap[:, h * T + j * 256:h * T + (j + 1) * 256],
                 i=ps_rb.ap[:, 0:256]:
                 v.scalar_tensor_tensor(o, gsb.ap, nw_sb.ap[:, h:h + 1], i, ALU.mult, ALU.mult),
                 reads=[gsb, nw_sb, ps_rb], writes=[yn])
        for h in range(16):
            S.op("vector", lambda v, h=h, j=j, soj=soj:
                 v.scalar_tensor_tensor(H.ap[:, h * 128:(h + 1) * 128],
                                        H.ap[:, h * 128:(h + 1) * 128],
                                        cdb.ap[:, j * 16 + h:j * 16 + h + 1],
                                        soj.ap[:, h * 128:(h + 1) * 128],
                                        ALU.mult, ALU.add),
                 reads=[soj, cdb], writes=[H])

    for blk in range(4):
        w = wT[blk % 2]
        for dc in range(NCH):
            S.op("sync", lambda s, blk=blk, dc=dc, o=w.ap[:, dc * 512:(dc + 1) * 512],
                 i=opw_bf.ap[blk * 512:(blk + 1) * 512, dc * 128:(dc + 1) * 128]:
                 s.dma_start_transpose(o, i),
                 reads=[opw_bf], writes=[w], dma=True)
        for m in range(4):
            oc = blk * 4 + m
            pm = take(m % 2, m % 2 + 1, pap[:, (m % 2) * 512:(m % 2 + 1) * 512])
            for hh in range(2):
                lo, hi_ = hh * 512, (hh + 1) * 512
                e = ev[hh]
                for dc in range(NCH):
                    S.op("tensor", lambda t, pm=pm,
                         l=w.ap[:, dc * 512 + m * 128:dc * 512 + m * 128 + 128],
                         r=yn.ap[:, dc * T + lo:dc * T + hi_],
                         st=(dc == 0), sp=(dc == NCH - 1):
                         t.matmul(pm.ap, l, r, start=st, stop=sp),
                         reads=[w, yn], writes=_aw(dc, NCH, pm))
                S.op("vector", lambda v, e=e, pm=pm: v.tensor_copy(e.ap, pm.ap),
                     reads=[pm], writes=[e])
                S.op("scalar", lambda s, e=e, o=fm_chunk(a1_o, oc)[:, lo:hi_], i=e.ap:
                     s.dma_start(out=o, in_=i),
                     reads=[e], dma=True)

    S.fence_all("sync")
    S.build()
    return nc




def build_l1moe():
    nc = bass.Bass("TRN2", target_bir_lowering=False, debug=False)
    xnew_d = nc.dram_tensor("xnew", [D, T], F32, kind="ExternalInput").ap()
    xn2_d = nc.dram_tensor("xn2", [D, T], F32, kind="ExternalInput").ap()
    a1_d = nc.dram_tensor("attn1", [D, T], F32, kind="ExternalInput").ap()
    sw1_d = nc.dram_tensor("sw1", [2 * D, 1408], F32, kind="ExternalInput").ap()
    sb1_d = nc.dram_tensor("sb1", [128, 2 * 11], F32, kind="ExternalInput").ap()
    sw2_d = nc.dram_tensor("sw2", [2 * 1408, D], F32, kind="ExternalInput").ap()
    sb2_d = nc.dram_tensor("sb2", [128, 2 * 16], F32, kind="ExternalInput").ap()
    rw1_d = nc.dram_tensor("rw1", [16 * D, 1408], F32, kind="ExternalInput").ap()
    rb1_d = nc.dram_tensor("rb1", [128, 16 * 11], F32, kind="ExternalInput").ap()
    rw2_d = nc.dram_tensor("rw2", [16 * 1408, D], F32, kind="ExternalInput").ap()
    rb2_d = nc.dram_tensor("rb2", [16, D], F32, kind="ExternalInput").ap()
    cent_d = nc.dram_tensor("cent", [16, D], F32, kind="ExternalInput").ap()
    cbias_d = nc.dram_tensor("cbias", [1, 16], F32, kind="ExternalInput").ap()
    fin_o = nc.dram_tensor("fin", [D, T], F32, kind="ExternalOutput").ap()

    bd = np.zeros((16, 16 * 128), dtype=np.float32)
    for h in range(16):
        bd[h, h * 128:(h + 1) * 128] = 1.0
    BD = nc.inline_tensor(bd, name="BD").ap()
    ident_f = nc.inline_tensor(np.eye(128, dtype=np.float32), name="ident_f").ap()
    ones_row_f = nc.inline_tensor(np.ones((1, 128), dtype=np.float32), name="ones_row_f").ap()

    S = Sched(nc)
    xn2b = sb(nc, "xn2b", [128, NCH * T], BF16)
    acc = sb(nc, "acc", [128, NCH * T], F32)
    h_sb2 = [sb(nc, "h_sbA", [128, 11 * T], BF16), sb(nc, "h_sbB", [128, 11 * T], BF16)]
    h_bufs = [[Buf(hs.ap[:, m * T:(m + 1) * T]) for m in range(11)] for hs in h_sb2]
    w1b = [sb(nc, "w1b%d" % i, [128, NCH * 128], BF16) for i in range(3)]
    w2b = [sb(nc, "w2b%d" % i, [128, 11 * 128], BF16) for i in range(3)]
    gb = sb(nc, "gb", [128, T], F32)
    scr = sb(nc, "scr", [128, 4096], F32)
    xcf = [Buf(scr.ap[:, 0:1024]), Buf(scr.ap[:, 1024:2048])]
    centT = sb(nc, "centT", [128, 16 * 16], F32)
    cent_sb = sb(nc, "cent_sb", [16, D], F32)
    sc_t = sb(nc, "sc_t", [128, 16], F32)
    eqv = sb(nc, "eqv", [128, 16], F32)
    gv = sb(nc, "gv", [128, 16], F32)
    mrow = sb(nc, "mrow", [128, 4], F32)
    biasb = sb(nc, "biasb", [128, 16], F32)
    b2s = sb(nc, "b2s", [128, 16], F32)
    sb1c = sb(nc, "sb1c", [128, 2 * 11], F32)
    sb2c = sb(nc, "sb2c", [128, 2 * 16], F32)
    rb1c = sb(nc, "rb1c", [128, 16 * 11], F32)
    rb2b = sb(nc, "rb2b", [16, D], BF16)
    g_fm = sb(nc, "g_fm", [16, 8 * 128], F32)
    g_fmb = sb(nc, "g_fmb", [16, 8 * 128], BF16)
    cb_row = sb(nc, "cb_row", [1, 16], F32)

    onesk1 = sb(nc, "onesk1", [1, 128], F32)
    idf = sb(nc, "idf", [128, 128], F32)

    pool = nc.alloc_psum_tensor("pool", [128, 4096], F32)
    pap = pool.ap()
    bank_owner = {}

    def take(lo, hi, view):
        olds = [bank_owner[i] for i in range(lo, hi) if i in bank_owner]
        b = alias_over(view, *olds)
        for i in range(lo, hi):
            bank_owner[i] = b
        return b

    for dst, srcap in ((sb1c, sb1_d), (sb2c, sb2_d), (rb1c, rb1_d),
                       (cb_row, cbias_d), (onesk1, ones_row_f), (idf, ident_f)):
        S.op("sync", lambda s, o=dst.ap, i=srcap: s.dma_start(out=o, in_=i),
             writes=[dst], dma=True)
    S.op("gpsimd", lambda g: g.dma_start(out=rb2b.ap, in_=rb2_d),
         writes=[rb2b], dma=True)
    S.op("sync", lambda s: s.dma_start(out=cent_sb.ap, in_=cent_d),
         writes=[cent_sb], dma=True)
    ps_ct = take(6, 7, pap[:, 6 * 512:7 * 512])
    for dc in range(NCH):
        S.op("tensor", lambda t, dc=dc, o=ps_ct.ap[:, 0:16],
             i=cent_sb.ap[0:16, dc * 128:(dc + 1) * 128]:
             t.transpose(o, i, idf.ap[0:16, 0:16]),
             reads=[cent_sb, idf], writes=[ps_ct])
        S.op("vector", lambda v, dc=dc, o=centT.ap[:, dc * 16:(dc + 1) * 16],
             i=ps_ct.ap[:, 0:16]: v.tensor_copy(o, i),
             reads=[ps_ct], writes=[centT])
    bd_sb = alias_over(cent_sb.ap, cent_sb)
    S.op("sync", lambda s: s.dma_start(out=bd_sb.ap, in_=BD), writes=[bd_sb], dma=True)
    ps_sc = take(5, 6, pap[:, 5 * 512:6 * 512])
    for c in range(NCH):
        xf = xcf[c % 2]
        S.op("sync", lambda s, o=xf.ap, i=fm_chunk(xn2_d, c): s.dma_start(out=o, in_=i),
             writes=[xf], dma=True)
        S.op("vector", lambda v, c=c, xf=xf: v.tensor_copy(xn2b.ap[:, c * T:(c + 1) * T], xf.ap),
             reads=[xf], writes=[xn2b])
        for tt in range(8):
            S.op("tensor", lambda t, c=c, tt=tt, xf=xf, o=ps_sc.ap[:, tt * 16:(tt + 1) * 16],
                 l=xf.ap[:, tt * 128:(tt + 1) * 128],
                 r=centT.ap[:, c * 16:(c + 1) * 16],
                 st=(c == 0 and tt == 0), sp=(c == NCH - 1 and tt == 7):
                 t.matmul(o, l, r, start=st, stop=sp),
                 reads=[xf, centT],
                 writes=[ps_sc] if (c == 0 and tt == 0) or (c == NCH - 1 and tt == 7) else [])
    S.op("vector", lambda v: v.tensor_tensor(b2s.ap, sb2c.ap[:, 0:16], sb2c.ap[:, 16:32], ALU.add),
         reads=[sb2c], writes=[b2s])
    ps_b = take(4, 5, pap[:, 4 * 512:5 * 512])
    S.op("tensor", lambda t: t.matmul(ps_b.ap[:, 0:16], onesk1.ap, cb_row.ap, start=True, stop=True),
         reads=[onesk1, cb_row], writes=[ps_b])
    S.op("vector", lambda v: v.tensor_copy(biasb.ap, ps_b.ap[:, 0:16]),
         reads=[ps_b], writes=[biasb])

    bkc = [0]

    def expert(eidx, w1ap, w2ap, b1col_off, b1buf, gated, j, first):
        h_sb = h_sb2[eidx % 2]
        hb = h_bufs[eidx % 2]
        for m in range(11):
            w = w1b[m % 3]
            wsrc = w1ap[:, m * 128:(m + 1) * 128].rearrange("(c p) n -> p c n", p=128)
            if m % 2 == 0:
                S.op("gpsimd", lambda g, o=sb3(w.ap, 128), i=wsrc: g.dma_start(out=o, in_=i),
                     writes=[w], dma=True)
            else:
                ws_ = wst[(m // 2) % 2]
                S.op("sync", lambda s, o=sb3(ws_.ap, 128), i=wsrc, ws_=ws_:
                     s.dma_start(out=o, in_=i),
                     writes=[ws_], dma=True)
                S.op("vector", lambda v, w=w, ws_=ws_: v.tensor_copy(w.ap, ws_.ap),
                     reads=[ws_], writes=[w])
            b0 = (bkc[0] % 2) * 2
            bkc[0] += 1
            pms = [take(b0 + hh, b0 + hh + 1, pap[:, (b0 + hh) * 512:(b0 + hh + 1) * 512])
                   for hh in range(2)]
            for dc in range(NCH):
                for hh in range(2):
                    S.op("tensor", lambda t, pm=pms[hh], w=w, dc=dc, hh=hh,
                         l=w.ap[:, dc * 128:(dc + 1) * 128],
                         r=xn2b.ap[:, dc * T + hh * 512:dc * T + hh * 512 + 512],
                         st=(dc == 0), sp=(dc == NCH - 1):
                         t.matmul(pm.ap, l, r, start=st, stop=sp),
                         reads=[w, xn2b], writes=_aw(dc, NCH, pms[hh]))
            for hh in range(2):
                lo, hi_ = hh * 512, (hh + 1) * 512
                S.op("scalar", lambda s, pm=pms[hh], m=m, lo=lo, hi_=hi_,
                     o=h_sb.ap[:, m * T + lo:m * T + hi_],
                     bc=b1buf.ap[:, b1col_off + m:b1col_off + m + 1]:
                     s.activation(o, pm.ap, AF.Relu, bias=bc),
                     reads=[pms[hh], b1buf], writes=[hb[m]])
        if gated:
            ps_g = take(5, 6, pap[:, 5 * 512:6 * 512])
            for hh in range(2):
                lo, hi_ = hh * 512, (hh + 1) * 512
                S.op("tensor", lambda t, j=j, lo=lo, hi_=hi_, o=ps_g.ap,
                     l=bd_sb.ap[0:16, j * 128:(j + 1) * 128], r=g_fm.ap[0:16, lo:hi_]:
                     t.matmul(o, l, r, start=True, stop=True),
                     reads=[bd_sb, g_fm], writes=[ps_g])
                S.op("vector", lambda v, lo=lo, hi_=hi_, o=gb.ap[:, lo:hi_], i=ps_g.ap:
                     v.tensor_copy(o, i),
                     reads=[ps_g], writes=[gb])
            for m in range(11):
                S.op("vector", lambda v, m=m, o=h_sb.ap[:, m * T:(m + 1) * T]:
                     v.tensor_tensor(o, o, gb.ap, ALU.mult),
                     reads=[gb], writes=[hb[m]])
        for mo in range(NCH):
            w = w2b[mo % 3]
            wsrc = w2ap[:, mo * 128:(mo + 1) * 128].rearrange("(c p) n -> p c n", p=128)
            if mo % 2 == 0:
                S.op("gpsimd", lambda g, o=w.ap.rearrange("p (c n) -> p c n", n=128), i=wsrc:
                     g.dma_start(out=o, in_=i),
                     writes=[w], dma=True)
            else:
                ws_ = wst[(mo // 2) % 2]
                S.op("sync", lambda s, o=sb3(ws_.ap[:, 0:11 * 128], 128), i=wsrc, ws_=ws_:
                     s.dma_start(out=o, in_=i),
                     writes=[ws_], dma=True)
                S.op("vector", lambda v, w=w, ws_=ws_: v.tensor_copy(w.ap, ws_.ap[:, 0:11 * 128]),
                     reads=[ws_], writes=[w])
            b0 = (bkc[0] % 2) * 2
            bkc[0] += 1
            pms = [take(b0 + hh, b0 + hh + 1, pap[:, (b0 + hh) * 512:(b0 + hh + 1) * 512])
                   for hh in range(2)]
            for dc in range(11):
                for hh in range(2):
                    S.op("tensor", lambda t, pm=pms[hh], w=w, dc=dc, hh=hh,
                         l=w.ap[:, dc * 128:(dc + 1) * 128],
                         r=h_sb.ap[:, dc * T + hh * 512:dc * T + hh * 512 + 512],
                         st=(dc == 0), sp=(dc == 10):
                         t.matmul(pm.ap, l, r, start=st, stop=sp),
                         reads=[w, hb[dc]], writes=_aw(dc, 11, pms[hh]))
            for hh in range(2):
                lo, hi_ = hh * 512, (hh + 1) * 512
                if first:
                    S.op("vector", lambda v, pm=pms[hh], mo=mo, lo=lo, hi_=hi_,
                         o=acc.ap[:, mo * T + lo:mo * T + hi_],
                         bc=b2s.ap[:, mo:mo + 1]:
                         v.tensor_scalar(o, pm.ap, bc, None, ALU.add),
                         reads=[pms[hh], b2s], writes=[acc])
                else:
                    S.op("vector", lambda v, pm=pms[hh], mo=mo, lo=lo, hi_=hi_,
                         o=acc.ap[:, mo * T + lo:mo * T + hi_]:
                         v.tensor_tensor(o, o, pm.ap, ALU.add),
                         reads=[pms[hh]], writes=[acc])

    wst = [alias_over(scr.ap[:, 0:2048], xcf[0], xcf[1]), Buf(scr.ap[:, 2048:4096])]
    expert(0, sw1_d[0:D, :], sw2_d[0:1408, :], 0, sb1c, False, 0, True)
    expert(1, sw1_d[D:2 * D, :], sw2_d[1408:2 * 1408, :], 11, sb1c, False, 0, False)
    for tt in range(8):
        S.op("scalar", lambda s, tt=tt: s.activation(sc_t.ap, ps_sc.ap[:, tt * 16:(tt + 1) * 16], AF.Sigmoid),
             reads=[ps_sc], writes=[sc_t])
        S.op("vector", lambda v: v.tensor_tensor(sc_t.ap, sc_t.ap, biasb.ap, ALU.add),
             reads=[biasb], writes=[sc_t])
        S.op("vector", lambda v: v.memset(gv.ap, 0.0), writes=[gv])
        S.op("vector", lambda v: v.memset(mrow.ap[:, 1:2], 0.0), writes=[mrow])
        for r in range(4):
            S.op("vector", lambda v: v.tensor_reduce(mrow.ap[:, 0:1], sc_t.ap, AX, ALU.max),
                 reads=[sc_t], writes=[mrow])
            S.op("vector", lambda v: v.tensor_scalar(eqv.ap, sc_t.ap, mrow.ap[:, 0:1], None, ALU.subtract),
                 reads=[sc_t, mrow], writes=[eqv])
            S.op("vector", lambda v: v.tensor_scalar(eqv.ap, eqv.ap, 1e12, 1.0, ALU.mult, ALU.add),
                 reads=[], writes=[eqv])
            S.op("vector", lambda v: v.tensor_scalar(eqv.ap, eqv.ap, 0.0, None, ALU.max),
                 reads=[], writes=[eqv])
            S.op("vector", lambda v: v.scalar_tensor_tensor(gv.ap, eqv.ap, mrow.ap[:, 0:1],
                                                            gv.ap, ALU.mult, ALU.add),
                 reads=[eqv, mrow], writes=[gv])
            S.op("vector", lambda v: v.scalar_tensor_tensor(sc_t.ap, eqv.ap, -1e4,
                                                            sc_t.ap, ALU.mult, ALU.add),
                 reads=[eqv], writes=[sc_t])
            S.op("vector", lambda v: v.tensor_tensor(mrow.ap[:, 1:2], mrow.ap[:, 1:2],
                                                     mrow.ap[:, 0:1], ALU.add),
                 reads=[], writes=[mrow])
        S.op("vector", lambda v: v.tensor_scalar(mrow.ap[:, 2:3], mrow.ap[:, 1:2], 1e-8, None, ALU.max),
             reads=[], writes=[mrow])
        S.op("vector", lambda v: v.reciprocal(mrow.ap[:, 3:4], mrow.ap[:, 2:3]),
             reads=[], writes=[mrow])
        S.op("vector", lambda v: v.tensor_scalar(gv.ap, gv.ap, mrow.ap[:, 3:4], None, ALU.mult),
             reads=[mrow], writes=[gv])
        ps_t = take(6, 7, pap[:, 6 * 512:7 * 512])
        S.op("tensor", lambda t, o=ps_t.ap[0:16, 0:128]: t.transpose(o, gv.ap, idf.ap),
             reads=[gv, idf], writes=[ps_t])
        S.op("vector", lambda v, tt=tt: v.tensor_copy(g_fm.ap[0:16, tt * 128:(tt + 1) * 128],
                                                      ps_t.ap[0:16, 0:128]),
             reads=[ps_t], writes=[g_fm])
    S.op("vector", lambda v: v.tensor_copy(g_fmb.ap, g_fm.ap),
         reads=[g_fm], writes=[g_fmb])

    for j in range(16):
        expert(2 + j, rw1_d[j * D:(j + 1) * D, :], rw2_d[j * 1408:(j + 1) * 1408, :],
               j * 11, rb1c, True, j, False)

    for mo in range(NCH):
        for hh in range(2):
            bk = (mo * 2 + hh) % 4
            pm = take(bk, bk + 1, pap[:, bk * 512:(bk + 1) * 512])
            lo, hi_ = hh * 512, (hh + 1) * 512
            S.op("tensor", lambda t, pm=pm, mo=mo, lo=lo, hi_=hi_,
                 l=rb2b.ap[0:16, mo * 128:(mo + 1) * 128], r=g_fmb.ap[0:16, lo:hi_]:
                 t.matmul(pm.ap, l, r, start=True, stop=True),
                 reads=[rb2b, g_fmb], writes=[pm])
            S.op("vector", lambda v, pm=pm, mo=mo, lo=lo, hi_=hi_,
                 o=acc.ap[:, mo * T + lo:mo * T + hi_]:
                 v.tensor_tensor(o, o, pm.ap, ALU.add),
                 reads=[pm], writes=[acc])

    sumb = [alias_over(scr.ap[:, 0:1536], wst[0]), alias_over(scr.ap[:, 2048:3584], wst[1])]
    evx = [alias_over(gb.ap[:, 0:512], gb), alias_over(gb.ap[:, 512:1024], gb)]
    for c in range(NCH):
        for hh in range(2):
            lo, hi_ = hh * 512, (hh + 1) * 512
            sm = sumb[(c * 2 + hh) % 2]
            e = evx[(c * 2 + hh) % 2]
            S.op("sync", lambda s, o=sm.ap[:, 0:512], i=fm_chunk(xnew_d, c)[:, lo:hi_]:
                 s.dma_start(out=o, in_=i), writes=[sm], dma=True)
            S.op("sync", lambda s, o=sm.ap[:, 512:1024], i=fm_chunk(xn2_d, c)[:, lo:hi_]:
                 s.dma_start(out=o, in_=i), writes=[sm], dma=True)
            S.op("sync", lambda s, o=sm.ap[:, 1024:1536], i=fm_chunk(a1_d, c)[:, lo:hi_]:
                 s.dma_start(out=o, in_=i), writes=[sm], dma=True)
            S.op("vector", lambda v, sm=sm, e=e: v.tensor_tensor(
                 e.ap, sm.ap[:, 0:512], sm.ap[:, 512:1024], ALU.add),
                 reads=[sm], writes=[e])
            S.op("vector", lambda v, sm=sm, e=e: v.tensor_tensor(
                 e.ap, e.ap, sm.ap[:, 1024:1536], ALU.add),
                 reads=[sm], writes=[e])
            S.op("vector", lambda v, c=c, lo=lo, hi_=hi_, e=e: v.tensor_tensor(
                 e.ap, e.ap, acc.ap[:, c * T + lo:c * T + hi_], ALU.add),
                 reads=[acc], writes=[e])
            S.op("scalar", lambda s, c=c, lo=lo, hi_=hi_, e=e,
                 o=fm_chunk(fin_o, c)[:, lo:hi_]:
                 s.dma_start(out=o, in_=e.ap),
                 reads=[e], dma=True)

    S.fence_all("sync")
    S.build()
    return nc


_NC_CACHE = {}


def _get_nc(name, builder):
    if name not in _NC_CACHE:
        _NC_CACHE[name] = builder()
    return _NC_CACHE[name]


def kernel(**d):
    cores = list(range(NCORE))
    x2d = d["L0_attn_x"][0]

    nc1 = _get_nc("l0qkv", build_l0qkv)
    im = [{"x": fm_shard(x2d, i), "qkvw": d["L0_attn_qkv"],
           "gamma": col128(d["L0_attn_gamma"]), "beta": col128(d["L0_attn_beta"])}
          for i in cores]
    r1 = run_bass_kernel_spmd(nc1, im, core_ids=cores).results

    nc2 = _get_nc("l0ffn", build_l0ffn)
    im = [{"xn": r1[i]["xn"], "wfc": d["L0_ffn_w_fc"], "wpr": d["L0_ffn_w_proj"]}
          for i in cores]
    r2 = run_bass_kernel_spmd(nc2, im, core_ids=cores).results

    nc3 = _get_nc("l0attn", build_l0attn)
    kf = np.ascontiguousarray(np.concatenate([r1[i]["k"] for i in cores], axis=1))
    vf = np.ascontiguousarray(np.concatenate([r1[i]["v"].T for i in cores], axis=0))
    qg = [r1[j // 2]["q"][:, (j % 2) * 512:(j % 2 + 1) * 512] for j in range(16)]
    xng = [r1[j // 2]["xn"][:, (j % 2) * 512:(j % 2 + 1) * 512] for j in range(16)]
    ffng = [r2[j // 2]["ffn"][:, (j % 2) * 512:(j % 2 + 1) * 512] for j in range(16)]
    im = []
    for i in cores:
        a, b = i, 15 - i
        qidx = np.empty((128, 8), np.float32)
        qidx[:, 0:4] = np.arange(128)[:, None] + a * 512 + np.arange(4)[None, :] * 128
        qidx[:, 4:8] = np.arange(128)[:, None] + b * 512 + np.arange(4)[None, :] * 128
        xab = np.concatenate([x2d[a * 512:(a + 1) * 512], x2d[b * 512:(b + 1) * 512]], axis=0)
        im.append({"q": np.ascontiguousarray(np.concatenate([qg[a], qg[b]], axis=1)),
                   "kf": kf, "vf": vf,
                   "x": np.ascontiguousarray(xab.T),
                   "xn": np.ascontiguousarray(np.concatenate([xng[a], xng[b]], axis=1)),
                   "ffn": np.ascontiguousarray(np.concatenate([ffng[a], ffng[b]], axis=1)),
                   "wo": d["L0_attn_o_proj"], "qidx": qidx})
    r3 = run_bass_kernel_spmd(nc3, im, core_ids=cores).results
    xnew_nat = [None] * 16
    for i in cores:
        xnew_nat[i] = r3[i]["xnew"][:, 0:512]
        xnew_nat[15 - i] = r3[i]["xnew"][:, 512:1024]
    xnew_c = [np.ascontiguousarray(np.concatenate([xnew_nat[2 * i], xnew_nat[2 * i + 1]],
                                                  axis=1)) for i in cores]

    nc4 = _get_nc("l1front", build_l1front)
    im = [{"x": xnew_c[i], "ipw": d["L1_attn_in_proj_w"],
           "dtb": np.ascontiguousarray(d["L1_attn_dt_bias"].reshape(16, 1))}
          for i in cores]
    r4 = run_bass_kernel_spmd(nc4, im, core_ids=cores).results

    nc5 = _get_nc("l1conv", build_l1conv)
    cw = np.ascontiguousarray(d["L1_attn_conv_w"].reshape(24, 128, 4).transpose(1, 0, 2).reshape(128, 96))
    cb = np.ascontiguousarray(d["L1_attn_conv_b"].reshape(24, 128).T)
    alog = np.ascontiguousarray(d["L1_attn_A_log"].reshape(16, 1))
    im = []
    for i in cores:
        halo = (np.zeros((3072, 3), np.float32) if i == 0
                else np.ascontiguousarray(r4[i - 1]["xbc"][:, T - 3:T]))
        im.append({"xbc": r4[i]["xbc"], "halo": halo, "cw": cw, "cb": cb,
                   "dts": r4[i]["dts"], "alog": alog})
    r5 = run_bass_kernel_spmd(nc5, im, core_ids=cores).results

    nc6 = _get_nc("l1ssd", build_l1ssd)
    s_full = np.concatenate([r5[i]["s"] for i in cores], axis=1)
    dec_all = np.stack([r5[i]["dec"][0] for i in cores], axis=0)
    im = []
    for i in cores:
        sin = s_full.copy()
        sin[:, i * 4 * 2048:] = 0
        din = np.ones((1, 512), np.float32)
        din[0, :i * 4 * 16] = dec_all[:i].reshape(-1)[:i * 4 * 16]
        im.append({"xbcp": r5[i]["xbcp"], "dts": r4[i]["dts"], "z": r4[i]["z"],
                   "sin": sin, "din": din, "sown": r5[i]["s"], "dec": r5[i]["dec"],
                   "alog": alog,
                   "dD": np.ascontiguousarray(d["L1_attn_D"].reshape(1, 16)),
                   "nw": col128(d["L1_attn_norm_w"]),
                   "opw": d["L1_attn_out_proj_w"]})
    r6 = run_bass_kernel_spmd(nc6, im, core_ids=cores).results

    nc7 = _get_nc("l1moe", build_l1moe)
    moe_w = {
        "sw1": np.ascontiguousarray(d["L1_ffn_shared_w1"].reshape(2 * D, 1408)),
        "sb1": np.ascontiguousarray(d["L1_ffn_shared_b1"].reshape(2, 11, 128).transpose(2, 0, 1).reshape(128, 22)),
        "sw2": np.ascontiguousarray(d["L1_ffn_shared_w2"].reshape(2 * 1408, D)),
        "sb2": np.ascontiguousarray(d["L1_ffn_shared_b2"].reshape(2, 16, 128).transpose(2, 0, 1).reshape(128, 32)),
        "rw1": np.ascontiguousarray(d["L1_ffn_routed_w1"].reshape(16 * D, 1408)),
        "rb1": np.ascontiguousarray(d["L1_ffn_routed_b1"].reshape(16, 11, 128).transpose(2, 0, 1).reshape(128, 176)),
        "rw2": np.ascontiguousarray(d["L1_ffn_routed_w2"].reshape(16 * 1408, D)),
        "rb2": d["L1_ffn_routed_b2"],
        "cent": d["L1_ffn_centroids"],
        "cbias": np.ascontiguousarray(d["L1_ffn_bias"].reshape(1, 16)),
    }
    im = [dict(moe_w, xnew=xnew_c[i], xn2=r4[i]["xn2"], attn1=r6[i]["attn1"])
          for i in cores]
    r7 = run_bass_kernel_spmd(nc7, im, core_ids=cores).results

    out = np.concatenate([r7[i]["fin"].T for i in cores], axis=0)
    return np.ascontiguousarray(out.reshape(1, 8 * T, D), dtype=np.float32)
